# version 30
# baseline (speedup 1.0000x reference)
import numpy as np
import ml_dtypes
from contextlib import ExitStack

import concourse.bass as bass
import concourse.mybir as mybir
from concourse.bass_utils import run_bass_kernel_spmd

F32 = mybir.dt.float32
BF16 = mybir.dt.bfloat16
FP16 = mybir.dt.float16
OP = mybir.AluOpType
AF = mybir.ActivationFunctionType
AX = mybir.AxisListType

S, D, NH, DH, DHR, DL, DI = 2048, 1024, 16, 64, 32, 256, 2816
NT = S // 128
QCH = 512
NQ = S // QCH
NB_D = D // 16
NB_I = DI // 16

import math
import ctypes
C_SM = 1.0 / float(np.float32(np.sqrt(np.float32(96.0))))
C_HI = float(np.float32(C_SM))
C_LO = C_SM - C_HI
E_HI = math.exp(80.0 * C_SM)
E_LO = math.exp(-80.0 * C_SM)
LN_EPS = 1e-5

SWAP32 = [1, 0, 3, 2, 5, 4, 7, 6, 9, 8, 11, 10, 13, 12, 15, 14,
          17, 16, 19, 18, 21, 20, 23, 22, 25, 24, 27, 26, 29, 28, 31, 30]


def _rope_tables():
    libm = ctypes.CDLL("libm.so.6")
    libm.cosf.restype = ctypes.c_float
    libm.cosf.argtypes = [ctypes.c_float]
    libm.sinf.restype = ctypes.c_float
    libm.sinf.argtypes = [ctypes.c_float]
    inv = (10000.0 ** (-np.arange(0, DHR, 2, dtype=np.float64) / DHR)).astype(np.float32)
    ang = np.arange(S, dtype=np.float32)[:, None] * inv
    cos = np.array([[libm.cosf(float(a)) for a in row] for row in ang], np.float32)
    sin = np.array([[libm.sinf(float(a)) for a in row] for row in ang], np.float32)
    cs = np.zeros((128, S), np.float32)
    sn = np.zeros((128, S), np.float32)
    for p in range(128):
        c = p % DHR
        i = c // 2
        cs[p] = cos[:, i]
        sn[p] = (-sin[:, i]) if (c % 2 == 0) else sin[:, i]
    return cs, sn


def _den_sel():
    s = np.zeros((16, 8, 128), np.float32)
    for c in range(8):
        s[2 * c, c, 0:64] = 1.0
        s[2 * c + 1, c, 64:128] = 1.0
    return s


def _causal_masks():
    m = np.zeros((128, 4, QCH), np.float32)
    for j in range(4):
        for k in range(128):
            q0 = k + 128 * j
            if q0 < QCH:
                m[k, j, q0:] = 1.0
    return m


class Sem:
    def __init__(self, ctx, nc, name):
        self.h = ctx.enter_context(nc.semaphore(name))
        self.count = 0

    def wait(self, eng, at=None):
        eng.wait_ge(self.h, self.count if at is None else at)


class DStream:
    def __init__(self, ctx, nc, name):
        self.h = ctx.enter_context(nc.semaphore(name))
        self.count = 0

    def dma(self, q, out, in_, **kw):
        if self.count:
            q.wait_ge(self.h, self.count)
        q.dma_start(out=out, in_=in_, **kw).then_inc(self.h, 16)
        self.count += 16
        return self.count

    def wait(self, eng, at=None):
        eng.wait_ge(self.h, self.count if at is None else at)


def build_nc(dump=False, dump_scratch=False):
    nc = bass.Bass("TRN2", target_bir_lowering=False,
                   dynamic_dma_scratch_size=16384,
                   detect_race_conditions=False)
    scratch_kind = "ExternalOutput" if (dump or dump_scratch) else "Internal"
    ctx = ExitStack()

    x_d = nc.dram_tensor("x", [S, D], F32, kind="ExternalInput")
    wdram = {}
    for name, shape in [("W_DQ", [D, DL]), ("W_UQ", [DL, D]), ("W_QR", [DL, NH * DHR]),
                        ("W_DKV", [D, DL]), ("W_UK", [DL, D]), ("W_UV", [DL, D]),
                        ("W_KR", [D, DHR]), ("W_O", [D, D]),
                        ("w_gate", [DI, D]), ("w_up", [DI, D]), ("w_down", [D, DI])]:
        wdram[name] = nc.dram_tensor(name, shape, F32, kind="ExternalInput")
    cs_d = nc.dram_tensor("rope_cos", [128, S], F32, kind="ExternalInput")
    sn_d = nc.dram_tensor("rope_sin", [128, S], F32, kind="ExternalInput")
    mask_d = nc.dram_tensor("masks", [128, 4, QCH], F32, kind="ExternalInput")
    sel_d = nc.dram_tensor("sel", [16, 8, 128], F32, kind="ExternalInput")
    den_dr = nc.dram_tensor("den_scr", [16, S], F32, kind=scratch_kind)
    out_d = nc.dram_tensor("out", [S, D], F32, kind="ExternalOutput")
    r0_dr = nc.dram_tensor("r0_scr", [S, D], BF16, kind=scratch_kind)
    r1_dr = nc.dram_tensor("r1_scr", [S, D], BF16, kind=scratch_kind)
    at_dr = nc.dram_tensor("attn_scr", [128, 8, S], BF16, kind=scratch_kind)
    wqd_dr = nc.dram_tensor("wqd_scr", [128, 22, D], BF16, kind=scratch_kind)
    wqu_dr = nc.dram_tensor("wqu_scr", [128, 8, DI], BF16, kind=scratch_kind)
    wqg_dr = nc.dram_tensor("wqg_scr", [128, 8, DI], BF16, kind=scratch_kind)

    t_d = Sem(ctx, nc, "t_d")
    v_d = Sem(ctx, nc, "v_d")
    a_d = Sem(ctx, nc, "a_d")
    g_c = Sem(ctx, nc, "g_c")

    s_tab = DStream(ctx, nc, "s_tab")
    s_x = [DStream(ctx, nc, f"s_x{i}") for i in range(2)]
    s_a4r = [DStream(ctx, nc, f"s_a4r{i}") for i in range(2)]
    s_a4a = [DStream(ctx, nc, f"s_a4a{i}") for i in range(2)]
    s_w = [DStream(ctx, nc, f"s_w{i}") for i in range(2)]
    s_r1 = [DStream(ctx, nc, f"s_r1{i}") for i in range(2)]
    s_wl = DStream(ctx, nc, "s_wl")
    s_uv = DStream(ctx, nc, "s_uv")
    s_kh = DStream(ctx, nc, "s_kh")
    s_qh = [DStream(ctx, nc, f"s_qh{i}") for i in range(2)]
    s_r0s = [DStream(ctx, nc, f"s_r0s{i}") for i in range(2)]
    s_ats = DStream(ctx, nc, "s_ats")
    s_den = DStream(ctx, nc, "s_den")
    s_dw = DStream(ctx, nc, "s_dw")
    s_wq = DStream(ctx, nc, "s_wq")
    s_r1s = [DStream(ctx, nc, f"s_r1s{i}") for i in range(2)]
    s_tp = [DStream(ctx, nc, f"s_tp{i}") for i in range(22)]
    s_out = DStream(ctx, nc, "s_out")
    s_dbg = DStream(ctx, nc, "s_dbg")

    def AL(name, shape, dtype):
        return ctx.enter_context(nc.sbuf_tensor(name, shape, dtype, side="left"))

    stat = AL("stat", [128, 16], F32)
    sqsc = AL("sqsc", [128, 512], F32)
    lnst = AL("lnst", [128, D], BF16)
    ones1 = AL("ones1", [1, 64], F32)
    xst = AL("xst", [128, 2, D], F32)
    r0st = AL("r0st", [128, 2, D], BF16)

    blk = ctx.enter_context(nc.Block())

    actxL = ExitStack()
    def AAL(name, shape, dtype):
        return actxL.enter_context(nc.sbuf_tensor(name, shape, dtype, side="left"))
    cqT = AAL("cqT", [128, 2, S], BF16)
    ckvT = AAL("ckvT", [128, 2, S], BF16)
    Vb = AAL("Vb", [128, NT, NH * 65], BF16)
    QrT = AAL("QrT", [128, 4, S], BF16)
    KrT = AAL("KrT", [32, S], BF16)
    mask_s = AAL("mask", [128, 4, QCH], BF16)
    wb_UQ = AAL("wb_UQ", [128, 2, D], BF16)
    wb_UK = AAL("wb_UK", [128, 2, D], BF16)
    wb_QR = AAL("wb_QR", [128, 2, NH * DHR], BF16)
    wb_WO = AAL("wb_WO", [128, 8, D], BF16)
    rowr = AAL("rowr", [1, 2, QCH], F32)
    sel_s = AAL("sel_s", [16, 8, 128], F32)

    actxR = ExitStack()
    def AAR(name, shape, dtype):
        return actxR.enter_context(nc.sbuf_tensor(name, shape, dtype, side="right"))
    wb_DQ = AAR("wb_DQ", [128, 8, DL], BF16)
    wb_DKV = AAR("wb_DKV", [128, 8, DL], BF16)
    wb_KR = AAR("wb_KR", [128, 8, DHR], BF16)
    ln1T = AAR("ln1T", [128, 8, S], BF16)
    cs_s = AAR("cs", [128, S], F32)
    sn_s = AAR("sn", [128, S], F32)
    ropeA = AAR("ropeA", [128, S], F32)
    ropeB = AAR("ropeB", [128, S], F32)
    ropeC = AAR("ropeC", [128, S], F32)

    P = nc.alloc_psum_tensor
    psA = P("psA", [128, 512], F32)
    psB = P("psB", [128, 512], F32)
    psC = P("psC", [128, 512], F32)
    psD = P("psD", [128, 512], F32)
    psS0 = P("psS0", [128, QCH], F32)
    psS1 = P("psS1", [128, QCH], F32)
    psAV2 = [P("psAV0", [65, QCH], F32), P("psAV1", [65, QCH], F32)]

    @blk.sync
    def _(sync):
        s_tab.dma(sync, out=cs_s[:], in_=cs_d[:])
        s_tab.dma(sync, out=sn_s[:], in_=sn_d[:])

    @blk.gpsimd
    def _(g):
        for dst, name in ((wb_DQ, "W_DQ"), (wb_DKV, "W_DKV"), (wb_KR, "W_KR"),
                          (wb_UQ, "W_UQ"), (wb_QR, "W_QR"), (wb_UK, "W_UK"),
                          (wb_WO, "W_O")):
            s_wl.dma(g, out=dst[:],
                     in_=wdram[name][:].rearrange("(a p) l -> p a l", p=128))
        s_wl.dma(g, out=mask_s[:], in_=mask_d[:])
        s_wl.dma(g, out=sel_s[:], in_=sel_d[:])
        g.memset(ones1[:], 1.0).then_inc(g_c.h, 1)
        g.memset(Vb[:].rearrange("p t (h c) -> p t h c", c=65)[:, :, :, 64:65],
                 1.0).then_inc(g_c.h, 1)
    g_c.count += 2

    @blk.vector
    def _(v):
        s_wl.wait(v)
        for wt, ncol in ((wb_UK, 2 * D), (wb_KR, 8 * DHR)):
            v.tensor_scalar(ropeA[:, 0:ncol], wt[:].rearrange("p a l -> p (a l)"),
                            C_HI, None, OP.mult)
            v.drain()
            v.scalar_tensor_tensor(wt[:].rearrange("p a l -> p (a l)"),
                                   wt[:].rearrange("p a l -> p (a l)"), C_LO,
                                   ropeA[:, 0:ncol], OP.mult, OP.add)
            v.drain()
        v.engine_nop().then_inc(v_d.h, 1)
    v_d.count += 1
    kw_scaled_v = v_d.count

    xst_free_v = [0, 0]
    lnst_tp_prev = [s.count for s in s_tp[:8]]

    ln1_tp_tiles = []
    for st in range(NT):
        b = st % 2

        @blk.sync
        def _(sync, st=st, b=b, w=xst_free_v[b]):
            sync.wait_ge(v_d.h, w)
            s_x[b].dma(sync, out=xst[:, b, :], in_=x_d[128 * st:128 * (st + 1), :])
        x_in = s_x[b].count

        @blk.vector
        def _(v, st=st, b=b, x_in=x_in):
            s_x[b].wait(v, x_in)
            v.reduce_sum(stat[:, 0:1], xst[:, b, :], AX.X)
            v.tensor_tensor(sqsc[:], xst[:, b, 0:512], xst[:, b, 0:512], OP.mult)
            v.reduce_sum(stat[:, 1:2], sqsc[:], AX.X)
            v.tensor_tensor(sqsc[:], xst[:, b, 512:1024], xst[:, b, 512:1024], OP.mult)
            v.reduce_sum(stat[:, 5:6], sqsc[:], AX.X)
            v.drain()
            v.tensor_tensor(stat[:, 1:2], stat[:, 1:2], stat[:, 5:6], OP.add)
            v.tensor_scalar(stat[:, 0:1], stat[:, 0:1], 1.0 / D, None, OP.mult)
            v.tensor_scalar(stat[:, 1:2], stat[:, 1:2], 1.0 / D, None, OP.mult)
            v.scalar_tensor_tensor(stat[:, 2:3], stat[:, 0:1], 1.0, stat[:, 0:1],
                                   OP.mult, OP.mult)
            v.tensor_tensor(stat[:, 1:2], stat[:, 1:2], stat[:, 2:3], OP.subtract)
            v.tensor_scalar(stat[:, 1:2], stat[:, 1:2], LN_EPS, None, OP.add)
            v.engine_nop().then_inc(v_d.h, 1)
        v_d.count += 1

        @blk.scalar
        def _(s, st=st):
            v_d.wait(s)
            s.activation(stat[:, 3:4], stat[:, 1:2], AF.Sqrt).then_inc(a_d.h, 1)
        a_d.count += 1

        @blk.vector
        def _(v, st=st, b=b, tp=list(lnst_tp_prev), r0f=s_r0s[b].count):
            a_d.wait(v)
            for f in range(8):
                v.wait_ge(s_tp[f].h, tp[f])
            v.wait_ge(s_r0s[b].h, r0f)
            v.reciprocal(stat[:, 4:5], stat[:, 3:4])
            v.tensor_scalar(lnst[:], xst[:, b, :], stat[:, 0:1], stat[:, 4:5],
                            OP.subtract, OP.mult)
            v.tensor_copy(r0st[:, b, :], xst[:, b, :])
            v.drain()
            v.engine_nop().then_inc(v_d.h, 1)
        v_d.count += 1
        xst_free_v[b] = v_d.count

        @blk.gpsimd
        def _(g, st=st, b=b):
            v_d.wait(g)
            s_r0s[b].dma(g, out=r0_dr[128 * st:128 * (st + 1), :], in_=r0st[:, b, :])

        @blk.sync
        def _(sync, st=st):
            v_d.wait(sync)
            for f in range(8):
                s_tp[f].dma(sync, out=ln1T[:, f, 128 * st:128 * (st + 1)],
                            in_=lnst[:, 128 * f:128 * (f + 1)], transpose=True)
        lnst_tp_prev = [s_tp[f].count for f in range(8)]
        ln1_tp_tiles.append(list(lnst_tp_prev))

    class MMPipe:
        def __init__(self):
            self.jobs = []

        def add(self, mm_fn, evac_fn):
            self.jobs.append((mm_fn, evac_fn))

        def emit(self, pre_tensor_wait=None, banks=2):
            n0_t = t_d.count
            n0_a = a_d.count
            jobs = self.jobs

            @blk.tensor
            def _(t):
                if pre_tensor_wait is not None:
                    pre_tensor_wait(t)
                for i, (mm_fn, _e) in enumerate(jobs):
                    t.wait_ge(a_d.h, max(n0_a, n0_a + i - banks + 1))
                    mm_fn(t).then_inc(t_d.h, 1)

            @blk.scalar
            def _(s):
                for i, (_m, evac_fn) in enumerate(jobs):
                    s.wait_ge(t_d.h, n0_t + i + 1)
                    evac_fn(s).then_inc(a_d.h, 1)

            t_d.count += len(jobs)
            a_d.count += len(jobs)

    ln1_tp = [s_tp[f].count for f in range(8)]

    def wait_ln1(t):
        s_wl.wait(t)
        s_tab.wait(t)

    pipe = MMPipe()
    for (wten, dst) in ((wb_DQ, cqT), (wb_DKV, ckvT)):
        for l in range(2):
            for sc in range(4):
                ps = psA if (sc % 2 == 0) else psB
                def mm(t, wten=wten, l=l, sc=sc, ps=ps,
                       tp=list(ln1_tp_tiles[4 * sc + 3])):
                    for f in range(8):
                        t.wait_ge(s_tp[f].h, tp[f])
                    for f in range(8):
                        i_ = t.matmul(ps[0:128, :], wten[:, f, 128 * l:128 * (l + 1)],
                                      ln1T[:, f, 512 * sc:512 * (sc + 1)],
                                      start=(f == 0), stop=(f == 7))
                    return i_
                def ev(s, dst=dst, l=l, sc=sc, ps=ps):
                    return s.activation(dst[:, l, 512 * sc:512 * (sc + 1)],
                                        ps[0:128, :], AF.Copy)
                pipe.add(mm, ev)
    pipe.emit(pre_tensor_wait=wait_ln1)

    pipe = MMPipe()
    for sc in range(4):
        ps = psA if (sc % 2 == 0) else psB
        def mm(t, sc=sc, ps=ps):
            for f in range(8):
                i_ = t.matmul(ps[0:32, :], wb_KR[:, f, :],
                              ln1T[:, f, 512 * sc:512 * (sc + 1)],
                              start=(f == 0), stop=(f == 7))
            return i_
        def ev(s, sc=sc, ps=ps):
            return s.activation(ropeA[0:32, 512 * sc:512 * (sc + 1)], ps[0:32, :], AF.Copy)
        pipe.add(mm, ev)

    def wait_kscale(t):
        t.wait_ge(v_d.h, kw_scaled_v)
    pipe.emit(pre_tensor_wait=wait_kscale)

    @blk.vector
    def _(v):
        a_d.wait(v)
        v.stream_shuffle(ropeB[0:32, :], ropeA[0:32, :], SWAP32)
        v.tensor_tensor(ropeC[0:32, :], ropeA[0:32, :], cs_s[0:32, :], OP.mult)
        v.tensor_tensor(ropeB[0:32, :], ropeB[0:32, :], sn_s[0:32, :], OP.mult)
        v.tensor_tensor(ropeC[0:32, :], ropeC[0:32, :], ropeB[0:32, :], OP.add)
        v.tensor_copy(KrT[:], ropeC[0:32, :])
        v.engine_nop().then_inc(v_d.h, 1)
    v_d.count += 1

    for qt in range(4):
        rope_free = v_d.count
        pipe = MMPipe()
        for sc in range(4):
            ps = psA if (sc % 2 == 0) else psB
            def mm(t, qt=qt, sc=sc, ps=ps, rope_free=rope_free):
                if sc == 0:
                    t.wait_ge(v_d.h, rope_free)
                for l in range(2):
                    i_ = t.matmul(ps[0:128, :], wb_QR[:, l, 128 * qt:128 * (qt + 1)],
                                  cqT[:, l, 512 * sc:512 * (sc + 1)],
                                  start=(l == 0), stop=(l == 1))
                return i_
            def ev(s, sc=sc, ps=ps):
                return s.activation(ropeA[:, 512 * sc:512 * (sc + 1)], ps[0:128, :], AF.Copy)
            pipe.add(mm, ev)
        pipe.emit()

        @blk.vector
        def _(v, qt=qt):
            a_d.wait(v)
            v.stream_shuffle(ropeB[:], ropeA[:], SWAP32)
            v.tensor_tensor(ropeC[:], ropeA[:], cs_s[:], OP.mult)
            v.tensor_tensor(ropeB[:], ropeB[:], sn_s[:], OP.mult)
            v.tensor_tensor(ropeC[:], ropeC[:], ropeB[:], OP.add)
            v.tensor_copy(QrT[:, qt, :], ropeC[:])
            v.engine_nop().then_inc(v_d.h, 1)
        v_d.count += 1
    qr_done_v = v_d.count

    wb_UQv = AAR("wb_UV", [128, 2, D], BF16)
    pipe = MMPipe()
    for st in range(NT):
        for half in range(2):
            ps = psC if (half == 0) else psD
            def mm(t, st=st, half=half, ps=ps):
                for l in range(2):
                    i_ = t.matmul(ps[0:128, :], ckvT[:, l, 128 * st:128 * (st + 1)],
                                  wb_UQv[:, l, 512 * half:512 * (half + 1)],
                                  start=(l == 0), stop=(l == 1))
                return i_
            def ev(s, st=st, half=half, ps=ps):
                dst = Vb[:, st, :].rearrange("p (h c) -> p h c", c=65)[:, 8 * half:8 * (half + 1), 0:64]
                return s.activation(dst, ps[0:128, :].rearrange("p (h c) -> p h c", c=64),
                                    AF.Copy)
            pipe.add(mm, ev)

    @blk.gpsimd
    def _(g):
        s_uv.dma(g, out=wb_UQv[:],
                 in_=wdram["W_UV"][:].rearrange("(a p) l -> p a l", p=128))

    def wait_uv(t):
        s_uv.wait(t)
        g_c.wait(t)
    pipe.emit(pre_tensor_wait=wait_uv)

    actxR.close()
    Qh = AAL("Qh", [128, S], BF16)
    Kh = AAL("Kh", [128, S], BF16)
    Eb = AAL("Eb", [128, NT, QCH], BF16)
    attnSt = AAL("attnSt", [64, 2, QCH], BF16)
    attnTs = AAL("attnTs", [128, 2, 8, 128], BF16)
    r1stA = AAL("r1stA", [128, 2, D], BF16)
    den_s = AAL("den_s", [16, 2, 128], F32)
    denb = AAL("denb", [128, 4, 128], BF16)

    HDI = DI // 2
    wraw2 = AAL("wraw2", [128, 2, HDI], F32)
    qw_gm = AAL("qw_gm", [128, 128], F32)
    qw_s = AAL("qw_s", [128, 128], F32)
    qw_s2 = AAL("qw_s2", [128, 128], F32)
    qw_r = AAL("qw_r", [128, 128], F32)
    qw_s16 = AAL("qw_s16", [128, 128], FP16)
    qw_an = AAL("qw_an", [128, HDI], FP16)
    qw_acc = AAL("qw_acc", [128, HDI], FP16)
    qw_cA = AAL("qw_cA", [128, HDI], FP16)
    qw_sg = AAL("qw_sg", [128, HDI], FP16)
    qw_out = AAL("qw_out", [128, HDI], BF16)
    qfh_an = AAL("qfh_an", [128, HDI], F32)
    qfh_acc = AAL("qfh_acc", [128, HDI], F32)
    qfh_cA = AAL("qfh_cA", [128, HDI], F32)
    dstTs = AAL("dstTs", [128, 11, 128], BF16)

    def q_chain(v, an_ap, acc_ap, cA_ap, out_ap, n):
        v.tensor_scalar(acc_ap, an_ap, 5.0, 2.0, OP.is_gt, OP.mult)
        v.scalar_tensor_tensor(acc_ap, an_ap, 2.5, acc_ap, OP.is_gt, OP.add)
        v.scalar_tensor_tensor(acc_ap, an_ap, 3.5, acc_ap, OP.is_gt, OP.add)
        v.tensor_scalar(cA_ap, an_ap, 0.25, None, OP.is_gt)
        v.scalar_tensor_tensor(cA_ap, an_ap, 0.75, cA_ap, OP.is_gt, OP.add)
        v.scalar_tensor_tensor(cA_ap, an_ap, 1.25, cA_ap, OP.is_gt, OP.add)
        v.scalar_tensor_tensor(cA_ap, an_ap, 1.75, cA_ap, OP.is_gt, OP.add)
        v.scalar_tensor_tensor(out_ap, cA_ap, 0.5, acc_ap, OP.mult, OP.add)

    wq_tp_guard = [[]]
    qw_load_guard = [0, 0]
    qw_sbuf_guard = [0]

    def make_wq_unit(src_dram, row0, col0, n_cols, chain, dst_dr, c0, nchunks, wt_col):
        def emit(u):
            p = u % 2

            @blk.gpsimd
            def _(g, p=p, vq=qw_load_guard[p]):
                if vq:
                    g.wait_ge(v_d.h, vq)
                s_w[p].dma(g, out=wraw2[:, p, 0:n_cols],
                           in_=src_dram[row0:row0 + 128, col0:col0 + n_cols])
            w_in = s_w[p].count

            abuf = qw_cA if chain == "fp16" else qfh_an

            @blk.scalar
            def _(s, p=p, w_in=w_in, vq=qw_sbuf_guard[0], abuf=abuf):
                s_w[p].wait(s, w_in)
                if vq:
                    s.wait_ge(v_d.h, vq)
                s.activation(abuf[:, 0:n_cols], wraw2[:, p, 0:n_cols], AF.Abs,
                             scale=6.0)
                s.activation(qw_sg[:, 0:n_cols], wraw2[:, p, 0:n_cols],
                             AF.Sign).then_inc(a_d.h, 1)
            a_d.count += 1

            anb = qw_an if chain == "fp16" else qfh_an
            accb = qw_acc if chain == "fp16" else qfh_acc
            cAb = qw_cA if chain == "fp16" else qfh_cA
            abufb = qw_cA if chain == "fp16" else qfh_an

            @blk.vector
            def _(v, p=p, aa=a_d.count, nb=n_cols // 16):
                v.wait_ge(a_d.h, aa)
                v.tensor_reduce(qw_gm[:, 0:nb],
                                wraw2[:, p, 0:n_cols].rearrange("p (c k) -> p c k", k=16),
                                AX.X, OP.max, apply_absolute_value=True)
                v.tensor_scalar(qw_s[:, 0:nb], qw_gm[:, 0:nb], 6e-12, None, OP.max)
                v.tensor_scalar(qw_s2[:, 0:nb], qw_gm[:, 0:nb], 0.16666666666666666,
                                1e-12, OP.mult, OP.max)
                v.drain()
                v.reciprocal(qw_r[:, 0:nb], qw_s[:, 0:nb])
                v.tensor_copy(qw_s16[:, 0:nb], qw_s2[:, 0:nb])
                v.drain()
                v.engine_nop().then_inc(v_d.h, 1)
            v_d.count += 1
            qw_load_guard[p] = v_d.count

            @blk.gpsimd
            def _(g, vv=v_d.count, nb=n_cols // 16, anb=anb, abufb=abufb):
                g.wait_ge(v_d.h, vv)
                g.tensor_tensor(anb[:, 0:n_cols].rearrange("p (c k) -> p c k", k=16),
                                abufb[:, 0:n_cols].rearrange("p (c k) -> p c k", k=16),
                                qw_r[:, 0:nb].unsqueeze(2).broadcast_to([128, nb, 16]),
                                OP.mult).then_inc(g_c.h, 1)
            g_c.count += 1

            @blk.vector
            def _(v, gg=g_c.count, anb=anb, accb=accb, cAb=cAb):
                v.wait_ge(g_c.h, gg)
                q_chain(v, anb[:, 0:n_cols], accb[:, 0:n_cols],
                        cAb[:, 0:n_cols], accb[:, 0:n_cols], n_cols)
                v.drain()
                v.engine_nop().then_inc(v_d.h, 1)
            v_d.count += 1
            qw_sbuf_guard[0] = v_d.count

            @blk.gpsimd
            def _(g, vv=v_d.count, guard=list(wq_tp_guard[0]), nb=n_cols // 16,
                  anb=anb, accb=accb):
                g.wait_ge(v_d.h, vv)
                for c, val in guard:
                    g.wait_ge(s_tp[c].h, val)
                g.tensor_tensor(anb[:, 0:n_cols], accb[:, 0:n_cols],
                                qw_sg[:, 0:n_cols], OP.mult)
                g.tensor_tensor(qw_out[:, 0:n_cols].rearrange("p (c k) -> p c k", k=16),
                                anb[:, 0:n_cols].rearrange("p (c k) -> p c k", k=16),
                                qw_s16[:, 0:nb].unsqueeze(2).broadcast_to([128, nb, 16]),
                                OP.mult).then_inc(g_c.h, 1)
            g_c.count += 1

            @blk.sync
            def _(sync, gg=g_c.count, dw=s_dw.count):
                sync.wait_ge(g_c.h, gg)
                sync.wait_ge(s_dw.h, dw)
                for ci in range(nchunks):
                    s_tp[ci].dma(sync, out=dstTs[:, ci, :],
                                 in_=qw_out[:, 128 * ci:128 * (ci + 1)], transpose=True)
            tpc = [(ci, s_tp[ci].count) for ci in range(nchunks)]
            wq_tp_guard[0] = tpc

            @blk.gpsimd
            def _(g, tpc=list(tpc)):
                for ci, val in tpc:
                    g.wait_ge(s_tp[ci].h, val)
                s_dw.dma(g, out=dst_dr[:, c0:c0 + nchunks,
                                       128 * wt_col:128 * (wt_col + 1)],
                         in_=dstTs[:, 0:nchunks, :])
        return emit

    wq_units = []
    for wt in range(22):
        wq_units.append(make_wq_unit(wdram["w_gate"], 128 * wt, 0, D, "fp16",
                                     wqg_dr, 0, 8, wt))
    for wt in range(22):
        wq_units.append(make_wq_unit(wdram["w_up"], 128 * wt, 0, D, "fp16",
                                     wqu_dr, 0, 8, wt))
    for wt in range(8):
        for half in range(2):
            wq_units.append(make_wq_unit(wdram["w_down"], 128 * wt, HDI * half,
                                         HDI, "fp32", wqd_dr, 11 * half, 11, wt))
    wq_done_marks = {}
    wq_unit_idx = [0]

    def pull_wq_unit():
        i = wq_unit_idx[0]
        if i < len(wq_units):
            wq_units[i](i)
            wq_unit_idx[0] = i + 1
            if i + 1 == 22:
                wq_done_marks["gate"] = s_dw.count
            elif i + 1 == 44:
                wq_done_marks["up"] = s_dw.count
            elif i + 1 == 60:
                wq_done_marks["down"] = s_dw.count

    t_after_V = t_d.count

    @blk.gpsimd
    def _(g):
        g.wait_ge(v_d.h, qr_done_v)
        g.wait_ge(t_d.h, t_after_V)
        s_kh.dma(g, out=Kh[64:96, :], in_=KrT[:])

    ats_prev2 = [s_ats.count, s_ats.count]
    den_prev2 = [s_den.count, s_den.count]
    av_tail_a = [0, 0]
    av_tail_v = [0, 0]
    for h in range(NH):
        pipe = MMPipe()
        for (wten, dst, prow) in ((wb_UQ, Qh, 0), (wb_UK, Kh, 64)):
            for sc in range(4):
                ps = psC if (sc % 2 == 0) else psD
                def mm(t, wten=wten, sc=sc, ps=ps, prow=prow, h=h):
                    src = cqT if wten is wb_UQ else ckvT
                    for l in range(2):
                        i_ = t.matmul(ps[prow:prow + 64, :], wten[:, l, 64 * h:64 * (h + 1)],
                                      src[:, l, 512 * sc:512 * (sc + 1)],
                                      start=(l == 0), stop=(l == 1))
                    return i_
                def ev(s, wten=wten, dst=dst, sc=sc, ps=ps, prow=prow):
                    return s.activation(dst[0:64, 512 * sc:512 * (sc + 1)],
                                        ps[prow:prow + 64, :], AF.Copy)
                pipe.add(mm, ev)

        def pre(t, h=h):
            if h == 0:
                t.wait_ge(v_d.h, qr_done_v)
                s_kh.wait(t)
        pipe.emit(pre_tensor_wait=pre)

        @blk.gpsimd
        def _(g, h=h):
            a_d.wait(g)
            qtile, qoff = divmod(32 * h, 128)
            s_qh[h % 2].dma(g, out=Qh[64:96, :], in_=QrT[qoff:qoff + 32, qtile, :])
        qh_in = s_qh[h % 2].count

        a_qk = a_d.count

        for qi in range(NQ):
            nkt = 4 * qi + 4
            pa = (4 * h + qi) % 2
            psAVc = psAV2[pa]
            t0 = t_d.count
            a0 = a_d.count
            vv0 = v_d.count

            @blk.tensor
            def _(t, h=h, qi=qi, nkt=nkt, a0=a0, vv0=vv0, a_qk=a_qk, qh_in=qh_in,
                  psAVc=psAVc, ta=av_tail_a[pa], tv=av_tail_v[pa]):
                if qi == 0:
                    t.wait_ge(a_d.h, a_qk)
                    s_qh[h % 2].wait(t, qh_in)
                if ta:
                    t.wait_ge(a_d.h, ta)
                    t.wait_ge(v_d.h, tv)
                for kt in range(nkt):
                    ps = psS0 if (kt % 2 == 0) else psS1
                    if kt >= 2:
                        t.wait_ge(a_d.h, a0 + kt - 1)
                    t.matmul(ps[0:128, :], Kh[0:96, 128 * kt:128 * (kt + 1)],
                             Qh[0:96, QCH * qi:QCH * (qi + 1)],
                             start=True, stop=True).then_inc(t_d.h, 1)
                    if kt >= 1:
                        t.wait_ge(v_d.h, vv0 + kt)
                        t.matmul(psAVc[:, :], Vb[:, kt - 1, 65 * h:65 * (h + 1)],
                                 Eb[:, kt - 1, :], start=(kt - 1 == 0), stop=False)
                t.wait_ge(v_d.h, vv0 + nkt)
                t.matmul(psAVc[:, :], Vb[:, nkt - 1, 65 * h:65 * (h + 1)],
                         Eb[:, nkt - 1, :], start=False, stop=True).then_inc(t_d.h, 1)
            t_d.count += nkt + 1
            t_av_done = t_d.count

            @blk.scalar
            def _(s, nkt=nkt, t0=t0):
                for kt in range(nkt):
                    ps = psS0 if (kt % 2 == 0) else psS1
                    s.wait_ge(t_d.h, t0 + kt + 1)
                    s.activation(Eb[:, kt, :], ps[0:128, :],
                                 AF.Exp).then_inc(a_d.h, 1)
            a_d.count += nkt

            @blk.vector
            def _(v, qi=qi, nkt=nkt, a0=a0):
                for kt in range(nkt):
                    v.wait_ge(a_d.h, a0 + kt + 1)
                    v.tensor_scalar(Eb[:, kt, :], Eb[:, kt, :], E_LO, E_HI, OP.max, OP.min)
                    if kt >= 4 * qi:
                        v.tensor_tensor(Eb[:, kt, :], Eb[:, kt, :],
                                        mask_s[:, kt - 4 * qi, :], OP.mult)
                    v.engine_nop().then_inc(v_d.h, 1)
            v_d.count += nkt

            @blk.vector
            def _(v, pa=pa, t_av_done=t_av_done, psAVc=psAVc, dp=den_prev2[pa]):
                v.wait_ge(t_d.h, t_av_done)
                v.wait_ge(s_den.h, dp)
                v.reciprocal(rowr[:, pa, :], psAVc[64:65, :])
                v.drain()
                v.engine_nop().then_inc(v_d.h, 1)
            v_d.count += 1
            av_tail_v[pa] = v_d.count

            @blk.scalar
            def _(s, pa=pa, t_av_done=t_av_done, psAVc=psAVc, atp=ats_prev2[pa]):
                s.wait_ge(t_d.h, t_av_done)
                s.wait_ge(s_ats.h, atp)
                s.activation(attnSt[:, pa, :], psAVc[0:64, :],
                             AF.Copy).then_inc(a_d.h, 1)
            a_d.count += 1
            av_tail_a[pa] = a_d.count

            @blk.gpsimd
            def _(g, h=h, qi=qi, pa=pa, ea=av_tail_a[pa], ev=av_tail_v[pa]):
                g.wait_ge(a_d.h, ea)
                g.wait_ge(v_d.h, ev)
                s_ats.dma(g, out=at_dr[64 * (h % 2):64 * (h % 2) + 64, h // 2,
                                       QCH * qi:QCH * (qi + 1)],
                          in_=attnSt[:, pa, :])
                s_den.dma(g, out=den_dr[h:h + 1, QCH * qi:QCH * (qi + 1)],
                          in_=rowr[:, pa, :])
            ats_prev2[pa] = s_ats.count
            den_prev2[pa] = s_den.count
            pull_wq_unit()

    while wq_unit_idx[0] < len(wq_units):
        pull_wq_unit()

    attn_done_v = v_d.count

    a4_t_prev = [0, 0]
    a4_tt_prev = 0
    for st in range(NT):
        b4 = st % 2

        @blk.sync
        def _(sync, st=st, b4=b4, tfree=a4_t_prev[b4]):
            if st == 0:
                for i in range(2):
                    s_r0s[i].wait(sync)
                s_ats.wait(sync)
                s_den.wait(sync)
            v_d.wait(sync)
            sync.wait_ge(t_d.h, tfree)
            s_a4r[b4].dma(sync, out=r0st[:, b4, :],
                          in_=r0_dr[128 * st:128 * (st + 1), :])
            s_a4a[b4].dma(sync, out=attnTs[:, b4, :, :],
                          in_=at_dr[:, :, 128 * st:128 * (st + 1)])
            s_a4a[b4].dma(sync, out=den_s[:, b4, :],
                          in_=den_dr[:, 128 * st:128 * (st + 1)])
        r0_in = s_a4r[b4].count
        at_in = s_a4a[b4].count

        td0 = t_d.count
        vprev = v_d.count

        v0 = v_d.count

        @blk.tensor
        def _(t, st=st, b4=b4, at_in=at_in, ttp=a4_tt_prev, v0=v0):
            if st == 0:
                t.wait_ge(v_d.h, attn_done_v)
            s_a4a[b4].wait(t, at_in)
            if ttp:
                t.wait_ge(v_d.h, ttp)
            for c in range(8):
                ps = psS0 if c % 2 == 0 else psS1
                sl = (c % 4) // 2
                if c >= 2:
                    t.wait_ge(v_d.h, v0 + c - 1)
                t.matmul(ps[:, 128 * sl:128 * (sl + 1)], sel_s[:, c, :],
                         den_s[:, b4, :], start=True, stop=True).then_inc(t_d.h, 1)
        t_d.count += 8

        @blk.vector
        def _(v, st=st, b4=b4, td0=td0):
            for c in range(8):
                ps = psS0 if c % 2 == 0 else psS1
                sl = (c % 4) // 2
                v.wait_ge(t_d.h, td0 + c + 1)
                v.tensor_copy(denb[:, c % 4, :], ps[:, 128 * sl:128 * (sl + 1)])
                v.tensor_tensor(attnTs[:, b4, c, :], attnTs[:, b4, c, :],
                                denb[:, c % 4, :], OP.mult)
                v.engine_nop().then_inc(v_d.h, 1)
        v_d.count += 8
        a4_tt_prev = v_d.count

        t0 = t_d.count

        @blk.tensor
        def _(t, st=st, b4=b4, vprev=vprev, tte=a4_tt_prev):
            t.wait_ge(v_d.h, tte)
            for half in range(2):
                ps = psC if half == 0 else psB
                for c in range(8):
                    i_ = t.matmul(ps[0:128, :], attnTs[:, b4, c, :],
                                  wb_WO[:, c, 512 * half:512 * (half + 1)],
                                  start=(c == 0), stop=(c == 7))
                i_.then_inc(t_d.h, 1)
        t_d.count += 2
        a4_t_prev[b4] = t_d.count

        @blk.vector
        def _(v, st=st, b4=b4, t0=t0, r0_in=r0_in, r1f=s_r1s[b4].count):
            s_a4r[b4].wait(v, r0_in)
            v.wait_ge(s_r1s[b4].h, r1f)
            for half in range(2):
                ps = psC if half == 0 else psB
                v.wait_ge(t_d.h, t0 + half + 1)
                v.tensor_tensor(r1stA[:, b4, 512 * half:512 * (half + 1)], ps[0:128, :],
                                r0st[:, b4, 512 * half:512 * (half + 1)], OP.add)
                v.drain()
                v.engine_nop().then_inc(v_d.h, 1)
        v_d.count += 2

        @blk.gpsimd
        def _(g, st=st, b4=b4):
            v_d.wait(g)
            s_r1s[b4].dma(g, out=r1_dr[128 * st:128 * (st + 1), :],
                          in_=r1stA[:, b4, :])

    if dump:
        dmp = {}
        for nm, ten in (("Qh", Qh), ("Kh", Kh), ("Eb", Eb), ("Vb", Vb),
                        ("QrT", QrT), ("KrT", KrT), ("cqT", cqT), ("ckvT", ckvT)):
            shp = [int(d) for d in ten.shape]
            dmp[nm] = nc.dram_tensor("dump_" + nm, shp, BF16, kind="ExternalOutput")

        @blk.sync
        def _(sync):
            v_d.wait(sync)
            a_d.wait(sync)
            for nm, ten in (("Qh", Qh), ("Kh", Kh), ("Eb", Eb), ("Vb", Vb),
                            ("QrT", QrT), ("KrT", KrT), ("cqT", cqT), ("ckvT", ckvT)):
                s_out.dma(sync, out=dmp[nm][:], in_=ten[:])
            s_out.wait(sync)

    actxL.close()

    bctxL = ExitStack()
    def BL(name, shape, dtype):
        return bctxL.enter_context(nc.sbuf_tensor(name, shape, dtype, side="left"))

    wqdT = BL("wqdT", [128, 22, D], BF16)
    q_gm = BL("q_gm", [128, NB_I], F32)
    q_s = BL("q_s", [128, NB_I], F32)
    q_r = BL("q_r", [128, NB_I], F32)
    q_s2 = BL("q_s2", [128, NB_I], F32)
    q_s16 = BL("q_s16", [128, NB_I], FP16)
    q_an = BL("q_an", [128, HDI], FP16)
    q_acc = BL("q_acc", [128, HDI], FP16)
    q_cA = BL("q_cA", [128, DI], FP16)
    q_sg = BL("q_sg", [128, DI], FP16)
    q_out = BL("q_out", [128, DI], BF16)
    wqgT = BL("wqgT", [128, 8, DI], BF16)
    wquT = BL("wquT", [128, 8, DI], BF16)

    a4_end_t = t_d.count
    a4_end_v = v_d.count

    @blk.sync
    def _(sync):
        sync.wait_ge(t_d.h, a4_end_t)
        sync.wait_ge(v_d.h, a4_end_v)
        for i in range(2):
            s_r1s[i].wait(sync)
        sync.wait_ge(s_dw.h, wq_done_marks["gate"])
        for q in range(4):
            s_wq.dma(sync, out=wqgT[:, 2 * q:2 * (q + 1), :],
                     in_=wqg_dr[:, 2 * q:2 * (q + 1), :])
        sync.wait_ge(s_dw.h, wq_done_marks["up"])
        for q in range(4):
            s_wq.dma(sync, out=wquT[:, 2 * q:2 * (q + 1), :],
                     in_=wqu_dr[:, 2 * q:2 * (q + 1), :])
    wq_gate_ld = s_wq.count - 64
    wq_up_ld = s_wq.count

    @blk.gpsimd
    def _(g):
        g.wait_ge(t_d.h, a4_end_t)
        g.wait_ge(v_d.h, a4_end_v)
        for i in range(2):
            s_r1s[i].wait(g)
        g.wait_ge(s_dw.h, wq_done_marks["down"])
        for q in range(4):
            s_uv.dma(g, out=wqdT[:, 6 * q:min(6 * (q + 1), 22), :],
                     in_=wqd_dr[:, 6 * q:min(6 * (q + 1), 22), :])
    wq_down_ld = s_uv.count

    xqT = BL("xqT", [128, 2, 8, 128], BF16)
    hqT = BL("hqT", [128, 2, 22, 128], BF16)
    gsil = BL("gsil", [128, 2, DI], BF16)
    r1st = BL("r1st", [128, 2, D], BF16)
    q2_cA = BL("q2_cA", [128, D], FP16)
    q2_sg = BL("q2_sg", [128, D], FP16)
    outst = xst
    xq_out = r0st

    if dump:
        dbgB = {}
        for nm, cols in (("ln2", D), ("xq", D), ("h", DI), ("hq", DI)):
            dbgB[nm] = nc.dram_tensor("dbgB_" + nm, [NT, 128, cols], BF16,
                                      kind="ExternalOutput")

    g_chunks = [(i * 512, min(512, DI - i * 512)) for i in range((DI + 511) // 512)]
    NG = len(g_chunks)
    out_v_prev = 0
    r1_free = [0, 0]
    xq_v_prev = 0
    hq_v_prev_any = 0
    hq_v_prev2 = [0, 0]
    down_t_prev2 = [0, 0]
    gateup_t_prev2 = [0, 0]
    xqT_tp_prev2 = [[], []]
    silu_a_prev = 0
    hTT_v_prev = 0
    lnst_guard = 0
    out_store_prev2 = [0, 0]
    hq_tp_prev = [(c, s_tp[c].count) for c in range(22)]

    def emit_act_quant(v, src_ap, cA, sg, n, nb, out_ap):
        v.tensor_reduce(q_gm[:, 0:nb], src_ap.rearrange("p (c k) -> p c k", k=16),
                        AX.X, OP.max, apply_absolute_value=True)
        v.tensor_scalar(q_s[:, 0:nb], q_gm[:, 0:nb], 6e-12, None, OP.max)
        v.tensor_scalar(q_s2[:, 0:nb], q_gm[:, 0:nb], 0.16666666666666666, 1e-12,
                        OP.mult, OP.max)
        v.drain()
        v.reciprocal(q_r[:, 0:nb], q_s[:, 0:nb])
        v.tensor_copy(q_s16[:, 0:nb], q_s2[:, 0:nb])
        v.drain()
        for h0 in range(0, n, HDI):
            m = min(HDI, n - h0)
            mb = m // 16
            b0 = h0 // 16
            v.tensor_tensor(q_an[:, 0:m].rearrange("p (c k) -> p c k", k=16),
                            cA[:, h0:h0 + m].rearrange("p (c k) -> p c k", k=16),
                            q_r[:, b0:b0 + mb].unsqueeze(2).broadcast_to([128, mb, 16]),
                            OP.mult)
            q_chain(v, q_an[:, 0:m], q_acc[:, 0:m], cA[:, h0:h0 + m], q_acc[:, 0:m], m)
            v.tensor_tensor(q_an[:, 0:m], q_acc[:, 0:m], sg[:, h0:h0 + m], OP.mult)
            v.tensor_tensor(out_ap[:, h0:h0 + m].rearrange("p (c k) -> p c k", k=16),
                            q_an[:, 0:m].rearrange("p (c k) -> p c k", k=16),
                            q_s16[:, b0:b0 + mb].unsqueeze(2).broadcast_to([128, mb, 16]),
                            OP.mult)
        v.drain()

    for st in range(NT):
        b = st % 2
        sb = 8 * b

        @blk.sync
        def _(sync, st=st, b=b, r1f=r1_free[b]):
            if st == 0:
                for i in range(2):
                    s_r1s[i].wait(sync)
                s_dw.wait(sync)
            if r1f:
                sync.wait_ge(v_d.h, r1f)
            s_r1[b].dma(sync, out=r1st[:, b, :], in_=r1_dr[128 * st:128 * (st + 1), :])
        r1_in = s_r1[b].count

        @blk.vector
        def _(v, st=st, b=b, sb=sb, r1_in=r1_in):
            s_r1[b].wait(v, r1_in)
            v.reduce_sum(stat[:, sb:sb + 1], r1st[:, b, :], AX.X)
            v.tensor_tensor(sqsc[:], r1st[:, b, 0:512], r1st[:, b, 0:512], OP.mult)
            v.reduce_sum(stat[:, sb + 1:sb + 2], sqsc[:], AX.X)
            v.tensor_tensor(sqsc[:], r1st[:, b, 512:1024], r1st[:, b, 512:1024], OP.mult)
            v.reduce_sum(stat[:, sb + 5:sb + 6], sqsc[:], AX.X)
            v.drain()
            v.tensor_tensor(stat[:, sb + 1:sb + 2], stat[:, sb + 1:sb + 2],
                            stat[:, sb + 5:sb + 6], OP.add)
            v.tensor_scalar(stat[:, sb:sb + 1], stat[:, sb:sb + 1], 1.0 / D, None, OP.mult)
            v.tensor_scalar(stat[:, sb + 1:sb + 2], stat[:, sb + 1:sb + 2], 1.0 / D,
                            None, OP.mult)
            v.scalar_tensor_tensor(stat[:, sb + 2:sb + 3], stat[:, sb:sb + 1], 1.0,
                                   stat[:, sb:sb + 1], OP.mult, OP.mult)
            v.tensor_tensor(stat[:, sb + 1:sb + 2], stat[:, sb + 1:sb + 2],
                            stat[:, sb + 2:sb + 3], OP.subtract)
            v.tensor_scalar(stat[:, sb + 1:sb + 2], stat[:, sb + 1:sb + 2], LN_EPS,
                            None, OP.add)
            v.engine_nop().then_inc(v_d.h, 1)
        v_d.count += 1
        stats_v = v_d.count

        @blk.scalar
        def _(s, st=st, sb=sb, sv=stats_v):
            s.wait_ge(v_d.h, sv)
            s.activation(stat[:, sb + 3:sb + 4], stat[:, sb + 1:sb + 2],
                         AF.Sqrt).then_inc(a_d.h, 1)
        a_d.count += 1
        sqrt_a = a_d.count

        @blk.vector
        def _(v, st=st, b=b, sb=sb, sa=sqrt_a, lg=lnst_guard):
            v.wait_ge(a_d.h, sa)
            if lg:
                v.wait_ge(a_d.h, lg)
            v.reciprocal(stat[:, sb + 4:sb + 5], stat[:, sb + 3:sb + 4])
            v.tensor_scalar(lnst[:], r1st[:, b, :], stat[:, sb:sb + 1],
                            stat[:, sb + 4:sb + 5], OP.subtract, OP.mult)
            v.engine_nop().then_inc(v_d.h, 1)
        v_d.count += 1
        apply_v = v_d.count

        @blk.scalar
        def _(s, st=st, b=b, av=apply_v, xqp=xq_v_prev):
            s.wait_ge(v_d.h, av)
            if xqp:
                s.wait_ge(v_d.h, xqp)
            s.activation(q2_cA[:], lnst[:], AF.Abs, scale=6.0)
            s.activation(q2_sg[:], lnst[:], AF.Sign).then_inc(a_d.h, 1)
        a_d.count += 1
        xq_abs_a = a_d.count
        lnst_guard = xq_abs_a

        @blk.vector
        def _(v, st=st, b=b, aa=xq_abs_a, guard=list(xqT_tp_prev2[b])):
            v.wait_ge(a_d.h, aa)
            for c, val in guard:
                v.wait_ge(s_tp[c].h, val)
            emit_act_quant(v, lnst[:], q2_cA, q2_sg, D, NB_D, xq_out[:, b, 0:D])
            v.engine_nop().then_inc(v_d.h, 1)
        v_d.count += 1
        xq_v = v_d.count
        xq_v_prev = xq_v

        if dump:
            @blk.gpsimd
            def _(g, st=st, b=b, xq_v=xq_v):
                g.wait_ge(v_d.h, xq_v)
                s_dbg.dma(g, out=dbgB["ln2"][st, :, :], in_=lnst[:])
                s_dbg.dma(g, out=dbgB["xq"][st, :, :], in_=xq_out[:, b, 0:D])

        @blk.scalar
        def _(s, st=st, b=b, xq_v=xq_v, tg=gateup_t_prev2[b]):
            s.wait_ge(v_d.h, xq_v)
            if tg:
                s.wait_ge(t_d.h, tg)
            for c in range(8):
                s_tp[c].dma(s, out=xqT[:, b, c, :],
                            in_=xq_out[:, b, 128 * c:128 * (c + 1)], transpose=True)
        xq_tp = [(c, s_tp[c].count) for c in range(8)]
        xqT_tp_prev2[b] = xq_tp

        t0 = t_d.count
        a0 = a_d.count
        V0 = v_d.count

        @blk.tensor
        def _(t, st=st, b=b, a0=a0, V0=V0, xq_tp=list(xq_tp), sprev=silu_a_prev,
              hprev=hTT_v_prev):
            for c, val in xq_tp:
                t.wait_ge(s_tp[c].h, val)
            if st == 0:
                s_wq.wait(t, wq_up_ld)
            if sprev:
                t.wait_ge(a_d.h, sprev)
            for gi, (off, w) in enumerate(g_chunks):
                ps = psA if gi % 2 == 0 else psB
                if gi >= 2:
                    t.wait_ge(a_d.h, a0 + gi - 1)
                for kc in range(8):
                    i_ = t.matmul(ps[0:128, 0:w], xqT[:, b, kc, :], wqgT[:, kc, off:off + w],
                                  start=(kc == 0), stop=(kc == 7))
                i_.then_inc(t_d.h, 1)
            if hprev:
                t.wait_ge(v_d.h, hprev)
            for gi, (off, w) in enumerate(g_chunks):
                ps = psC if gi % 2 == 0 else psD
                if gi >= 2:
                    t.wait_ge(v_d.h, V0 + gi - 1)
                for kc in range(8):
                    i_ = t.matmul(ps[0:128, 0:w], xqT[:, b, kc, :], wquT[:, kc, off:off + w],
                                  start=(kc == 0), stop=(kc == 7))
                i_.then_inc(t_d.h, 1)
        t_d.count += 2 * NG
        gateup_t_prev2[b] = t_d.count

        @blk.scalar
        def _(s, st=st, b=b, t0=t0, hqp=hq_v_prev2[b]):
            if hqp:
                s.wait_ge(v_d.h, hqp)
            for gi, (off, w) in enumerate(g_chunks):
                ps = psA if gi % 2 == 0 else psB
                s.wait_ge(t_d.h, t0 + gi + 1)
                s.activation(gsil[:, b, off:off + w], ps[0:128, 0:w],
                             AF.Silu).then_inc(a_d.h, 1)
        a_d.count += NG
        silu_a_prev = a_d.count

        @blk.vector
        def _(v, st=st, b=b, t0=t0, a0=a0):
            for gi, (off, w) in enumerate(g_chunks):
                ps = psC if gi % 2 == 0 else psD
                v.wait_ge(t_d.h, t0 + NG + gi + 1)
                v.wait_ge(a_d.h, a0 + gi + 1)
                v.tensor_tensor(gsil[:, b, off:off + w], ps[0:128, 0:w],
                                gsil[:, b, off:off + w], OP.mult)
                v.engine_nop().then_inc(v_d.h, 1)
        v_d.count += NG
        hTT_v_prev = v_d.count
        h_v = v_d.count

        @blk.scalar
        def _(s, st=st, b=b, h_v=h_v, hqp=hq_v_prev_any):
            s.wait_ge(v_d.h, h_v)
            if hqp:
                s.wait_ge(v_d.h, hqp)
            s.activation(q_cA[:, 0:DI], gsil[:, b, :], AF.Abs, scale=6.0)
            s.activation(q_sg[:, 0:DI], gsil[:, b, :], AF.Sign).then_inc(a_d.h, 1)
        a_d.count += 1
        hq_abs_a = a_d.count

        @blk.vector
        def _(v, st=st, b=b, aa=hq_abs_a, guard=list(hq_tp_prev)):
            v.wait_ge(a_d.h, aa)
            for c, val in guard:
                v.wait_ge(s_tp[c].h, val)
            emit_act_quant(v, gsil[:, b, :], q_cA, q_sg, DI, NB_I, q_out[:, 0:DI])
            v.engine_nop().then_inc(v_d.h, 1)
        v_d.count += 1
        hq_v = v_d.count
        hq_v_prev_any = hq_v
        hq_v_prev2[b] = hq_v

        if dump:
            @blk.gpsimd
            def _(g, st=st, b=b, hq_v=hq_v):
                g.wait_ge(v_d.h, hq_v)
                s_dbg.dma(g, out=dbgB["h"][st, :, :], in_=gsil[:, b, :])
                s_dbg.dma(g, out=dbgB["hq"][st, :, :], in_=q_out[:, 0:DI])

        @blk.scalar
        def _(s, st=st, b=b, hq_v=hq_v, dtp=down_t_prev2[b]):
            s.wait_ge(v_d.h, hq_v)
            if dtp:
                s.wait_ge(t_d.h, dtp)
            for c in range(22):
                s_tp[c].dma(s, out=hqT[:, b, c, :], in_=q_out[:, 128 * c:128 * (c + 1)],
                            transpose=True)
        hq_tp = [(c, s_tp[c].count) for c in range(22)]
        hq_tp_prev = hq_tp

        t1 = t_d.count

        @blk.tensor
        def _(t, st=st, b=b, hq_tp=list(hq_tp), ovp=out_v_prev):
            for c, val in hq_tp:
                t.wait_ge(s_tp[c].h, val)
            if st == 0:
                s_uv.wait(t, wq_down_ld)
            if ovp:
                t.wait_ge(v_d.h, ovp)
            for half in range(2):
                ps = psS0 if half == 0 else psS1
                for kc in range(22):
                    i_ = t.matmul(ps[0:128, :], hqT[:, b, kc, :],
                                  wqdT[:, kc, 512 * half:512 * (half + 1)],
                                  start=(kc == 0), stop=(kc == 21))
                i_.then_inc(t_d.h, 1)
        t_d.count += 2
        down_t_prev2[b] = t_d.count

        @blk.vector
        def _(v, st=st, b=b, t1=t1, stp=out_store_prev2[b]):
            if stp:
                v.wait_ge(s_out.h, stp)
            for half in range(2):
                ps = psS0 if half == 0 else psS1
                v.wait_ge(t_d.h, t1 + half + 1)
                v.tensor_tensor(outst[:, b, 512 * half:512 * (half + 1)], ps[0:128, :],
                                r1st[:, b, 512 * half:512 * (half + 1)], OP.add)
            v.drain()
            v.engine_nop().then_inc(v_d.h, 1)
        v_d.count += 1
        out_v = v_d.count
        out_v_prev = out_v
        r1_free[b] = out_v

        @blk.sync
        def _(sync, st=st, b=b, out_v=out_v):
            sync.wait_ge(v_d.h, out_v)
            s_out.dma(sync, out=out_d[128 * st:128 * (st + 1), :], in_=outst[:, b, :])
        out_store_prev2[b] = s_out.count

    if dump:
        dmp2 = {}
        for nm, ten in (("wqgT", wqgT), ("wquT", wquT), ("wqdT", wqdT),
                        ("qout", q_out)):
            shp = [int(d) for d in ten.shape]
            dmp2[nm] = nc.dram_tensor("dumpB_" + nm, shp, BF16, kind="ExternalOutput")

        @blk.sync
        def _(sync):
            s_out.wait(sync)
            for nm, ten in (("wqgT", wqgT), ("wquT", wquT), ("wqdT", wqdT),
                            ("qout", q_out)):
                s_out.dma(sync, out=dmp2[nm][:], in_=ten[:])
            s_out.wait(sync)

    @blk.sync
    def _(sync):
        s_out.wait(sync)
        if dump:
            s_dbg.wait(sync)

    bctxL.close()
    ctx.close()
    return nc


_NC_CACHE = None


def kernel(L0_attn_x, L0_attn_W_DQ, L0_attn_W_UQ, L0_attn_W_QR, L0_attn_W_DKV,
           L0_attn_W_UK, L0_attn_W_UV, L0_attn_W_KR, L0_attn_W_O,
           L0_ffn_w_gate, L0_ffn_w_up, L0_ffn_w_down):
    global _NC_CACHE
    if _NC_CACHE is None:
        _NC_CACHE = build_nc()
    nc = _NC_CACHE

    cs, sn = _rope_tables()
    masks = _causal_masks()
    x = np.ascontiguousarray(np.asarray(L0_attn_x, dtype=np.float32))
    common = {
        "W_DQ": np.ascontiguousarray(np.asarray(L0_attn_W_DQ, np.float32)),
        "W_UQ": np.ascontiguousarray(np.asarray(L0_attn_W_UQ, np.float32)),
        "W_QR": np.ascontiguousarray(np.asarray(L0_attn_W_QR, np.float32)),
        "W_DKV": np.ascontiguousarray(np.asarray(L0_attn_W_DKV, np.float32)),
        "W_UK": np.ascontiguousarray(np.asarray(L0_attn_W_UK, np.float32)),
        "W_UV": np.ascontiguousarray(np.asarray(L0_attn_W_UV, np.float32)),
        "W_KR": np.ascontiguousarray(np.asarray(L0_attn_W_KR, np.float32)),
        "W_O": np.ascontiguousarray(np.asarray(L0_attn_W_O, np.float32)),
        "w_gate": np.ascontiguousarray(np.asarray(L0_ffn_w_gate, np.float32)),
        "w_up": np.ascontiguousarray(np.asarray(L0_ffn_w_up, np.float32)),
        "w_down": np.ascontiguousarray(np.asarray(L0_ffn_w_down, np.float32)),
        "rope_cos": cs, "rope_sin": sn, "masks": masks, "sel": _den_sel(),
    }
    in_maps = [dict(common, x=np.ascontiguousarray(x[c])) for c in range(8)]
    res = run_bass_kernel_spmd(nc, in_maps, core_ids=list(range(8)))
    out = np.stack([np.asarray(res.results[c]["out"], np.float32) for c in range(8)])
    return out


# version 31
# speedup vs baseline: 1.0013x; 1.0013x over previous
import numpy as np
import ml_dtypes
from contextlib import ExitStack

import concourse.bass as bass
import concourse.mybir as mybir
from concourse.bass_utils import run_bass_kernel_spmd

F32 = mybir.dt.float32
BF16 = mybir.dt.bfloat16
FP16 = mybir.dt.float16
OP = mybir.AluOpType
AF = mybir.ActivationFunctionType
AX = mybir.AxisListType

S, D, NH, DH, DHR, DL, DI = 2048, 1024, 16, 64, 32, 256, 2816
NT = S // 128
QCH = 512
NQ = S // QCH
NB_D = D // 16
NB_I = DI // 16

import math
import ctypes
C_SM = 1.0 / float(np.float32(np.sqrt(np.float32(96.0))))
C_HI = float(np.float32(C_SM))
C_LO = C_SM - C_HI
E_HI = math.exp(80.0 * C_SM)
E_LO = math.exp(-80.0 * C_SM)
LN_EPS = 1e-5

SWAP32 = [1, 0, 3, 2, 5, 4, 7, 6, 9, 8, 11, 10, 13, 12, 15, 14,
          17, 16, 19, 18, 21, 20, 23, 22, 25, 24, 27, 26, 29, 28, 31, 30]


def _rope_tables():
    libm = ctypes.CDLL("libm.so.6")
    libm.cosf.restype = ctypes.c_float
    libm.cosf.argtypes = [ctypes.c_float]
    libm.sinf.restype = ctypes.c_float
    libm.sinf.argtypes = [ctypes.c_float]
    inv = (10000.0 ** (-np.arange(0, DHR, 2, dtype=np.float64) / DHR)).astype(np.float32)
    ang = np.arange(S, dtype=np.float32)[:, None] * inv
    cos = np.array([[libm.cosf(float(a)) for a in row] for row in ang], np.float32)
    sin = np.array([[libm.sinf(float(a)) for a in row] for row in ang], np.float32)
    cs = np.zeros((128, S), np.float32)
    sn = np.zeros((128, S), np.float32)
    for p in range(128):
        c = p % DHR
        i = c // 2
        cs[p] = cos[:, i]
        sn[p] = (-sin[:, i]) if (c % 2 == 0) else sin[:, i]
    return cs, sn


def _den_sel():
    s = np.zeros((16, 8, 128), np.float32)
    for c in range(8):
        s[2 * c, c, 0:64] = 1.0
        s[2 * c + 1, c, 64:128] = 1.0
    return s


def _causal_masks():
    m = np.zeros((128, 4, QCH), np.float32)
    for j in range(4):
        for k in range(128):
            q0 = k + 128 * j
            if q0 < QCH:
                m[k, j, q0:] = 1.0
    return m


class Sem:
    def __init__(self, ctx, nc, name):
        self.h = ctx.enter_context(nc.semaphore(name))
        self.count = 0

    def wait(self, eng, at=None):
        eng.wait_ge(self.h, self.count if at is None else at)


class DStream:
    def __init__(self, ctx, nc, name):
        self.h = ctx.enter_context(nc.semaphore(name))
        self.count = 0

    def dma(self, q, out, in_, **kw):
        if self.count:
            q.wait_ge(self.h, self.count)
        q.dma_start(out=out, in_=in_, **kw).then_inc(self.h, 16)
        self.count += 16
        return self.count

    def wait(self, eng, at=None):
        eng.wait_ge(self.h, self.count if at is None else at)


def build_nc(dump=False, dump_scratch=False):
    nc = bass.Bass("TRN2", target_bir_lowering=False,
                   dynamic_dma_scratch_size=16384,
                   detect_race_conditions=False)
    scratch_kind = "ExternalOutput" if (dump or dump_scratch) else "Internal"
    ctx = ExitStack()

    x_d = nc.dram_tensor("x", [S, D], F32, kind="ExternalInput")
    wdram = {}
    for name, shape in [("W_DQ", [D, DL]), ("W_UQ", [DL, D]), ("W_QR", [DL, NH * DHR]),
                        ("W_DKV", [D, DL]), ("W_UK", [DL, D]), ("W_UV", [DL, D]),
                        ("W_KR", [D, DHR]), ("W_O", [D, D]),
                        ("w_gate", [DI, D]), ("w_up", [DI, D]), ("w_down", [D, DI])]:
        wdram[name] = nc.dram_tensor(name, shape, F32, kind="ExternalInput")
    cs_d = nc.dram_tensor("rope_cos", [128, S], F32, kind="ExternalInput")
    sn_d = nc.dram_tensor("rope_sin", [128, S], F32, kind="ExternalInput")
    mask_d = nc.dram_tensor("masks", [128, 4, QCH], F32, kind="ExternalInput")
    sel_d = nc.dram_tensor("sel", [16, 8, 128], F32, kind="ExternalInput")
    den_dr = nc.dram_tensor("den_scr", [16, S], F32, kind=scratch_kind)
    out_d = nc.dram_tensor("out", [S, D], F32, kind="ExternalOutput")
    r0_dr = nc.dram_tensor("r0_scr", [S, D], BF16, kind=scratch_kind)
    r1_dr = nc.dram_tensor("r1_scr", [S, D], BF16, kind=scratch_kind)
    at_dr = nc.dram_tensor("attn_scr", [128, 8, S], BF16, kind=scratch_kind)
    wqd_dr = nc.dram_tensor("wqd_scr", [128, 22, D], BF16, kind=scratch_kind)
    wqu_dr = nc.dram_tensor("wqu_scr", [128, 8, DI], BF16, kind=scratch_kind)
    wqg_dr = nc.dram_tensor("wqg_scr", [128, 8, DI], BF16, kind=scratch_kind)

    t_d = Sem(ctx, nc, "t_d")
    v_d = Sem(ctx, nc, "v_d")
    a_d = Sem(ctx, nc, "a_d")
    g_c = Sem(ctx, nc, "g_c")

    s_tab = DStream(ctx, nc, "s_tab")
    s_x = [DStream(ctx, nc, f"s_x{i}") for i in range(2)]
    s_a4r = [DStream(ctx, nc, f"s_a4r{i}") for i in range(2)]
    s_a4a = [DStream(ctx, nc, f"s_a4a{i}") for i in range(2)]
    s_w = [DStream(ctx, nc, f"s_w{i}") for i in range(2)]
    s_r1 = [DStream(ctx, nc, f"s_r1{i}") for i in range(2)]
    s_wl = DStream(ctx, nc, "s_wl")
    s_uv = DStream(ctx, nc, "s_uv")
    s_kh = DStream(ctx, nc, "s_kh")
    s_qh = [DStream(ctx, nc, f"s_qh{i}") for i in range(2)]
    s_r0s = [DStream(ctx, nc, f"s_r0s{i}") for i in range(2)]
    s_ats = DStream(ctx, nc, "s_ats")
    s_den = DStream(ctx, nc, "s_den")
    s_dw = DStream(ctx, nc, "s_dw")
    s_wq = DStream(ctx, nc, "s_wq")
    s_r1s = [DStream(ctx, nc, f"s_r1s{i}") for i in range(2)]
    s_tp = [DStream(ctx, nc, f"s_tp{i}") for i in range(22)]
    s_out = DStream(ctx, nc, "s_out")
    s_dbg = DStream(ctx, nc, "s_dbg")

    def AL(name, shape, dtype):
        return ctx.enter_context(nc.sbuf_tensor(name, shape, dtype, side="left"))

    stat = AL("stat", [128, 16], F32)
    sqsc = AL("sqsc", [128, 512], F32)
    lnst = AL("lnst", [128, D], BF16)
    ones1 = AL("ones1", [1, 64], F32)
    xst = AL("xst", [128, 2, D], F32)
    r0st = AL("r0st", [128, 2, D], BF16)

    blk = ctx.enter_context(nc.Block())

    actxL = ExitStack()
    def AAL(name, shape, dtype):
        return actxL.enter_context(nc.sbuf_tensor(name, shape, dtype, side="left"))
    cqT = AAL("cqT", [128, 2, S], BF16)
    ckvT = AAL("ckvT", [128, 2, S], BF16)
    Vb = AAL("Vb", [128, NT, NH * 65], BF16)
    QrT = AAL("QrT", [128, 4, S], BF16)
    KrT = AAL("KrT", [32, S], BF16)
    mask_s = AAL("mask", [128, 4, QCH], BF16)
    wb_UQ = AAL("wb_UQ", [128, 2, D], BF16)
    wb_UK = AAL("wb_UK", [128, 2, D], BF16)
    wb_QR = AAL("wb_QR", [128, 2, NH * DHR], BF16)
    wb_WO = AAL("wb_WO", [128, 8, D], BF16)
    rowr = AAL("rowr", [1, 2, QCH], F32)
    sel_s = AAL("sel_s", [16, 8, 128], F32)

    actxR = ExitStack()
    def AAR(name, shape, dtype):
        return actxR.enter_context(nc.sbuf_tensor(name, shape, dtype, side="right"))
    wb_DQ = AAR("wb_DQ", [128, 8, DL], BF16)
    wb_DKV = AAR("wb_DKV", [128, 8, DL], BF16)
    wb_KR = AAR("wb_KR", [128, 8, DHR], BF16)
    ln1T = AAR("ln1T", [128, 8, S], BF16)
    cs_s = AAR("cs", [128, S], F32)
    sn_s = AAR("sn", [128, S], F32)
    ropeA = AAR("ropeA", [128, S], F32)
    ropeB = AAR("ropeB", [128, S], F32)
    ropeC = AAR("ropeC", [128, S], F32)

    P = nc.alloc_psum_tensor
    psA = P("psA", [128, 512], F32)
    psB = P("psB", [128, 512], F32)
    psC = P("psC", [128, 512], F32)
    psD = P("psD", [128, 512], F32)
    psS0 = P("psS0", [128, QCH], F32)
    psS1 = P("psS1", [128, QCH], F32)
    psAV2 = [P("psAV0", [65, QCH], F32), P("psAV1", [65, QCH], F32)]

    @blk.sync
    def _(sync):
        s_tab.dma(sync, out=cs_s[:], in_=cs_d[:])
        s_tab.dma(sync, out=sn_s[:], in_=sn_d[:])

    @blk.gpsimd
    def _(g):
        for dst, name in ((wb_DQ, "W_DQ"), (wb_DKV, "W_DKV"), (wb_KR, "W_KR"),
                          (wb_UQ, "W_UQ"), (wb_QR, "W_QR"), (wb_UK, "W_UK"),
                          (wb_WO, "W_O")):
            s_wl.dma(g, out=dst[:],
                     in_=wdram[name][:].rearrange("(a p) l -> p a l", p=128))
        s_wl.dma(g, out=mask_s[:], in_=mask_d[:])
        s_wl.dma(g, out=sel_s[:], in_=sel_d[:])
        g.memset(ones1[:], 1.0).then_inc(g_c.h, 1)
        g.memset(Vb[:].rearrange("p t (h c) -> p t h c", c=65)[:, :, :, 64:65],
                 1.0).then_inc(g_c.h, 1)
    g_c.count += 2

    @blk.vector
    def _(v):
        s_wl.wait(v)
        for wt, ncol in ((wb_UK, 2 * D), (wb_KR, 8 * DHR)):
            v.tensor_scalar(ropeA[:, 0:ncol], wt[:].rearrange("p a l -> p (a l)"),
                            C_HI, None, OP.mult)
            v.drain()
            v.scalar_tensor_tensor(wt[:].rearrange("p a l -> p (a l)"),
                                   wt[:].rearrange("p a l -> p (a l)"), C_LO,
                                   ropeA[:, 0:ncol], OP.mult, OP.add)
            v.drain()
        v.engine_nop().then_inc(v_d.h, 1)
    v_d.count += 1
    kw_scaled_v = v_d.count

    xst_free_v = [0, 0]
    lnst_tp_prev = [s.count for s in s_tp[:8]]

    ln1_tp_tiles = []
    for st in range(NT):
        b = st % 2

        @blk.sync
        def _(sync, st=st, b=b, w=xst_free_v[b]):
            sync.wait_ge(v_d.h, w)
            s_x[b].dma(sync, out=xst[:, b, :], in_=x_d[128 * st:128 * (st + 1), :])
        x_in = s_x[b].count

        @blk.vector
        def _(v, st=st, b=b, x_in=x_in):
            s_x[b].wait(v, x_in)
            v.reduce_sum(stat[:, 0:1], xst[:, b, :], AX.X)
            v.tensor_tensor(sqsc[:], xst[:, b, 0:512], xst[:, b, 0:512], OP.mult)
            v.reduce_sum(stat[:, 1:2], sqsc[:], AX.X)
            v.tensor_tensor(sqsc[:], xst[:, b, 512:1024], xst[:, b, 512:1024], OP.mult)
            v.reduce_sum(stat[:, 5:6], sqsc[:], AX.X)
            v.drain()
            v.tensor_tensor(stat[:, 1:2], stat[:, 1:2], stat[:, 5:6], OP.add)
            v.tensor_scalar(stat[:, 0:1], stat[:, 0:1], 1.0 / D, None, OP.mult)
            v.tensor_scalar(stat[:, 1:2], stat[:, 1:2], 1.0 / D, None, OP.mult)
            v.scalar_tensor_tensor(stat[:, 2:3], stat[:, 0:1], 1.0, stat[:, 0:1],
                                   OP.mult, OP.mult)
            v.tensor_tensor(stat[:, 1:2], stat[:, 1:2], stat[:, 2:3], OP.subtract)
            v.tensor_scalar(stat[:, 1:2], stat[:, 1:2], LN_EPS, None, OP.add)
            v.engine_nop().then_inc(v_d.h, 1)
        v_d.count += 1

        @blk.scalar
        def _(s, st=st):
            v_d.wait(s)
            s.activation(stat[:, 3:4], stat[:, 1:2], AF.Sqrt).then_inc(a_d.h, 1)
        a_d.count += 1

        @blk.vector
        def _(v, st=st, b=b, tp=list(lnst_tp_prev), r0f=s_r0s[b].count):
            a_d.wait(v)
            for f in range(8):
                v.wait_ge(s_tp[f].h, tp[f])
            v.wait_ge(s_r0s[b].h, r0f)
            v.reciprocal(stat[:, 4:5], stat[:, 3:4])
            v.tensor_scalar(lnst[:], xst[:, b, :], stat[:, 0:1], stat[:, 4:5],
                            OP.subtract, OP.mult)
            v.tensor_copy(r0st[:, b, :], xst[:, b, :])
            v.drain()
            v.engine_nop().then_inc(v_d.h, 1)
        v_d.count += 1
        xst_free_v[b] = v_d.count

        @blk.gpsimd
        def _(g, st=st, b=b):
            v_d.wait(g)
            s_r0s[b].dma(g, out=r0_dr[128 * st:128 * (st + 1), :], in_=r0st[:, b, :])

        @blk.sync
        def _(sync, st=st):
            v_d.wait(sync)
            for f in range(8):
                s_tp[f].dma(sync, out=ln1T[:, f, 128 * st:128 * (st + 1)],
                            in_=lnst[:, 128 * f:128 * (f + 1)], transpose=True)
        lnst_tp_prev = [s_tp[f].count for f in range(8)]
        ln1_tp_tiles.append(list(lnst_tp_prev))

    class MMPipe:
        def __init__(self):
            self.jobs = []

        def add(self, mm_fn, evac_fn):
            self.jobs.append((mm_fn, evac_fn))

        def emit(self, pre_tensor_wait=None, banks=2):
            n0_t = t_d.count
            n0_a = a_d.count
            jobs = self.jobs

            @blk.tensor
            def _(t):
                if pre_tensor_wait is not None:
                    pre_tensor_wait(t)
                for i, (mm_fn, _e) in enumerate(jobs):
                    t.wait_ge(a_d.h, max(n0_a, n0_a + i - banks + 1))
                    mm_fn(t).then_inc(t_d.h, 1)

            @blk.scalar
            def _(s):
                for i, (_m, evac_fn) in enumerate(jobs):
                    s.wait_ge(t_d.h, n0_t + i + 1)
                    evac_fn(s).then_inc(a_d.h, 1)

            t_d.count += len(jobs)
            a_d.count += len(jobs)

    ln1_tp = [s_tp[f].count for f in range(8)]

    def wait_ln1(t):
        s_wl.wait(t)
        s_tab.wait(t)

    pipe = MMPipe()
    for (wten, dst) in ((wb_DQ, cqT), (wb_DKV, ckvT)):
        for l in range(2):
            for sc in range(4):
                ps = psA if (sc % 2 == 0) else psB
                def mm(t, wten=wten, l=l, sc=sc, ps=ps,
                       tp=list(ln1_tp_tiles[4 * sc + 3])):
                    for f in range(8):
                        t.wait_ge(s_tp[f].h, tp[f])
                    for f in range(8):
                        i_ = t.matmul(ps[0:128, :], wten[:, f, 128 * l:128 * (l + 1)],
                                      ln1T[:, f, 512 * sc:512 * (sc + 1)],
                                      start=(f == 0), stop=(f == 7))
                    return i_
                def ev(s, dst=dst, l=l, sc=sc, ps=ps):
                    return s.activation(dst[:, l, 512 * sc:512 * (sc + 1)],
                                        ps[0:128, :], AF.Copy)
                pipe.add(mm, ev)
    pipe.emit(pre_tensor_wait=wait_ln1)

    pipe = MMPipe()
    for sc in range(4):
        ps = psA if (sc % 2 == 0) else psB
        def mm(t, sc=sc, ps=ps):
            for f in range(8):
                i_ = t.matmul(ps[0:32, :], wb_KR[:, f, :],
                              ln1T[:, f, 512 * sc:512 * (sc + 1)],
                              start=(f == 0), stop=(f == 7))
            return i_
        def ev(s, sc=sc, ps=ps):
            return s.activation(ropeA[0:32, 512 * sc:512 * (sc + 1)], ps[0:32, :], AF.Copy)
        pipe.add(mm, ev)

    def wait_kscale(t):
        t.wait_ge(v_d.h, kw_scaled_v)
    pipe.emit(pre_tensor_wait=wait_kscale)

    @blk.vector
    def _(v):
        a_d.wait(v)
        v.stream_shuffle(ropeB[0:32, :], ropeA[0:32, :], SWAP32)
        v.tensor_tensor(ropeC[0:32, :], ropeA[0:32, :], cs_s[0:32, :], OP.mult)
        v.tensor_tensor(ropeB[0:32, :], ropeB[0:32, :], sn_s[0:32, :], OP.mult)
        v.tensor_tensor(ropeC[0:32, :], ropeC[0:32, :], ropeB[0:32, :], OP.add)
        v.tensor_copy(KrT[:], ropeC[0:32, :])
        v.engine_nop().then_inc(v_d.h, 1)
    v_d.count += 1

    for qt in range(4):
        rope_free = v_d.count
        pipe = MMPipe()
        for sc in range(4):
            ps = psA if (sc % 2 == 0) else psB
            def mm(t, qt=qt, sc=sc, ps=ps, rope_free=rope_free):
                if sc == 0:
                    t.wait_ge(v_d.h, rope_free)
                for l in range(2):
                    i_ = t.matmul(ps[0:128, :], wb_QR[:, l, 128 * qt:128 * (qt + 1)],
                                  cqT[:, l, 512 * sc:512 * (sc + 1)],
                                  start=(l == 0), stop=(l == 1))
                return i_
            def ev(s, sc=sc, ps=ps):
                return s.activation(ropeA[:, 512 * sc:512 * (sc + 1)], ps[0:128, :], AF.Copy)
            pipe.add(mm, ev)
        pipe.emit()

        @blk.vector
        def _(v, qt=qt):
            a_d.wait(v)
            v.stream_shuffle(ropeB[:], ropeA[:], SWAP32)
            v.tensor_tensor(ropeC[:], ropeA[:], cs_s[:], OP.mult)
            v.tensor_tensor(ropeB[:], ropeB[:], sn_s[:], OP.mult)
            v.tensor_tensor(ropeC[:], ropeC[:], ropeB[:], OP.add)
            v.tensor_copy(QrT[:, qt, :], ropeC[:])
            v.engine_nop().then_inc(v_d.h, 1)
        v_d.count += 1
    qr_done_v = v_d.count

    wb_UQv = AAR("wb_UV", [128, 2, D], BF16)
    pipe = MMPipe()
    for st in range(NT):
        for half in range(2):
            ps = psC if (half == 0) else psD
            def mm(t, st=st, half=half, ps=ps):
                for l in range(2):
                    i_ = t.matmul(ps[0:128, :], ckvT[:, l, 128 * st:128 * (st + 1)],
                                  wb_UQv[:, l, 512 * half:512 * (half + 1)],
                                  start=(l == 0), stop=(l == 1))
                return i_
            def ev(s, st=st, half=half, ps=ps):
                dst = Vb[:, st, :].rearrange("p (h c) -> p h c", c=65)[:, 8 * half:8 * (half + 1), 0:64]
                return s.activation(dst, ps[0:128, :].rearrange("p (h c) -> p h c", c=64),
                                    AF.Copy)
            pipe.add(mm, ev)

    @blk.gpsimd
    def _(g):
        s_uv.dma(g, out=wb_UQv[:],
                 in_=wdram["W_UV"][:].rearrange("(a p) l -> p a l", p=128))

    def wait_uv(t):
        s_uv.wait(t)
        g_c.wait(t)
    pipe.emit(pre_tensor_wait=wait_uv)

    actxR.close()
    Qh = AAL("Qh", [128, S], BF16)
    Kh = AAL("Kh", [128, S], BF16)
    Eb = AAL("Eb", [128, NT, QCH], BF16)
    attnSt = AAL("attnSt", [64, 2, QCH], BF16)
    attnTs = AAL("attnTs", [128, 2, 8, 128], BF16)
    r1stA = AAL("r1stA", [128, 2, D], BF16)
    den_s = AAL("den_s", [16, 2, 128], F32)
    denb = AAL("denb", [128, 4, 128], BF16)

    HDI = DI // 2
    wraw2 = AAL("wraw2", [128, 2, HDI], F32)
    qw_gm = AAL("qw_gm", [128, 128], F32)
    qw_s = AAL("qw_s", [128, 128], F32)
    qw_s2 = AAL("qw_s2", [128, 128], F32)
    qw_r = AAL("qw_r", [128, 128], F32)
    qw_s16 = AAL("qw_s16", [128, 128], FP16)
    qw_an = AAL("qw_an", [128, HDI], FP16)
    qw_acc = AAL("qw_acc", [128, HDI], FP16)
    qw_cA = AAL("qw_cA", [128, HDI], FP16)
    qw_sg = AAL("qw_sg", [128, HDI], FP16)
    qw_out = AAL("qw_out", [128, HDI], BF16)
    qfh_an = AAL("qfh_an", [128, HDI], F32)
    qfh_acc = AAL("qfh_acc", [128, HDI], F32)
    qfh_cA = AAL("qfh_cA", [128, HDI], F32)
    dstTs = AAL("dstTs", [128, 11, 128], BF16)

    def q_chain(v, an_ap, acc_ap, cA_ap, out_ap, n):
        v.tensor_scalar(acc_ap, an_ap, 5.0, 2.0, OP.is_gt, OP.mult)
        v.scalar_tensor_tensor(acc_ap, an_ap, 2.5, acc_ap, OP.is_gt, OP.add)
        v.scalar_tensor_tensor(acc_ap, an_ap, 3.5, acc_ap, OP.is_gt, OP.add)
        v.tensor_scalar(cA_ap, an_ap, 0.25, None, OP.is_gt)
        v.scalar_tensor_tensor(cA_ap, an_ap, 0.75, cA_ap, OP.is_gt, OP.add)
        v.scalar_tensor_tensor(cA_ap, an_ap, 1.25, cA_ap, OP.is_gt, OP.add)
        v.scalar_tensor_tensor(cA_ap, an_ap, 1.75, cA_ap, OP.is_gt, OP.add)
        v.scalar_tensor_tensor(out_ap, cA_ap, 0.5, acc_ap, OP.mult, OP.add)

    wq_tp_guard = [[]]
    qw_load_guard = [0, 0]
    qw_sbuf_guard = [0]

    def make_wq_unit(src_dram, row0, col0, n_cols, chain, dst_dr, c0, nchunks, wt_col):
        def emit(u):
            p = u % 2

            @blk.gpsimd
            def _(g, p=p, vq=qw_load_guard[p]):
                if vq:
                    g.wait_ge(v_d.h, vq)
                s_w[p].dma(g, out=wraw2[:, p, 0:n_cols],
                           in_=src_dram[row0:row0 + 128, col0:col0 + n_cols])
            w_in = s_w[p].count

            abuf = qw_cA if chain == "fp16" else qfh_an

            @blk.scalar
            def _(s, p=p, w_in=w_in, gq=qw_sbuf_guard[0], abuf=abuf):
                s_w[p].wait(s, w_in)
                if gq:
                    s.wait_ge(g_c.h, gq)
                s.activation(abuf[:, 0:n_cols], wraw2[:, p, 0:n_cols], AF.Abs,
                             scale=6.0)
                s.activation(qw_sg[:, 0:n_cols], wraw2[:, p, 0:n_cols],
                             AF.Sign).then_inc(a_d.h, 1)
            a_d.count += 1

            anb = qw_an if chain == "fp16" else qfh_an
            accb = qw_acc if chain == "fp16" else qfh_acc
            cAb = qw_cA if chain == "fp16" else qfh_cA
            abufb = qw_cA if chain == "fp16" else qfh_an

            @blk.vector
            def _(v, p=p, aa=a_d.count, nb=n_cols // 16):
                v.wait_ge(a_d.h, aa)
                v.tensor_reduce(qw_gm[:, 0:nb],
                                wraw2[:, p, 0:n_cols].rearrange("p (c k) -> p c k", k=16),
                                AX.X, OP.max, apply_absolute_value=True)
                v.tensor_scalar(qw_s[:, 0:nb], qw_gm[:, 0:nb], 6e-12, None, OP.max)
                v.tensor_scalar(qw_s2[:, 0:nb], qw_gm[:, 0:nb], 0.16666666666666666,
                                1e-12, OP.mult, OP.max)
                v.drain()
                v.reciprocal(qw_r[:, 0:nb], qw_s[:, 0:nb])
                v.tensor_copy(qw_s16[:, 0:nb], qw_s2[:, 0:nb])
                v.drain()
                v.engine_nop().then_inc(v_d.h, 1)
            v_d.count += 1
            qw_load_guard[p] = v_d.count

            @blk.gpsimd
            def _(g, vv=v_d.count, nb=n_cols // 16, anb=anb, abufb=abufb):
                g.wait_ge(v_d.h, vv)
                g.tensor_tensor(anb[:, 0:n_cols].rearrange("p (c k) -> p c k", k=16),
                                abufb[:, 0:n_cols].rearrange("p (c k) -> p c k", k=16),
                                qw_r[:, 0:nb].unsqueeze(2).broadcast_to([128, nb, 16]),
                                OP.mult).then_inc(g_c.h, 1)
            g_c.count += 1

            @blk.vector
            def _(v, gg=g_c.count, anb=anb, accb=accb, cAb=cAb):
                v.wait_ge(g_c.h, gg)
                q_chain(v, anb[:, 0:n_cols], accb[:, 0:n_cols],
                        cAb[:, 0:n_cols], accb[:, 0:n_cols], n_cols)
                v.drain()
                v.engine_nop().then_inc(v_d.h, 1)
            v_d.count += 1

            @blk.gpsimd
            def _(g, vv=v_d.count, guard=list(wq_tp_guard[0]), nb=n_cols // 16,
                  anb=anb, accb=accb):
                g.wait_ge(v_d.h, vv)
                for c, val in guard:
                    g.wait_ge(s_tp[c].h, val)
                g.tensor_tensor(anb[:, 0:n_cols], accb[:, 0:n_cols],
                                qw_sg[:, 0:n_cols], OP.mult)
                g.tensor_tensor(qw_out[:, 0:n_cols].rearrange("p (c k) -> p c k", k=16),
                                anb[:, 0:n_cols].rearrange("p (c k) -> p c k", k=16),
                                qw_s16[:, 0:nb].unsqueeze(2).broadcast_to([128, nb, 16]),
                                OP.mult).then_inc(g_c.h, 1)
            g_c.count += 1
            qw_sbuf_guard[0] = g_c.count

            @blk.sync
            def _(sync, gg=g_c.count, dw=s_dw.count):
                sync.wait_ge(g_c.h, gg)
                sync.wait_ge(s_dw.h, dw)
                for ci in range(nchunks):
                    s_tp[ci].dma(sync, out=dstTs[:, ci, :],
                                 in_=qw_out[:, 128 * ci:128 * (ci + 1)], transpose=True)
            tpc = [(ci, s_tp[ci].count) for ci in range(nchunks)]
            wq_tp_guard[0] = tpc

            @blk.gpsimd
            def _(g, tpc=list(tpc)):
                for ci, val in tpc:
                    g.wait_ge(s_tp[ci].h, val)
                s_dw.dma(g, out=dst_dr[:, c0:c0 + nchunks,
                                       128 * wt_col:128 * (wt_col + 1)],
                         in_=dstTs[:, 0:nchunks, :])
        return emit

    wq_units = []
    for wt in range(22):
        wq_units.append(make_wq_unit(wdram["w_gate"], 128 * wt, 0, D, "fp16",
                                     wqg_dr, 0, 8, wt))
    for wt in range(22):
        wq_units.append(make_wq_unit(wdram["w_up"], 128 * wt, 0, D, "fp16",
                                     wqu_dr, 0, 8, wt))
    for wt in range(8):
        for half in range(2):
            wq_units.append(make_wq_unit(wdram["w_down"], 128 * wt, HDI * half,
                                         HDI, "fp32", wqd_dr, 11 * half, 11, wt))
    wq_done_marks = {}
    wq_unit_idx = [0]

    def pull_wq_unit():
        i = wq_unit_idx[0]
        if i < len(wq_units):
            wq_units[i](i)
            wq_unit_idx[0] = i + 1
            if i + 1 == 22:
                wq_done_marks["gate"] = s_dw.count
            elif i + 1 == 44:
                wq_done_marks["up"] = s_dw.count
            elif i + 1 == 60:
                wq_done_marks["down"] = s_dw.count

    t_after_V = t_d.count

    @blk.gpsimd
    def _(g):
        g.wait_ge(v_d.h, qr_done_v)
        g.wait_ge(t_d.h, t_after_V)
        s_kh.dma(g, out=Kh[64:96, :], in_=KrT[:])

    ats_prev2 = [s_ats.count, s_ats.count]
    den_prev2 = [s_den.count, s_den.count]
    av_tail_a = [0, 0]
    av_tail_v = [0, 0]
    for h in range(NH):
        pipe = MMPipe()
        for (wten, dst, prow) in ((wb_UQ, Qh, 0), (wb_UK, Kh, 64)):
            for sc in range(4):
                ps = psC if (sc % 2 == 0) else psD
                def mm(t, wten=wten, sc=sc, ps=ps, prow=prow, h=h):
                    src = cqT if wten is wb_UQ else ckvT
                    for l in range(2):
                        i_ = t.matmul(ps[prow:prow + 64, :], wten[:, l, 64 * h:64 * (h + 1)],
                                      src[:, l, 512 * sc:512 * (sc + 1)],
                                      start=(l == 0), stop=(l == 1))
                    return i_
                def ev(s, wten=wten, dst=dst, sc=sc, ps=ps, prow=prow):
                    return s.activation(dst[0:64, 512 * sc:512 * (sc + 1)],
                                        ps[prow:prow + 64, :], AF.Copy)
                pipe.add(mm, ev)

        def pre(t, h=h):
            if h == 0:
                t.wait_ge(v_d.h, qr_done_v)
                s_kh.wait(t)
        pipe.emit(pre_tensor_wait=pre)

        @blk.gpsimd
        def _(g, h=h):
            a_d.wait(g)
            qtile, qoff = divmod(32 * h, 128)
            s_qh[h % 2].dma(g, out=Qh[64:96, :], in_=QrT[qoff:qoff + 32, qtile, :])
        qh_in = s_qh[h % 2].count

        a_qk = a_d.count

        for qi in range(NQ):
            nkt = 4 * qi + 4
            pa = (4 * h + qi) % 2
            psAVc = psAV2[pa]
            t0 = t_d.count
            a0 = a_d.count
            vv0 = v_d.count

            @blk.tensor
            def _(t, h=h, qi=qi, nkt=nkt, a0=a0, vv0=vv0, a_qk=a_qk, qh_in=qh_in,
                  psAVc=psAVc, ta=av_tail_a[pa], tv=av_tail_v[pa]):
                if qi == 0:
                    t.wait_ge(a_d.h, a_qk)
                    s_qh[h % 2].wait(t, qh_in)
                if ta:
                    t.wait_ge(a_d.h, ta)
                    t.wait_ge(v_d.h, tv)
                for kt in range(nkt):
                    ps = psS0 if (kt % 2 == 0) else psS1
                    if kt >= 2:
                        t.wait_ge(a_d.h, a0 + kt - 1)
                    t.matmul(ps[0:128, :], Kh[0:96, 128 * kt:128 * (kt + 1)],
                             Qh[0:96, QCH * qi:QCH * (qi + 1)],
                             start=True, stop=True).then_inc(t_d.h, 1)
                    if kt >= 1:
                        t.wait_ge(v_d.h, vv0 + kt)
                        t.matmul(psAVc[:, :], Vb[:, kt - 1, 65 * h:65 * (h + 1)],
                                 Eb[:, kt - 1, :], start=(kt - 1 == 0), stop=False)
                t.wait_ge(v_d.h, vv0 + nkt)
                t.matmul(psAVc[:, :], Vb[:, nkt - 1, 65 * h:65 * (h + 1)],
                         Eb[:, nkt - 1, :], start=False, stop=True).then_inc(t_d.h, 1)
            t_d.count += nkt + 1
            t_av_done = t_d.count

            @blk.scalar
            def _(s, nkt=nkt, t0=t0):
                for kt in range(nkt):
                    ps = psS0 if (kt % 2 == 0) else psS1
                    s.wait_ge(t_d.h, t0 + kt + 1)
                    s.activation(Eb[:, kt, :], ps[0:128, :],
                                 AF.Exp).then_inc(a_d.h, 1)
            a_d.count += nkt

            @blk.vector
            def _(v, qi=qi, nkt=nkt, a0=a0):
                for kt in range(nkt):
                    v.wait_ge(a_d.h, a0 + kt + 1)
                    v.tensor_scalar(Eb[:, kt, :], Eb[:, kt, :], E_LO, E_HI, OP.max, OP.min)
                    if kt >= 4 * qi:
                        v.tensor_tensor(Eb[:, kt, :], Eb[:, kt, :],
                                        mask_s[:, kt - 4 * qi, :], OP.mult)
                    v.engine_nop().then_inc(v_d.h, 1)
            v_d.count += nkt

            @blk.vector
            def _(v, pa=pa, t_av_done=t_av_done, psAVc=psAVc, dp=den_prev2[pa]):
                v.wait_ge(t_d.h, t_av_done)
                v.wait_ge(s_den.h, dp)
                v.reciprocal(rowr[:, pa, :], psAVc[64:65, :])
                v.drain()
                v.engine_nop().then_inc(v_d.h, 1)
            v_d.count += 1
            av_tail_v[pa] = v_d.count

            @blk.scalar
            def _(s, pa=pa, t_av_done=t_av_done, psAVc=psAVc, atp=ats_prev2[pa]):
                s.wait_ge(t_d.h, t_av_done)
                s.wait_ge(s_ats.h, atp)
                s.activation(attnSt[:, pa, :], psAVc[0:64, :],
                             AF.Copy).then_inc(a_d.h, 1)
            a_d.count += 1
            av_tail_a[pa] = a_d.count

            @blk.gpsimd
            def _(g, h=h, qi=qi, pa=pa, ea=av_tail_a[pa], ev=av_tail_v[pa]):
                g.wait_ge(a_d.h, ea)
                g.wait_ge(v_d.h, ev)
                s_ats.dma(g, out=at_dr[64 * (h % 2):64 * (h % 2) + 64, h // 2,
                                       QCH * qi:QCH * (qi + 1)],
                          in_=attnSt[:, pa, :])
                s_den.dma(g, out=den_dr[h:h + 1, QCH * qi:QCH * (qi + 1)],
                          in_=rowr[:, pa, :])
            ats_prev2[pa] = s_ats.count
            den_prev2[pa] = s_den.count
            pull_wq_unit()

    while wq_unit_idx[0] < len(wq_units):
        pull_wq_unit()

    attn_done_v = v_d.count

    a4_t_prev = [0, 0]
    a4_tt_prev = 0
    for st in range(NT):
        b4 = st % 2

        @blk.sync
        def _(sync, st=st, b4=b4, tfree=a4_t_prev[b4]):
            if st == 0:
                for i in range(2):
                    s_r0s[i].wait(sync)
                s_ats.wait(sync)
                s_den.wait(sync)
            v_d.wait(sync)
            sync.wait_ge(t_d.h, tfree)
            s_a4r[b4].dma(sync, out=r0st[:, b4, :],
                          in_=r0_dr[128 * st:128 * (st + 1), :])
            s_a4a[b4].dma(sync, out=attnTs[:, b4, :, :],
                          in_=at_dr[:, :, 128 * st:128 * (st + 1)])
            s_a4a[b4].dma(sync, out=den_s[:, b4, :],
                          in_=den_dr[:, 128 * st:128 * (st + 1)])
        r0_in = s_a4r[b4].count
        at_in = s_a4a[b4].count

        td0 = t_d.count
        vprev = v_d.count

        v0 = v_d.count

        @blk.tensor
        def _(t, st=st, b4=b4, at_in=at_in, ttp=a4_tt_prev, v0=v0):
            if st == 0:
                t.wait_ge(v_d.h, attn_done_v)
            s_a4a[b4].wait(t, at_in)
            if ttp:
                t.wait_ge(v_d.h, ttp)
            for c in range(8):
                ps = psS0 if c % 2 == 0 else psS1
                sl = (c % 4) // 2
                if c >= 2:
                    t.wait_ge(v_d.h, v0 + c - 1)
                t.matmul(ps[:, 128 * sl:128 * (sl + 1)], sel_s[:, c, :],
                         den_s[:, b4, :], start=True, stop=True).then_inc(t_d.h, 1)
        t_d.count += 8

        @blk.vector
        def _(v, st=st, b4=b4, td0=td0):
            for c in range(8):
                ps = psS0 if c % 2 == 0 else psS1
                sl = (c % 4) // 2
                v.wait_ge(t_d.h, td0 + c + 1)
                v.tensor_copy(denb[:, c % 4, :], ps[:, 128 * sl:128 * (sl + 1)])
                v.tensor_tensor(attnTs[:, b4, c, :], attnTs[:, b4, c, :],
                                denb[:, c % 4, :], OP.mult)
                v.engine_nop().then_inc(v_d.h, 1)
        v_d.count += 8
        a4_tt_prev = v_d.count

        t0 = t_d.count

        @blk.tensor
        def _(t, st=st, b4=b4, vprev=vprev, tte=a4_tt_prev):
            t.wait_ge(v_d.h, tte)
            for half in range(2):
                ps = psC if half == 0 else psB
                for c in range(8):
                    i_ = t.matmul(ps[0:128, :], attnTs[:, b4, c, :],
                                  wb_WO[:, c, 512 * half:512 * (half + 1)],
                                  start=(c == 0), stop=(c == 7))
                i_.then_inc(t_d.h, 1)
        t_d.count += 2
        a4_t_prev[b4] = t_d.count

        @blk.vector
        def _(v, st=st, b4=b4, t0=t0, r0_in=r0_in, r1f=s_r1s[b4].count):
            s_a4r[b4].wait(v, r0_in)
            v.wait_ge(s_r1s[b4].h, r1f)
            for half in range(2):
                ps = psC if half == 0 else psB
                v.wait_ge(t_d.h, t0 + half + 1)
                v.tensor_tensor(r1stA[:, b4, 512 * half:512 * (half + 1)], ps[0:128, :],
                                r0st[:, b4, 512 * half:512 * (half + 1)], OP.add)
                v.drain()
                v.engine_nop().then_inc(v_d.h, 1)
        v_d.count += 2

        @blk.gpsimd
        def _(g, st=st, b4=b4):
            v_d.wait(g)
            s_r1s[b4].dma(g, out=r1_dr[128 * st:128 * (st + 1), :],
                          in_=r1stA[:, b4, :])

    if dump:
        dmp = {}
        for nm, ten in (("Qh", Qh), ("Kh", Kh), ("Eb", Eb), ("Vb", Vb),
                        ("QrT", QrT), ("KrT", KrT), ("cqT", cqT), ("ckvT", ckvT)):
            shp = [int(d) for d in ten.shape]
            dmp[nm] = nc.dram_tensor("dump_" + nm, shp, BF16, kind="ExternalOutput")

        @blk.sync
        def _(sync):
            v_d.wait(sync)
            a_d.wait(sync)
            for nm, ten in (("Qh", Qh), ("Kh", Kh), ("Eb", Eb), ("Vb", Vb),
                            ("QrT", QrT), ("KrT", KrT), ("cqT", cqT), ("ckvT", ckvT)):
                s_out.dma(sync, out=dmp[nm][:], in_=ten[:])
            s_out.wait(sync)

    actxL.close()

    bctxL = ExitStack()
    def BL(name, shape, dtype):
        return bctxL.enter_context(nc.sbuf_tensor(name, shape, dtype, side="left"))

    wqdT = BL("wqdT", [128, 22, D], BF16)
    q_gm = BL("q_gm", [128, NB_I], F32)
    q_s = BL("q_s", [128, NB_I], F32)
    q_r = BL("q_r", [128, NB_I], F32)
    q_s2 = BL("q_s2", [128, NB_I], F32)
    q_s16 = BL("q_s16", [128, NB_I], FP16)
    q_an = BL("q_an", [128, HDI], FP16)
    q_acc = BL("q_acc", [128, HDI], FP16)
    q_cA = BL("q_cA", [128, DI], FP16)
    q_sg = BL("q_sg", [128, DI], FP16)
    q_out = BL("q_out", [128, DI], BF16)
    wqgT = BL("wqgT", [128, 8, DI], BF16)
    wquT = BL("wquT", [128, 8, DI], BF16)

    a4_end_t = t_d.count
    a4_end_v = v_d.count

    @blk.sync
    def _(sync):
        sync.wait_ge(t_d.h, a4_end_t)
        sync.wait_ge(v_d.h, a4_end_v)
        for i in range(2):
            s_r1s[i].wait(sync)
        sync.wait_ge(s_dw.h, wq_done_marks["gate"])
        for q in range(4):
            s_wq.dma(sync, out=wqgT[:, 2 * q:2 * (q + 1), :],
                     in_=wqg_dr[:, 2 * q:2 * (q + 1), :])
        sync.wait_ge(s_dw.h, wq_done_marks["up"])
        for q in range(4):
            s_wq.dma(sync, out=wquT[:, 2 * q:2 * (q + 1), :],
                     in_=wqu_dr[:, 2 * q:2 * (q + 1), :])
    wq_gate_ld = s_wq.count - 64
    wq_up_ld = s_wq.count

    @blk.gpsimd
    def _(g):
        g.wait_ge(t_d.h, a4_end_t)
        g.wait_ge(v_d.h, a4_end_v)
        for i in range(2):
            s_r1s[i].wait(g)
        g.wait_ge(s_dw.h, wq_done_marks["down"])
        for q in range(4):
            s_uv.dma(g, out=wqdT[:, 6 * q:min(6 * (q + 1), 22), :],
                     in_=wqd_dr[:, 6 * q:min(6 * (q + 1), 22), :])
    wq_down_ld = s_uv.count

    xqT = BL("xqT", [128, 2, 8, 128], BF16)
    hqT = BL("hqT", [128, 2, 22, 128], BF16)
    gsil = BL("gsil", [128, 2, DI], BF16)
    r1st = BL("r1st", [128, 2, D], BF16)
    q2_cA = BL("q2_cA", [128, D], FP16)
    q2_sg = BL("q2_sg", [128, D], FP16)
    outst = xst
    xq_out = r0st

    if dump:
        dbgB = {}
        for nm, cols in (("ln2", D), ("xq", D), ("h", DI), ("hq", DI)):
            dbgB[nm] = nc.dram_tensor("dbgB_" + nm, [NT, 128, cols], BF16,
                                      kind="ExternalOutput")

    g_chunks = [(i * 512, min(512, DI - i * 512)) for i in range((DI + 511) // 512)]
    NG = len(g_chunks)
    out_v_prev = 0
    r1_free = [0, 0]
    xq_v_prev = 0
    hq_v_prev_any = 0
    hq_v_prev2 = [0, 0]
    down_t_prev2 = [0, 0]
    gateup_t_prev2 = [0, 0]
    xqT_tp_prev2 = [[], []]
    silu_a_prev = 0
    hTT_v_prev = 0
    lnst_guard = 0
    out_store_prev2 = [0, 0]
    hq_tp_prev = [(c, s_tp[c].count) for c in range(22)]

    def emit_act_quant(v, src_ap, cA, sg, n, nb, out_ap):
        v.tensor_reduce(q_gm[:, 0:nb], src_ap.rearrange("p (c k) -> p c k", k=16),
                        AX.X, OP.max, apply_absolute_value=True)
        v.tensor_scalar(q_s[:, 0:nb], q_gm[:, 0:nb], 6e-12, None, OP.max)
        v.tensor_scalar(q_s2[:, 0:nb], q_gm[:, 0:nb], 0.16666666666666666, 1e-12,
                        OP.mult, OP.max)
        v.drain()
        v.reciprocal(q_r[:, 0:nb], q_s[:, 0:nb])
        v.tensor_copy(q_s16[:, 0:nb], q_s2[:, 0:nb])
        v.drain()
        for h0 in range(0, n, HDI):
            m = min(HDI, n - h0)
            mb = m // 16
            b0 = h0 // 16
            v.tensor_tensor(q_an[:, 0:m].rearrange("p (c k) -> p c k", k=16),
                            cA[:, h0:h0 + m].rearrange("p (c k) -> p c k", k=16),
                            q_r[:, b0:b0 + mb].unsqueeze(2).broadcast_to([128, mb, 16]),
                            OP.mult)
            q_chain(v, q_an[:, 0:m], q_acc[:, 0:m], cA[:, h0:h0 + m], q_acc[:, 0:m], m)
            v.tensor_tensor(q_an[:, 0:m], q_acc[:, 0:m], sg[:, h0:h0 + m], OP.mult)
            v.tensor_tensor(out_ap[:, h0:h0 + m].rearrange("p (c k) -> p c k", k=16),
                            q_an[:, 0:m].rearrange("p (c k) -> p c k", k=16),
                            q_s16[:, b0:b0 + mb].unsqueeze(2).broadcast_to([128, mb, 16]),
                            OP.mult)
        v.drain()

    for st in range(NT):
        b = st % 2
        sb = 8 * b

        @blk.sync
        def _(sync, st=st, b=b, r1f=r1_free[b]):
            if st == 0:
                for i in range(2):
                    s_r1s[i].wait(sync)
                s_dw.wait(sync)
            if r1f:
                sync.wait_ge(v_d.h, r1f)
            s_r1[b].dma(sync, out=r1st[:, b, :], in_=r1_dr[128 * st:128 * (st + 1), :])
        r1_in = s_r1[b].count

        @blk.vector
        def _(v, st=st, b=b, sb=sb, r1_in=r1_in):
            s_r1[b].wait(v, r1_in)
            v.reduce_sum(stat[:, sb:sb + 1], r1st[:, b, :], AX.X)
            v.tensor_tensor(sqsc[:], r1st[:, b, 0:512], r1st[:, b, 0:512], OP.mult)
            v.reduce_sum(stat[:, sb + 1:sb + 2], sqsc[:], AX.X)
            v.tensor_tensor(sqsc[:], r1st[:, b, 512:1024], r1st[:, b, 512:1024], OP.mult)
            v.reduce_sum(stat[:, sb + 5:sb + 6], sqsc[:], AX.X)
            v.drain()
            v.tensor_tensor(stat[:, sb + 1:sb + 2], stat[:, sb + 1:sb + 2],
                            stat[:, sb + 5:sb + 6], OP.add)
            v.tensor_scalar(stat[:, sb:sb + 1], stat[:, sb:sb + 1], 1.0 / D, None, OP.mult)
            v.tensor_scalar(stat[:, sb + 1:sb + 2], stat[:, sb + 1:sb + 2], 1.0 / D,
                            None, OP.mult)
            v.scalar_tensor_tensor(stat[:, sb + 2:sb + 3], stat[:, sb:sb + 1], 1.0,
                                   stat[:, sb:sb + 1], OP.mult, OP.mult)
            v.tensor_tensor(stat[:, sb + 1:sb + 2], stat[:, sb + 1:sb + 2],
                            stat[:, sb + 2:sb + 3], OP.subtract)
            v.tensor_scalar(stat[:, sb + 1:sb + 2], stat[:, sb + 1:sb + 2], LN_EPS,
                            None, OP.add)
            v.engine_nop().then_inc(v_d.h, 1)
        v_d.count += 1
        stats_v = v_d.count

        @blk.scalar
        def _(s, st=st, sb=sb, sv=stats_v):
            s.wait_ge(v_d.h, sv)
            s.activation(stat[:, sb + 3:sb + 4], stat[:, sb + 1:sb + 2],
                         AF.Sqrt).then_inc(a_d.h, 1)
        a_d.count += 1
        sqrt_a = a_d.count

        @blk.vector
        def _(v, st=st, b=b, sb=sb, sa=sqrt_a, lg=lnst_guard):
            v.wait_ge(a_d.h, sa)
            if lg:
                v.wait_ge(a_d.h, lg)
            v.reciprocal(stat[:, sb + 4:sb + 5], stat[:, sb + 3:sb + 4])
            v.tensor_scalar(lnst[:], r1st[:, b, :], stat[:, sb:sb + 1],
                            stat[:, sb + 4:sb + 5], OP.subtract, OP.mult)
            v.engine_nop().then_inc(v_d.h, 1)
        v_d.count += 1
        apply_v = v_d.count

        @blk.scalar
        def _(s, st=st, b=b, av=apply_v, xqp=xq_v_prev):
            s.wait_ge(v_d.h, av)
            if xqp:
                s.wait_ge(v_d.h, xqp)
            s.activation(q2_cA[:], lnst[:], AF.Abs, scale=6.0)
            s.activation(q2_sg[:], lnst[:], AF.Sign).then_inc(a_d.h, 1)
        a_d.count += 1
        xq_abs_a = a_d.count
        lnst_guard = xq_abs_a

        @blk.vector
        def _(v, st=st, b=b, aa=xq_abs_a, guard=list(xqT_tp_prev2[b])):
            v.wait_ge(a_d.h, aa)
            for c, val in guard:
                v.wait_ge(s_tp[c].h, val)
            emit_act_quant(v, lnst[:], q2_cA, q2_sg, D, NB_D, xq_out[:, b, 0:D])
            v.engine_nop().then_inc(v_d.h, 1)
        v_d.count += 1
        xq_v = v_d.count
        xq_v_prev = xq_v

        if dump:
            @blk.gpsimd
            def _(g, st=st, b=b, xq_v=xq_v):
                g.wait_ge(v_d.h, xq_v)
                s_dbg.dma(g, out=dbgB["ln2"][st, :, :], in_=lnst[:])
                s_dbg.dma(g, out=dbgB["xq"][st, :, :], in_=xq_out[:, b, 0:D])

        @blk.scalar
        def _(s, st=st, b=b, xq_v=xq_v, tg=gateup_t_prev2[b]):
            s.wait_ge(v_d.h, xq_v)
            if tg:
                s.wait_ge(t_d.h, tg)
            for c in range(8):
                s_tp[c].dma(s, out=xqT[:, b, c, :],
                            in_=xq_out[:, b, 128 * c:128 * (c + 1)], transpose=True)
        xq_tp = [(c, s_tp[c].count) for c in range(8)]
        xqT_tp_prev2[b] = xq_tp

        t0 = t_d.count
        a0 = a_d.count
        V0 = v_d.count

        @blk.tensor
        def _(t, st=st, b=b, a0=a0, V0=V0, xq_tp=list(xq_tp), sprev=silu_a_prev,
              hprev=hTT_v_prev):
            for c, val in xq_tp:
                t.wait_ge(s_tp[c].h, val)
            if st == 0:
                s_wq.wait(t, wq_up_ld)
            if sprev:
                t.wait_ge(a_d.h, sprev)
            for gi, (off, w) in enumerate(g_chunks):
                ps = psA if gi % 2 == 0 else psB
                if gi >= 2:
                    t.wait_ge(a_d.h, a0 + gi - 1)
                for kc in range(8):
                    i_ = t.matmul(ps[0:128, 0:w], xqT[:, b, kc, :], wqgT[:, kc, off:off + w],
                                  start=(kc == 0), stop=(kc == 7))
                i_.then_inc(t_d.h, 1)
            if hprev:
                t.wait_ge(v_d.h, hprev)
            for gi, (off, w) in enumerate(g_chunks):
                ps = psC if gi % 2 == 0 else psD
                if gi >= 2:
                    t.wait_ge(v_d.h, V0 + gi - 1)
                for kc in range(8):
                    i_ = t.matmul(ps[0:128, 0:w], xqT[:, b, kc, :], wquT[:, kc, off:off + w],
                                  start=(kc == 0), stop=(kc == 7))
                i_.then_inc(t_d.h, 1)
        t_d.count += 2 * NG
        gateup_t_prev2[b] = t_d.count

        @blk.scalar
        def _(s, st=st, b=b, t0=t0, hqp=hq_v_prev2[b]):
            if hqp:
                s.wait_ge(v_d.h, hqp)
            for gi, (off, w) in enumerate(g_chunks):
                ps = psA if gi % 2 == 0 else psB
                s.wait_ge(t_d.h, t0 + gi + 1)
                s.activation(gsil[:, b, off:off + w], ps[0:128, 0:w],
                             AF.Silu).then_inc(a_d.h, 1)
        a_d.count += NG
        silu_a_prev = a_d.count

        @blk.vector
        def _(v, st=st, b=b, t0=t0, a0=a0):
            for gi, (off, w) in enumerate(g_chunks):
                ps = psC if gi % 2 == 0 else psD
                v.wait_ge(t_d.h, t0 + NG + gi + 1)
                v.wait_ge(a_d.h, a0 + gi + 1)
                v.tensor_tensor(gsil[:, b, off:off + w], ps[0:128, 0:w],
                                gsil[:, b, off:off + w], OP.mult)
                v.engine_nop().then_inc(v_d.h, 1)
        v_d.count += NG
        hTT_v_prev = v_d.count
        h_v = v_d.count

        @blk.scalar
        def _(s, st=st, b=b, h_v=h_v, hqp=hq_v_prev_any):
            s.wait_ge(v_d.h, h_v)
            if hqp:
                s.wait_ge(v_d.h, hqp)
            s.activation(q_cA[:, 0:DI], gsil[:, b, :], AF.Abs, scale=6.0)
            s.activation(q_sg[:, 0:DI], gsil[:, b, :], AF.Sign).then_inc(a_d.h, 1)
        a_d.count += 1
        hq_abs_a = a_d.count

        @blk.vector
        def _(v, st=st, b=b, aa=hq_abs_a, guard=list(hq_tp_prev)):
            v.wait_ge(a_d.h, aa)
            for c, val in guard:
                v.wait_ge(s_tp[c].h, val)
            emit_act_quant(v, gsil[:, b, :], q_cA, q_sg, DI, NB_I, q_out[:, 0:DI])
            v.engine_nop().then_inc(v_d.h, 1)
        v_d.count += 1
        hq_v = v_d.count
        hq_v_prev_any = hq_v
        hq_v_prev2[b] = hq_v

        if dump:
            @blk.gpsimd
            def _(g, st=st, b=b, hq_v=hq_v):
                g.wait_ge(v_d.h, hq_v)
                s_dbg.dma(g, out=dbgB["h"][st, :, :], in_=gsil[:, b, :])
                s_dbg.dma(g, out=dbgB["hq"][st, :, :], in_=q_out[:, 0:DI])

        @blk.scalar
        def _(s, st=st, b=b, hq_v=hq_v, dtp=down_t_prev2[b]):
            s.wait_ge(v_d.h, hq_v)
            if dtp:
                s.wait_ge(t_d.h, dtp)
            for c in range(22):
                s_tp[c].dma(s, out=hqT[:, b, c, :], in_=q_out[:, 128 * c:128 * (c + 1)],
                            transpose=True)
        hq_tp = [(c, s_tp[c].count) for c in range(22)]
        hq_tp_prev = hq_tp

        t1 = t_d.count

        @blk.tensor
        def _(t, st=st, b=b, hq_tp=list(hq_tp), ovp=out_v_prev):
            for c, val in hq_tp:
                t.wait_ge(s_tp[c].h, val)
            if st == 0:
                s_uv.wait(t, wq_down_ld)
            if ovp:
                t.wait_ge(v_d.h, ovp)
            for half in range(2):
                ps = psS0 if half == 0 else psS1
                for kc in range(22):
                    i_ = t.matmul(ps[0:128, :], hqT[:, b, kc, :],
                                  wqdT[:, kc, 512 * half:512 * (half + 1)],
                                  start=(kc == 0), stop=(kc == 21))
                i_.then_inc(t_d.h, 1)
        t_d.count += 2
        down_t_prev2[b] = t_d.count

        @blk.vector
        def _(v, st=st, b=b, t1=t1, stp=out_store_prev2[b]):
            if stp:
                v.wait_ge(s_out.h, stp)
            for half in range(2):
                ps = psS0 if half == 0 else psS1
                v.wait_ge(t_d.h, t1 + half + 1)
                v.tensor_tensor(outst[:, b, 512 * half:512 * (half + 1)], ps[0:128, :],
                                r1st[:, b, 512 * half:512 * (half + 1)], OP.add)
            v.drain()
            v.engine_nop().then_inc(v_d.h, 1)
        v_d.count += 1
        out_v = v_d.count
        out_v_prev = out_v
        r1_free[b] = out_v

        @blk.sync
        def _(sync, st=st, b=b, out_v=out_v):
            sync.wait_ge(v_d.h, out_v)
            s_out.dma(sync, out=out_d[128 * st:128 * (st + 1), :], in_=outst[:, b, :])
        out_store_prev2[b] = s_out.count

    if dump:
        dmp2 = {}
        for nm, ten in (("wqgT", wqgT), ("wquT", wquT), ("wqdT", wqdT),
                        ("qout", q_out)):
            shp = [int(d) for d in ten.shape]
            dmp2[nm] = nc.dram_tensor("dumpB_" + nm, shp, BF16, kind="ExternalOutput")

        @blk.sync
        def _(sync):
            s_out.wait(sync)
            for nm, ten in (("wqgT", wqgT), ("wquT", wquT), ("wqdT", wqdT),
                            ("qout", q_out)):
                s_out.dma(sync, out=dmp2[nm][:], in_=ten[:])
            s_out.wait(sync)

    @blk.sync
    def _(sync):
        s_out.wait(sync)
        if dump:
            s_dbg.wait(sync)

    bctxL.close()
    ctx.close()
    return nc


_NC_CACHE = None


def kernel(L0_attn_x, L0_attn_W_DQ, L0_attn_W_UQ, L0_attn_W_QR, L0_attn_W_DKV,
           L0_attn_W_UK, L0_attn_W_UV, L0_attn_W_KR, L0_attn_W_O,
           L0_ffn_w_gate, L0_ffn_w_up, L0_ffn_w_down):
    global _NC_CACHE
    if _NC_CACHE is None:
        _NC_CACHE = build_nc()
    nc = _NC_CACHE

    cs, sn = _rope_tables()
    masks = _causal_masks()
    x = np.ascontiguousarray(np.asarray(L0_attn_x, dtype=np.float32))
    common = {
        "W_DQ": np.ascontiguousarray(np.asarray(L0_attn_W_DQ, np.float32)),
        "W_UQ": np.ascontiguousarray(np.asarray(L0_attn_W_UQ, np.float32)),
        "W_QR": np.ascontiguousarray(np.asarray(L0_attn_W_QR, np.float32)),
        "W_DKV": np.ascontiguousarray(np.asarray(L0_attn_W_DKV, np.float32)),
        "W_UK": np.ascontiguousarray(np.asarray(L0_attn_W_UK, np.float32)),
        "W_UV": np.ascontiguousarray(np.asarray(L0_attn_W_UV, np.float32)),
        "W_KR": np.ascontiguousarray(np.asarray(L0_attn_W_KR, np.float32)),
        "W_O": np.ascontiguousarray(np.asarray(L0_attn_W_O, np.float32)),
        "w_gate": np.ascontiguousarray(np.asarray(L0_ffn_w_gate, np.float32)),
        "w_up": np.ascontiguousarray(np.asarray(L0_ffn_w_up, np.float32)),
        "w_down": np.ascontiguousarray(np.asarray(L0_ffn_w_down, np.float32)),
        "rope_cos": cs, "rope_sin": sn, "masks": masks, "sel": _den_sel(),
    }
    in_maps = [dict(common, x=np.ascontiguousarray(x[c])) for c in range(8)]
    res = run_bass_kernel_spmd(nc, in_maps, core_ids=list(range(8)))
    out = np.stack([np.asarray(res.results[c]["out"], np.float32) for c in range(8)])
    return out


# version 32
# speedup vs baseline: 1.0352x; 1.0338x over previous
import numpy as np
import ml_dtypes
from contextlib import ExitStack

import concourse.bass as bass
import concourse.mybir as mybir
from concourse.bass_utils import run_bass_kernel_spmd

F32 = mybir.dt.float32
BF16 = mybir.dt.bfloat16
FP16 = mybir.dt.float16
OP = mybir.AluOpType
AF = mybir.ActivationFunctionType
AX = mybir.AxisListType

S, D, NH, DH, DHR, DL, DI = 2048, 1024, 16, 64, 32, 256, 2816
NT = S // 128
QCH = 512
NQ = S // QCH
NB_D = D // 16
NB_I = DI // 16

import math
import ctypes
C_SM = 1.0 / float(np.float32(np.sqrt(np.float32(96.0))))
C_HI = float(np.float32(C_SM))
C_LO = C_SM - C_HI
E_HI = math.exp(80.0 * C_SM)
E_LO = math.exp(-80.0 * C_SM)
LN_EPS = 1e-5

SWAP32 = [1, 0, 3, 2, 5, 4, 7, 6, 9, 8, 11, 10, 13, 12, 15, 14,
          17, 16, 19, 18, 21, 20, 23, 22, 25, 24, 27, 26, 29, 28, 31, 30]


def _rope_tables():
    libm = ctypes.CDLL("libm.so.6")
    libm.cosf.restype = ctypes.c_float
    libm.cosf.argtypes = [ctypes.c_float]
    libm.sinf.restype = ctypes.c_float
    libm.sinf.argtypes = [ctypes.c_float]
    inv = (10000.0 ** (-np.arange(0, DHR, 2, dtype=np.float64) / DHR)).astype(np.float32)
    ang = np.arange(S, dtype=np.float32)[:, None] * inv
    cos = np.array([[libm.cosf(float(a)) for a in row] for row in ang], np.float32)
    sin = np.array([[libm.sinf(float(a)) for a in row] for row in ang], np.float32)
    cs = np.zeros((128, S), np.float32)
    sn = np.zeros((128, S), np.float32)
    for p in range(128):
        c = p % DHR
        i = c // 2
        cs[p] = cos[:, i]
        sn[p] = (-sin[:, i]) if (c % 2 == 0) else sin[:, i]
    return cs, sn


def _den_sel():
    s = np.zeros((16, 8, 128), np.float32)
    for c in range(8):
        s[2 * c, c, 0:64] = 1.0
        s[2 * c + 1, c, 64:128] = 1.0
    return s


def _causal_masks():
    m = np.zeros((128, 4, QCH), np.float32)
    for j in range(4):
        for k in range(128):
            q0 = k + 128 * j
            if q0 < QCH:
                m[k, j, q0:] = 1.0
    return m


class Sem:
    def __init__(self, ctx, nc, name):
        self.h = ctx.enter_context(nc.semaphore(name))
        self.count = 0

    def wait(self, eng, at=None):
        eng.wait_ge(self.h, self.count if at is None else at)


class DStream:
    def __init__(self, ctx, nc, name):
        self.h = ctx.enter_context(nc.semaphore(name))
        self.count = 0

    def dma(self, q, out, in_, **kw):
        if self.count:
            q.wait_ge(self.h, self.count)
        q.dma_start(out=out, in_=in_, **kw).then_inc(self.h, 16)
        self.count += 16
        return self.count

    def wait(self, eng, at=None):
        eng.wait_ge(self.h, self.count if at is None else at)


def build_nc(dump=False, dump_scratch=False):
    nc = bass.Bass("TRN2", target_bir_lowering=False,
                   dynamic_dma_scratch_size=16384,
                   detect_race_conditions=False)
    scratch_kind = "ExternalOutput" if (dump or dump_scratch) else "Internal"
    ctx = ExitStack()

    x_d = nc.dram_tensor("x", [S, D], F32, kind="ExternalInput")
    wdram = {}
    for name, shape in [("W_DQ", [D, DL]), ("W_UQ", [DL, D]), ("W_QR", [DL, NH * DHR]),
                        ("W_DKV", [D, DL]), ("W_UK", [DL, D]), ("W_UV", [DL, D]),
                        ("W_KR", [D, DHR]), ("W_O", [D, D]),
                        ("w_gate", [DI, D]), ("w_up", [DI, D]), ("w_down", [D, DI])]:
        wdram[name] = nc.dram_tensor(name, shape, F32, kind="ExternalInput")
    cs_d = nc.dram_tensor("rope_cos", [128, S], F32, kind="ExternalInput")
    sn_d = nc.dram_tensor("rope_sin", [128, S], F32, kind="ExternalInput")
    mask_d = nc.dram_tensor("masks", [128, 4, QCH], F32, kind="ExternalInput")
    sel_d = nc.dram_tensor("sel", [16, 8, 128], F32, kind="ExternalInput")
    den_dr = nc.dram_tensor("den_scr", [16, S], F32, kind=scratch_kind)
    out_d = nc.dram_tensor("out", [S, D], F32, kind="ExternalOutput")
    r0_dr = nc.dram_tensor("r0_scr", [S, D], BF16, kind=scratch_kind)
    r1_dr = nc.dram_tensor("r1_scr", [S, D], BF16, kind=scratch_kind)
    at_dr = nc.dram_tensor("attn_scr", [128, 8, S], BF16, kind=scratch_kind)
    wqd_dr = nc.dram_tensor("wqd_scr", [128, 22, D], BF16, kind=scratch_kind)
    wqu_dr = nc.dram_tensor("wqu_scr", [128, 8, DI], BF16, kind=scratch_kind)
    wqg_dr = nc.dram_tensor("wqg_scr", [128, 8, DI], BF16, kind=scratch_kind)

    t_d = Sem(ctx, nc, "t_d")
    v_d = Sem(ctx, nc, "v_d")
    a_d = Sem(ctx, nc, "a_d")
    g_c = Sem(ctx, nc, "g_c")

    s_tab = DStream(ctx, nc, "s_tab")
    s_x = [DStream(ctx, nc, f"s_x{i}") for i in range(2)]
    s_a4r = [DStream(ctx, nc, f"s_a4r{i}") for i in range(2)]
    s_a4a = [DStream(ctx, nc, f"s_a4a{i}") for i in range(2)]
    s_w = [DStream(ctx, nc, f"s_w{i}") for i in range(2)]
    s_r1 = [DStream(ctx, nc, f"s_r1{i}") for i in range(2)]
    s_wl = DStream(ctx, nc, "s_wl")
    s_uv = DStream(ctx, nc, "s_uv")
    s_kh = DStream(ctx, nc, "s_kh")
    s_qh = [DStream(ctx, nc, f"s_qh{i}") for i in range(2)]
    s_r0s = [DStream(ctx, nc, f"s_r0s{i}") for i in range(2)]
    s_ats = DStream(ctx, nc, "s_ats")
    s_den = DStream(ctx, nc, "s_den")
    s_dw = DStream(ctx, nc, "s_dw")
    s_wq = DStream(ctx, nc, "s_wq")
    s_r1s = [DStream(ctx, nc, f"s_r1s{i}") for i in range(2)]
    s_tp = [DStream(ctx, nc, f"s_tp{i}") for i in range(22)]
    s_out = DStream(ctx, nc, "s_out")
    s_dbg = DStream(ctx, nc, "s_dbg")

    def AL(name, shape, dtype):
        return ctx.enter_context(nc.sbuf_tensor(name, shape, dtype, side="left"))

    stat = AL("stat", [128, 16], F32)
    sqsc = AL("sqsc", [128, 512], F32)
    lnst = AL("lnst", [128, D], BF16)
    ones1 = AL("ones1", [1, 64], F32)
    xst = AL("xst", [128, 2, D], F32)
    r0st = AL("r0st", [128, 2, D], BF16)

    blk = ctx.enter_context(nc.Block())

    actxL = ExitStack()
    def AAL(name, shape, dtype):
        return actxL.enter_context(nc.sbuf_tensor(name, shape, dtype, side="left"))
    cqT = AAL("cqT", [128, 2, S], BF16)
    ckvT = AAL("ckvT", [128, 2, S], BF16)
    Vb = AAL("Vb", [128, NT, NH * 65], BF16)
    QrT = AAL("QrT", [128, 4, S], BF16)
    KrT = AAL("KrT", [32, S], BF16)
    mask_s = AAL("mask", [128, 4, QCH], BF16)
    wb_UQ = AAL("wb_UQ", [128, 2, D], BF16)
    wb_UK = AAL("wb_UK", [128, 2, D], BF16)
    wb_QR = AAL("wb_QR", [128, 2, NH * DHR], BF16)
    wb_WO = AAL("wb_WO", [128, 8, D], BF16)
    rowr = AAL("rowr", [1, 2, QCH], F32)
    sel_s = AAL("sel_s", [16, 8, 128], F32)

    actxR = ExitStack()
    def AAR(name, shape, dtype):
        return actxR.enter_context(nc.sbuf_tensor(name, shape, dtype, side="right"))
    wb_DQ = AAR("wb_DQ", [128, 8, DL], BF16)
    wb_DKV = AAR("wb_DKV", [128, 8, DL], BF16)
    wb_KR = AAR("wb_KR", [128, 8, DHR], BF16)
    ln1T = AAR("ln1T", [128, 8, S], BF16)
    cs_s = AAR("cs", [128, S], F32)
    sn_s = AAR("sn", [128, S], F32)
    ropeA = AAR("ropeA", [128, S], F32)
    ropeB = AAR("ropeB", [128, S], F32)
    ropeC = AAR("ropeC", [128, S], F32)

    P = nc.alloc_psum_tensor
    psA = P("psA", [128, 512], F32)
    psB = P("psB", [128, 512], F32)
    psC = P("psC", [128, 512], F32)
    psD = P("psD", [128, 512], F32)
    psS0 = P("psS0", [128, QCH], F32)
    psS1 = P("psS1", [128, QCH], F32)
    psAV2 = [P("psAV0", [65, QCH], F32), P("psAV1", [65, QCH], F32)]

    @blk.sync
    def _(sync):
        s_tab.dma(sync, out=cs_s[:], in_=cs_d[:])
        s_tab.dma(sync, out=sn_s[:], in_=sn_d[:])

    @blk.gpsimd
    def _(g):
        for dst, name in ((wb_DQ, "W_DQ"), (wb_DKV, "W_DKV"), (wb_KR, "W_KR"),
                          (wb_UQ, "W_UQ"), (wb_QR, "W_QR"), (wb_UK, "W_UK"),
                          (wb_WO, "W_O")):
            s_wl.dma(g, out=dst[:],
                     in_=wdram[name][:].rearrange("(a p) l -> p a l", p=128))
        s_wl.dma(g, out=mask_s[:], in_=mask_d[:])
        s_wl.dma(g, out=sel_s[:], in_=sel_d[:])
        g.memset(ones1[:], 1.0).then_inc(g_c.h, 1)
        g.memset(Vb[:].rearrange("p t (h c) -> p t h c", c=65)[:, :, :, 64:65],
                 1.0).then_inc(g_c.h, 1)
    g_c.count += 2

    @blk.vector
    def _(v):
        s_wl.wait(v)
        for wt, ncol in ((wb_UK, 2 * D), (wb_KR, 8 * DHR)):
            v.tensor_scalar(ropeA[:, 0:ncol], wt[:].rearrange("p a l -> p (a l)"),
                            C_HI, None, OP.mult)
            v.drain()
            v.scalar_tensor_tensor(wt[:].rearrange("p a l -> p (a l)"),
                                   wt[:].rearrange("p a l -> p (a l)"), C_LO,
                                   ropeA[:, 0:ncol], OP.mult, OP.add)
            v.drain()
        v.engine_nop().then_inc(v_d.h, 1)
    v_d.count += 1
    kw_scaled_v = v_d.count

    xst_free_v = [0, 0]
    lnst_tp_prev = [s.count for s in s_tp[:8]]

    ln1_tp_tiles = []
    for st in range(NT):
        b = st % 2

        @blk.sync
        def _(sync, st=st, b=b, w=xst_free_v[b]):
            sync.wait_ge(v_d.h, w)
            s_x[b].dma(sync, out=xst[:, b, :], in_=x_d[128 * st:128 * (st + 1), :])
        x_in = s_x[b].count

        @blk.vector
        def _(v, st=st, b=b, x_in=x_in):
            s_x[b].wait(v, x_in)
            v.reduce_sum(stat[:, 0:1], xst[:, b, :], AX.X)
            v.tensor_tensor(sqsc[:], xst[:, b, 0:512], xst[:, b, 0:512], OP.mult)
            v.reduce_sum(stat[:, 1:2], sqsc[:], AX.X)
            v.tensor_tensor(sqsc[:], xst[:, b, 512:1024], xst[:, b, 512:1024], OP.mult)
            v.reduce_sum(stat[:, 5:6], sqsc[:], AX.X)
            v.drain()
            v.tensor_tensor(stat[:, 1:2], stat[:, 1:2], stat[:, 5:6], OP.add)
            v.tensor_scalar(stat[:, 0:1], stat[:, 0:1], 1.0 / D, None, OP.mult)
            v.tensor_scalar(stat[:, 1:2], stat[:, 1:2], 1.0 / D, None, OP.mult)
            v.scalar_tensor_tensor(stat[:, 2:3], stat[:, 0:1], 1.0, stat[:, 0:1],
                                   OP.mult, OP.mult)
            v.tensor_tensor(stat[:, 1:2], stat[:, 1:2], stat[:, 2:3], OP.subtract)
            v.tensor_scalar(stat[:, 1:2], stat[:, 1:2], LN_EPS, None, OP.add)
            v.engine_nop().then_inc(v_d.h, 1)
        v_d.count += 1

        @blk.scalar
        def _(s, st=st):
            v_d.wait(s)
            s.activation(stat[:, 3:4], stat[:, 1:2], AF.Sqrt).then_inc(a_d.h, 1)
        a_d.count += 1

        @blk.vector
        def _(v, st=st, b=b, tp=list(lnst_tp_prev), r0f=s_r0s[b].count):
            a_d.wait(v)
            for f in range(8):
                v.wait_ge(s_tp[f].h, tp[f])
            v.wait_ge(s_r0s[b].h, r0f)
            v.reciprocal(stat[:, 4:5], stat[:, 3:4])
            v.tensor_scalar(lnst[:], xst[:, b, :], stat[:, 0:1], stat[:, 4:5],
                            OP.subtract, OP.mult)
            v.tensor_copy(r0st[:, b, :], xst[:, b, :])
            v.drain()
            v.engine_nop().then_inc(v_d.h, 1)
        v_d.count += 1
        xst_free_v[b] = v_d.count

        @blk.gpsimd
        def _(g, st=st, b=b):
            v_d.wait(g)
            s_r0s[b].dma(g, out=r0_dr[128 * st:128 * (st + 1), :], in_=r0st[:, b, :])

        @blk.sync
        def _(sync, st=st):
            v_d.wait(sync)
            for f in range(8):
                s_tp[f].dma(sync, out=ln1T[:, f, 128 * st:128 * (st + 1)],
                            in_=lnst[:, 128 * f:128 * (f + 1)], transpose=True)
        lnst_tp_prev = [s_tp[f].count for f in range(8)]
        ln1_tp_tiles.append(list(lnst_tp_prev))

    class MMPipe:
        def __init__(self):
            self.jobs = []

        def add(self, mm_fn, evac_fn):
            self.jobs.append((mm_fn, evac_fn))

        def emit(self, pre_tensor_wait=None, banks=2):
            n0_t = t_d.count
            n0_a = a_d.count
            jobs = self.jobs

            @blk.tensor
            def _(t):
                if pre_tensor_wait is not None:
                    pre_tensor_wait(t)
                for i, (mm_fn, _e) in enumerate(jobs):
                    t.wait_ge(a_d.h, max(n0_a, n0_a + i - banks + 1))
                    mm_fn(t).then_inc(t_d.h, 1)

            @blk.scalar
            def _(s):
                for i, (_m, evac_fn) in enumerate(jobs):
                    s.wait_ge(t_d.h, n0_t + i + 1)
                    evac_fn(s).then_inc(a_d.h, 1)

            t_d.count += len(jobs)
            a_d.count += len(jobs)

    ln1_tp = [s_tp[f].count for f in range(8)]

    def wait_ln1(t):
        s_wl.wait(t)
        s_tab.wait(t)

    pipe = MMPipe()
    for (wten, dst) in ((wb_DQ, cqT), (wb_DKV, ckvT)):
        for l in range(2):
            for sc in range(4):
                ps = psA if (sc % 2 == 0) else psB
                def mm(t, wten=wten, l=l, sc=sc, ps=ps,
                       tp=list(ln1_tp_tiles[4 * sc + 3])):
                    for f in range(8):
                        t.wait_ge(s_tp[f].h, tp[f])
                    for f in range(8):
                        i_ = t.matmul(ps[0:128, :], wten[:, f, 128 * l:128 * (l + 1)],
                                      ln1T[:, f, 512 * sc:512 * (sc + 1)],
                                      start=(f == 0), stop=(f == 7))
                    return i_
                def ev(s, dst=dst, l=l, sc=sc, ps=ps):
                    return s.activation(dst[:, l, 512 * sc:512 * (sc + 1)],
                                        ps[0:128, :], AF.Copy)
                pipe.add(mm, ev)
    pipe.emit(pre_tensor_wait=wait_ln1)

    pipe = MMPipe()
    for sc in range(4):
        ps = psA if (sc % 2 == 0) else psB
        def mm(t, sc=sc, ps=ps):
            for f in range(8):
                i_ = t.matmul(ps[0:32, :], wb_KR[:, f, :],
                              ln1T[:, f, 512 * sc:512 * (sc + 1)],
                              start=(f == 0), stop=(f == 7))
            return i_
        def ev(s, sc=sc, ps=ps):
            return s.activation(ropeA[0:32, 512 * sc:512 * (sc + 1)], ps[0:32, :], AF.Copy)
        pipe.add(mm, ev)

    def wait_kscale(t):
        t.wait_ge(v_d.h, kw_scaled_v)
    pipe.emit(pre_tensor_wait=wait_kscale)

    @blk.vector
    def _(v):
        a_d.wait(v)
        v.stream_shuffle(ropeB[0:32, :], ropeA[0:32, :], SWAP32)
        v.tensor_tensor(ropeC[0:32, :], ropeA[0:32, :], cs_s[0:32, :], OP.mult)
        v.tensor_tensor(ropeB[0:32, :], ropeB[0:32, :], sn_s[0:32, :], OP.mult)
        v.tensor_tensor(ropeC[0:32, :], ropeC[0:32, :], ropeB[0:32, :], OP.add)
        v.tensor_copy(KrT[:], ropeC[0:32, :])
        v.engine_nop().then_inc(v_d.h, 1)
    v_d.count += 1

    for qt in range(4):
        rope_free = v_d.count
        pipe = MMPipe()
        for sc in range(4):
            ps = psA if (sc % 2 == 0) else psB
            def mm(t, qt=qt, sc=sc, ps=ps, rope_free=rope_free):
                if sc == 0:
                    t.wait_ge(v_d.h, rope_free)
                for l in range(2):
                    i_ = t.matmul(ps[0:128, :], wb_QR[:, l, 128 * qt:128 * (qt + 1)],
                                  cqT[:, l, 512 * sc:512 * (sc + 1)],
                                  start=(l == 0), stop=(l == 1))
                return i_
            def ev(s, sc=sc, ps=ps):
                return s.activation(ropeA[:, 512 * sc:512 * (sc + 1)], ps[0:128, :], AF.Copy)
            pipe.add(mm, ev)
        pipe.emit()

        @blk.vector
        def _(v, qt=qt):
            a_d.wait(v)
            v.stream_shuffle(ropeB[:], ropeA[:], SWAP32)
            v.tensor_tensor(ropeC[:], ropeA[:], cs_s[:], OP.mult)
            v.tensor_tensor(ropeB[:], ropeB[:], sn_s[:], OP.mult)
            v.tensor_tensor(ropeC[:], ropeC[:], ropeB[:], OP.add)
            v.tensor_copy(QrT[:, qt, :], ropeC[:])
            v.engine_nop().then_inc(v_d.h, 1)
        v_d.count += 1
    qr_done_v = v_d.count

    wb_UQv = AAR("wb_UV", [128, 2, D], BF16)
    pipe = MMPipe()
    for st in range(NT):
        for half in range(2):
            ps = psC if (half == 0) else psD
            def mm(t, st=st, half=half, ps=ps):
                for l in range(2):
                    i_ = t.matmul(ps[0:128, :], ckvT[:, l, 128 * st:128 * (st + 1)],
                                  wb_UQv[:, l, 512 * half:512 * (half + 1)],
                                  start=(l == 0), stop=(l == 1))
                return i_
            def ev(s, st=st, half=half, ps=ps):
                dst = Vb[:, st, :].rearrange("p (h c) -> p h c", c=65)[:, 8 * half:8 * (half + 1), 0:64]
                return s.activation(dst, ps[0:128, :].rearrange("p (h c) -> p h c", c=64),
                                    AF.Copy)
            pipe.add(mm, ev)

    @blk.gpsimd
    def _(g):
        s_uv.dma(g, out=wb_UQv[:],
                 in_=wdram["W_UV"][:].rearrange("(a p) l -> p a l", p=128))

    def wait_uv(t):
        s_uv.wait(t)
        g_c.wait(t)
    pipe.emit(pre_tensor_wait=wait_uv)

    actxR.close()
    Qh = AAL("Qh", [128, S], BF16)
    Kh = AAL("Kh", [128, S], BF16)
    Eb = AAL("Eb", [128, NT, QCH], BF16)
    attnSt = AAL("attnSt", [64, 2, QCH], BF16)
    attnTs = AAL("attnTs", [128, 2, 8, 128], BF16)
    r1stA = AAL("r1stA", [128, 2, D], BF16)
    den_s = AAL("den_s", [16, 2, 128], F32)
    denb = AAL("denb", [128, 4, 128], BF16)

    HDI = DI // 2
    wraw2 = AAL("wraw2", [128, 2, HDI], F32)
    qw_gm = AAL("qw_gm", [128, 128], F32)
    qw_s = AAL("qw_s", [128, 128], F32)
    qw_s2 = AAL("qw_s2", [128, 128], F32)
    qw_r = AAL("qw_r", [128, 128], F32)
    qw_s16 = AAL("qw_s16", [128, 128], FP16)
    qw_an = AAL("qw_an", [128, HDI], FP16)
    qw_acc = AAL("qw_acc", [128, HDI], FP16)
    qw_cA = AAL("qw_cA", [128, HDI], FP16)
    qw_sg = AAL("qw_sg", [128, HDI], FP16)
    qw_out = AAL("qw_out", [128, HDI], BF16)
    qfh_an = AAL("qfh_an", [128, HDI], F32)
    qfh_acc = AAL("qfh_acc", [128, HDI], F32)
    qfh_cA = AAL("qfh_cA", [128, HDI], F32)
    dstTs = AAL("dstTs", [128, 11, 128], BF16)

    def q_chain(v, an_ap, acc_ap, cA_ap, out_ap, n):
        v.tensor_scalar(acc_ap, an_ap, 5.0, 2.0, OP.is_gt, OP.mult)
        v.scalar_tensor_tensor(acc_ap, an_ap, 2.5, acc_ap, OP.is_gt, OP.add)
        v.scalar_tensor_tensor(acc_ap, an_ap, 3.5, acc_ap, OP.is_gt, OP.add)
        v.tensor_scalar(cA_ap, an_ap, 0.25, None, OP.is_gt)
        v.scalar_tensor_tensor(cA_ap, an_ap, 0.75, cA_ap, OP.is_gt, OP.add)
        v.scalar_tensor_tensor(cA_ap, an_ap, 1.25, cA_ap, OP.is_gt, OP.add)
        v.scalar_tensor_tensor(cA_ap, an_ap, 1.75, cA_ap, OP.is_gt, OP.add)
        v.scalar_tensor_tensor(out_ap, cA_ap, 0.5, acc_ap, OP.mult, OP.add)

    wq_tp_guard = [[]]
    qw_load_guard = [0, 0]
    qw_sbuf_guard = [0]

    def make_wq_unit(src_dram, row0, col0, n_cols, chain, dst_dr, c0, nchunks, wt_col):
        def emit(u):
            p = u % 2

            @blk.gpsimd
            def _(g, p=p, vq=qw_load_guard[p]):
                if vq:
                    g.wait_ge(v_d.h, vq)
                s_w[p].dma(g, out=wraw2[:, p, 0:n_cols],
                           in_=src_dram[row0:row0 + 128, col0:col0 + n_cols])
            w_in = s_w[p].count

            abuf = qw_cA if chain == "fp16" else qfh_an

            @blk.scalar
            def _(s, p=p, w_in=w_in, vq=qw_sbuf_guard[0], abuf=abuf):
                s_w[p].wait(s, w_in)
                if vq:
                    s.wait_ge(v_d.h, vq)
                s.activation(abuf[:, 0:n_cols], wraw2[:, p, 0:n_cols], AF.Abs,
                             scale=6.0)
                s.activation(qw_sg[:, 0:n_cols], wraw2[:, p, 0:n_cols],
                             AF.Sign).then_inc(a_d.h, 1)
            a_d.count += 1

            @blk.vector
            def _(v, p=p, aa=a_d.count, guard=list(wq_tp_guard[0]), nb=n_cols // 16):
                v.wait_ge(a_d.h, aa)
                for c, val in guard:
                    v.wait_ge(s_tp[c].h, val)
                v.tensor_reduce(qw_gm[:, 0:nb],
                                wraw2[:, p, 0:n_cols].rearrange("p (c k) -> p c k", k=16),
                                AX.X, OP.max, apply_absolute_value=True)
                v.tensor_scalar(qw_s[:, 0:nb], qw_gm[:, 0:nb], 6e-12, None, OP.max)
                v.tensor_scalar(qw_s2[:, 0:nb], qw_gm[:, 0:nb], 0.16666666666666666,
                                1e-12, OP.mult, OP.max)
                v.drain()
                v.reciprocal(qw_r[:, 0:nb], qw_s[:, 0:nb])
                v.tensor_copy(qw_s16[:, 0:nb], qw_s2[:, 0:nb])
                v.drain()
                if chain == "fp16":
                    v.tensor_tensor(qw_an[:, 0:n_cols].rearrange("p (c k) -> p c k", k=16),
                                    qw_cA[:, 0:n_cols].rearrange("p (c k) -> p c k", k=16),
                                    qw_r[:, 0:nb].unsqueeze(2).broadcast_to([128, nb, 16]),
                                    OP.mult)
                    q_chain(v, qw_an[:, 0:n_cols], qw_acc[:, 0:n_cols],
                            qw_cA[:, 0:n_cols], qw_acc[:, 0:n_cols], n_cols)
                    v.tensor_tensor(qw_an[:, 0:n_cols], qw_acc[:, 0:n_cols],
                                    qw_sg[:, 0:n_cols], OP.mult)
                    v.tensor_tensor(qw_out[:, 0:n_cols].rearrange("p (c k) -> p c k", k=16),
                                    qw_an[:, 0:n_cols].rearrange("p (c k) -> p c k", k=16),
                                    qw_s16[:, 0:nb].unsqueeze(2).broadcast_to([128, nb, 16]),
                                    OP.mult)
                else:
                    v.tensor_tensor(qfh_an[:, 0:n_cols].rearrange("p (c k) -> p c k", k=16),
                                    qfh_an[:, 0:n_cols].rearrange("p (c k) -> p c k", k=16),
                                    qw_r[:, 0:nb].unsqueeze(2).broadcast_to([128, nb, 16]),
                                    OP.mult)
                    q_chain(v, qfh_an[:, 0:n_cols], qfh_acc[:, 0:n_cols],
                            qfh_cA[:, 0:n_cols], qfh_an[:, 0:n_cols], n_cols)
                    v.tensor_tensor(qfh_acc[:, 0:n_cols], qfh_an[:, 0:n_cols],
                                    qw_sg[:, 0:n_cols], OP.mult)
                    v.tensor_tensor(qw_out[:, 0:n_cols].rearrange("p (c k) -> p c k", k=16),
                                    qfh_acc[:, 0:n_cols].rearrange("p (c k) -> p c k", k=16),
                                    qw_s16[:, 0:nb].unsqueeze(2).broadcast_to([128, nb, 16]),
                                    OP.mult)
                v.drain()
                v.engine_nop().then_inc(v_d.h, 1)
            v_d.count += 1
            qw_load_guard[p] = v_d.count
            qw_sbuf_guard[0] = v_d.count

            @blk.sync
            def _(sync, vv=v_d.count, dw=s_dw.count):
                sync.wait_ge(v_d.h, vv)
                sync.wait_ge(s_dw.h, dw)
                for ci in range(nchunks):
                    s_tp[ci].dma(sync, out=dstTs[:, ci, :],
                                 in_=qw_out[:, 128 * ci:128 * (ci + 1)], transpose=True)
            tpc = [(ci, s_tp[ci].count) for ci in range(nchunks)]
            wq_tp_guard[0] = tpc

            @blk.gpsimd
            def _(g, tpc=list(tpc)):
                for ci, val in tpc:
                    g.wait_ge(s_tp[ci].h, val)
                s_dw.dma(g, out=dst_dr[:, c0:c0 + nchunks,
                                       128 * wt_col:128 * (wt_col + 1)],
                         in_=dstTs[:, 0:nchunks, :])
        return emit

    wq_units = []
    for wt in range(22):
        wq_units.append(make_wq_unit(wdram["w_gate"], 128 * wt, 0, D, "fp16",
                                     wqg_dr, 0, 8, wt))
    for wt in range(22):
        wq_units.append(make_wq_unit(wdram["w_up"], 128 * wt, 0, D, "fp16",
                                     wqu_dr, 0, 8, wt))
    for wt in range(8):
        for half in range(2):
            wq_units.append(make_wq_unit(wdram["w_down"], 128 * wt, HDI * half,
                                         HDI, "fp32", wqd_dr, 11 * half, 11, wt))
    wq_done_marks = {}
    wq_unit_idx = [0]

    def pull_wq_unit():
        i = wq_unit_idx[0]
        if i < len(wq_units):
            wq_units[i](i)
            wq_unit_idx[0] = i + 1
            if i + 1 == 22:
                wq_done_marks["gate"] = s_dw.count
            elif i + 1 == 44:
                wq_done_marks["up"] = s_dw.count
            elif i + 1 == 60:
                wq_done_marks["down"] = s_dw.count

    t_after_V = t_d.count

    @blk.gpsimd
    def _(g):
        g.wait_ge(v_d.h, qr_done_v)
        g.wait_ge(t_d.h, t_after_V)
        s_kh.dma(g, out=Kh[64:96, :], in_=KrT[:])

    ats_prev2 = [s_ats.count, s_ats.count]
    den_prev2 = [s_den.count, s_den.count]
    av_tail_a = [0, 0]
    av_tail_v = [0, 0]
    for h in range(NH):
        pipe = MMPipe()
        for (wten, dst, prow) in ((wb_UQ, Qh, 0), (wb_UK, Kh, 64)):
            for sc in range(4):
                ps = psC if (sc % 2 == 0) else psD
                def mm(t, wten=wten, sc=sc, ps=ps, prow=prow, h=h):
                    src = cqT if wten is wb_UQ else ckvT
                    for l in range(2):
                        i_ = t.matmul(ps[prow:prow + 64, :], wten[:, l, 64 * h:64 * (h + 1)],
                                      src[:, l, 512 * sc:512 * (sc + 1)],
                                      start=(l == 0), stop=(l == 1))
                    return i_
                def ev(s, wten=wten, dst=dst, sc=sc, ps=ps, prow=prow):
                    return s.activation(dst[0:64, 512 * sc:512 * (sc + 1)],
                                        ps[prow:prow + 64, :], AF.Copy)
                pipe.add(mm, ev)

        def pre(t, h=h):
            if h == 0:
                t.wait_ge(v_d.h, qr_done_v)
                s_kh.wait(t)
        pipe.emit(pre_tensor_wait=pre)

        @blk.gpsimd
        def _(g, h=h):
            a_d.wait(g)
            qtile, qoff = divmod(32 * h, 128)
            s_qh[h % 2].dma(g, out=Qh[64:96, :], in_=QrT[qoff:qoff + 32, qtile, :])
        qh_in = s_qh[h % 2].count

        a_qk = a_d.count

        for qi in range(NQ):
            nkt = 4 * qi + 4
            pa = (4 * h + qi) % 2
            psAVc = psAV2[pa]
            t0 = t_d.count
            a0 = a_d.count
            vv0 = v_d.count

            @blk.tensor
            def _(t, h=h, qi=qi, nkt=nkt, a0=a0, vv0=vv0, a_qk=a_qk, qh_in=qh_in,
                  psAVc=psAVc, ta=av_tail_a[pa], tv=av_tail_v[pa]):
                if qi == 0:
                    t.wait_ge(a_d.h, a_qk)
                    s_qh[h % 2].wait(t, qh_in)
                if ta:
                    t.wait_ge(a_d.h, ta)
                    t.wait_ge(v_d.h, tv)
                for kt in range(nkt):
                    ps = psS0 if (kt % 2 == 0) else psS1
                    if kt >= 2:
                        t.wait_ge(a_d.h, a0 + kt - 1)
                    t.matmul(ps[0:128, :], Kh[0:96, 128 * kt:128 * (kt + 1)],
                             Qh[0:96, QCH * qi:QCH * (qi + 1)],
                             start=True, stop=True).then_inc(t_d.h, 1)
                    if kt >= 1:
                        t.wait_ge(v_d.h, vv0 + kt)
                        t.matmul(psAVc[:, :], Vb[:, kt - 1, 65 * h:65 * (h + 1)],
                                 Eb[:, kt - 1, :], start=(kt - 1 == 0), stop=False)
                t.wait_ge(v_d.h, vv0 + nkt)
                t.matmul(psAVc[:, :], Vb[:, nkt - 1, 65 * h:65 * (h + 1)],
                         Eb[:, nkt - 1, :], start=False, stop=True).then_inc(t_d.h, 1)
            t_d.count += nkt + 1
            t_av_done = t_d.count

            @blk.scalar
            def _(s, nkt=nkt, t0=t0):
                for kt in range(nkt):
                    ps = psS0 if (kt % 2 == 0) else psS1
                    s.wait_ge(t_d.h, t0 + kt + 1)
                    s.activation(Eb[:, kt, :], ps[0:128, :],
                                 AF.Exp).then_inc(a_d.h, 1)
            a_d.count += nkt

            @blk.vector
            def _(v, qi=qi, nkt=nkt, a0=a0):
                for kt in range(nkt):
                    v.wait_ge(a_d.h, a0 + kt + 1)
                    v.tensor_scalar(Eb[:, kt, :], Eb[:, kt, :], E_LO, E_HI, OP.max, OP.min)
                    if kt >= 4 * qi:
                        v.tensor_tensor(Eb[:, kt, :], Eb[:, kt, :],
                                        mask_s[:, kt - 4 * qi, :], OP.mult)
                    v.engine_nop().then_inc(v_d.h, 1)
            v_d.count += nkt

            @blk.vector
            def _(v, pa=pa, t_av_done=t_av_done, psAVc=psAVc, dp=den_prev2[pa]):
                v.wait_ge(t_d.h, t_av_done)
                v.wait_ge(s_den.h, dp)
                v.reciprocal(rowr[:, pa, :], psAVc[64:65, :])
                v.drain()
                v.engine_nop().then_inc(v_d.h, 1)
            v_d.count += 1
            av_tail_v[pa] = v_d.count

            @blk.scalar
            def _(s, pa=pa, t_av_done=t_av_done, psAVc=psAVc, atp=ats_prev2[pa]):
                s.wait_ge(t_d.h, t_av_done)
                s.wait_ge(s_ats.h, atp)
                s.activation(attnSt[:, pa, :], psAVc[0:64, :],
                             AF.Copy).then_inc(a_d.h, 1)
            a_d.count += 1
            av_tail_a[pa] = a_d.count

            @blk.gpsimd
            def _(g, h=h, qi=qi, pa=pa, ea=av_tail_a[pa], ev=av_tail_v[pa]):
                g.wait_ge(a_d.h, ea)
                g.wait_ge(v_d.h, ev)
                s_ats.dma(g, out=at_dr[64 * (h % 2):64 * (h % 2) + 64, h // 2,
                                       QCH * qi:QCH * (qi + 1)],
                          in_=attnSt[:, pa, :])
                s_den.dma(g, out=den_dr[h:h + 1, QCH * qi:QCH * (qi + 1)],
                          in_=rowr[:, pa, :])
            ats_prev2[pa] = s_ats.count
            den_prev2[pa] = s_den.count
            pull_wq_unit()

    while wq_unit_idx[0] < len(wq_units):
        pull_wq_unit()

    attn_done_v = v_d.count

    a4_t_prev = [0, 0]
    a4_res_prev = [0, 0]
    a4_tt_prev = 0
    for st in range(NT):
        b4 = st % 2

        @blk.sync
        def _(sync, st=st, b4=b4, tfree=a4_t_prev[b4], rfree=a4_res_prev[b4]):
            if st == 0:
                for i in range(2):
                    s_r0s[i].wait(sync)
                s_ats.wait(sync)
                s_den.wait(sync)
            if rfree:
                sync.wait_ge(v_d.h, rfree)
            sync.wait_ge(t_d.h, tfree)
            s_a4r[b4].dma(sync, out=r0st[:, b4, :],
                          in_=r0_dr[128 * st:128 * (st + 1), :])
            s_a4a[b4].dma(sync, out=attnTs[:, b4, :, :],
                          in_=at_dr[:, :, 128 * st:128 * (st + 1)])
            s_a4a[b4].dma(sync, out=den_s[:, b4, :],
                          in_=den_dr[:, 128 * st:128 * (st + 1)])
        r0_in = s_a4r[b4].count
        at_in = s_a4a[b4].count

        td0 = t_d.count
        vprev = v_d.count

        v0 = v_d.count

        @blk.tensor
        def _(t, st=st, b4=b4, at_in=at_in, ttp=a4_tt_prev, v0=v0):
            if st == 0:
                t.wait_ge(v_d.h, attn_done_v)
            s_a4a[b4].wait(t, at_in)
            if ttp:
                t.wait_ge(v_d.h, ttp)
            for c in range(8):
                ps = psS0 if c % 2 == 0 else psS1
                sl = (c % 4) // 2
                if c >= 2:
                    t.wait_ge(v_d.h, v0 + c - 1)
                t.matmul(ps[:, 128 * sl:128 * (sl + 1)], sel_s[:, c, :],
                         den_s[:, b4, :], start=True, stop=True).then_inc(t_d.h, 1)
        t_d.count += 8

        @blk.vector
        def _(v, st=st, b4=b4, td0=td0):
            for c in range(8):
                ps = psS0 if c % 2 == 0 else psS1
                sl = (c % 4) // 2
                v.wait_ge(t_d.h, td0 + c + 1)
                v.tensor_copy(denb[:, c % 4, :], ps[:, 128 * sl:128 * (sl + 1)])
                v.tensor_tensor(attnTs[:, b4, c, :], attnTs[:, b4, c, :],
                                denb[:, c % 4, :], OP.mult)
                v.engine_nop().then_inc(v_d.h, 1)
        v_d.count += 8
        a4_tt_prev = v_d.count

        t0 = t_d.count

        @blk.tensor
        def _(t, st=st, b4=b4, vprev=vprev, tte=a4_tt_prev):
            t.wait_ge(v_d.h, tte)
            for half in range(2):
                ps = psC if half == 0 else psB
                for c in range(8):
                    i_ = t.matmul(ps[0:128, :], attnTs[:, b4, c, :],
                                  wb_WO[:, c, 512 * half:512 * (half + 1)],
                                  start=(c == 0), stop=(c == 7))
                i_.then_inc(t_d.h, 1)
        t_d.count += 2
        a4_t_prev[b4] = t_d.count

        @blk.vector
        def _(v, st=st, b4=b4, t0=t0, r0_in=r0_in, r1f=s_r1s[b4].count):
            s_a4r[b4].wait(v, r0_in)
            v.wait_ge(s_r1s[b4].h, r1f)
            for half in range(2):
                ps = psC if half == 0 else psB
                v.wait_ge(t_d.h, t0 + half + 1)
                v.tensor_tensor(r1stA[:, b4, 512 * half:512 * (half + 1)], ps[0:128, :],
                                r0st[:, b4, 512 * half:512 * (half + 1)], OP.add)
                v.drain()
                v.engine_nop().then_inc(v_d.h, 1)
        v_d.count += 2
        a4_res_prev[b4] = v_d.count

        @blk.gpsimd
        def _(g, st=st, b4=b4):
            v_d.wait(g)
            s_r1s[b4].dma(g, out=r1_dr[128 * st:128 * (st + 1), :],
                          in_=r1stA[:, b4, :])

    if dump:
        dmp = {}
        for nm, ten in (("Qh", Qh), ("Kh", Kh), ("Eb", Eb), ("Vb", Vb),
                        ("QrT", QrT), ("KrT", KrT), ("cqT", cqT), ("ckvT", ckvT)):
            shp = [int(d) for d in ten.shape]
            dmp[nm] = nc.dram_tensor("dump_" + nm, shp, BF16, kind="ExternalOutput")

        @blk.sync
        def _(sync):
            v_d.wait(sync)
            a_d.wait(sync)
            for nm, ten in (("Qh", Qh), ("Kh", Kh), ("Eb", Eb), ("Vb", Vb),
                            ("QrT", QrT), ("KrT", KrT), ("cqT", cqT), ("ckvT", ckvT)):
                s_out.dma(sync, out=dmp[nm][:], in_=ten[:])
            s_out.wait(sync)

    actxL.close()

    bctxL = ExitStack()
    def BL(name, shape, dtype):
        return bctxL.enter_context(nc.sbuf_tensor(name, shape, dtype, side="left"))

    wqdT = BL("wqdT", [128, 22, D], BF16)
    q_gm = BL("q_gm", [128, NB_I], F32)
    q_s = BL("q_s", [128, NB_I], F32)
    q_r = BL("q_r", [128, NB_I], F32)
    q_s2 = BL("q_s2", [128, NB_I], F32)
    q_s16 = BL("q_s16", [128, NB_I], FP16)
    q_an = BL("q_an", [128, HDI], FP16)
    q_acc = BL("q_acc", [128, HDI], FP16)
    q_cA = BL("q_cA", [128, DI], FP16)
    q_sg = BL("q_sg", [128, DI], FP16)
    q_out = BL("q_out", [128, DI], BF16)
    wqgT = BL("wqgT", [128, 8, DI], BF16)
    wquT = BL("wquT", [128, 8, DI], BF16)

    a4_end_t = t_d.count
    a4_end_v = v_d.count

    @blk.sync
    def _(sync):
        sync.wait_ge(t_d.h, a4_end_t)
        sync.wait_ge(v_d.h, a4_end_v)
        for i in range(2):
            s_r1s[i].wait(sync)
        sync.wait_ge(s_dw.h, wq_done_marks["gate"])
        for q in range(4):
            s_wq.dma(sync, out=wqgT[:, 2 * q:2 * (q + 1), :],
                     in_=wqg_dr[:, 2 * q:2 * (q + 1), :])
        sync.wait_ge(s_dw.h, wq_done_marks["up"])
        for q in range(4):
            s_wq.dma(sync, out=wquT[:, 2 * q:2 * (q + 1), :],
                     in_=wqu_dr[:, 2 * q:2 * (q + 1), :])
    wq_gate_ld = s_wq.count - 64
    wq_up_ld = s_wq.count

    @blk.gpsimd
    def _(g):
        g.wait_ge(t_d.h, a4_end_t)
        g.wait_ge(v_d.h, a4_end_v)
        for i in range(2):
            s_r1s[i].wait(g)
        g.wait_ge(s_dw.h, wq_done_marks["down"])
        for q in range(4):
            s_uv.dma(g, out=wqdT[:, 6 * q:min(6 * (q + 1), 22), :],
                     in_=wqd_dr[:, 6 * q:min(6 * (q + 1), 22), :])
    wq_down_ld = s_uv.count

    xqT = BL("xqT", [128, 2, 8, 128], BF16)
    hqT = BL("hqT", [128, 2, 22, 128], BF16)
    gsil = BL("gsil", [128, 2, DI], BF16)
    r1st = BL("r1st", [128, 2, D], BF16)
    q2_cA = BL("q2_cA", [128, D], FP16)
    q2_sg = BL("q2_sg", [128, D], FP16)
    outst = xst
    xq_out = r0st

    if dump:
        dbgB = {}
        for nm, cols in (("ln2", D), ("xq", D), ("h", DI), ("hq", DI)):
            dbgB[nm] = nc.dram_tensor("dbgB_" + nm, [NT, 128, cols], BF16,
                                      kind="ExternalOutput")

    g_chunks = [(i * 512, min(512, DI - i * 512)) for i in range((DI + 511) // 512)]
    NG = len(g_chunks)
    out_v_prev = 0
    r1_free = [0, 0]
    xq_v_prev = 0
    hq_v_prev_any = 0
    hq_v_prev2 = [0, 0]
    down_t_prev2 = [0, 0]
    gateup_t_prev2 = [0, 0]
    xqT_tp_prev2 = [[], []]
    silu_a_prev = 0
    hTT_v_prev = 0
    lnst_guard = 0
    out_store_prev2 = [0, 0]
    hq_tp_prev = [(c, s_tp[c].count) for c in range(22)]

    def emit_act_quant(v, src_ap, cA, sg, n, nb, out_ap):
        v.tensor_reduce(q_gm[:, 0:nb], src_ap.rearrange("p (c k) -> p c k", k=16),
                        AX.X, OP.max, apply_absolute_value=True)
        v.tensor_scalar(q_s[:, 0:nb], q_gm[:, 0:nb], 6e-12, None, OP.max)
        v.tensor_scalar(q_s2[:, 0:nb], q_gm[:, 0:nb], 0.16666666666666666, 1e-12,
                        OP.mult, OP.max)
        v.drain()
        v.reciprocal(q_r[:, 0:nb], q_s[:, 0:nb])
        v.tensor_copy(q_s16[:, 0:nb], q_s2[:, 0:nb])
        v.drain()
        for h0 in range(0, n, HDI):
            m = min(HDI, n - h0)
            mb = m // 16
            b0 = h0 // 16
            v.tensor_tensor(q_an[:, 0:m].rearrange("p (c k) -> p c k", k=16),
                            cA[:, h0:h0 + m].rearrange("p (c k) -> p c k", k=16),
                            q_r[:, b0:b0 + mb].unsqueeze(2).broadcast_to([128, mb, 16]),
                            OP.mult)
            q_chain(v, q_an[:, 0:m], q_acc[:, 0:m], cA[:, h0:h0 + m], q_acc[:, 0:m], m)
            v.tensor_tensor(q_an[:, 0:m], q_acc[:, 0:m], sg[:, h0:h0 + m], OP.mult)
            v.tensor_tensor(out_ap[:, h0:h0 + m].rearrange("p (c k) -> p c k", k=16),
                            q_an[:, 0:m].rearrange("p (c k) -> p c k", k=16),
                            q_s16[:, b0:b0 + mb].unsqueeze(2).broadcast_to([128, mb, 16]),
                            OP.mult)
        v.drain()

    for st in range(NT):
        b = st % 2
        sb = 8 * b

        @blk.sync
        def _(sync, st=st, b=b, r1f=r1_free[b]):
            if st == 0:
                for i in range(2):
                    s_r1s[i].wait(sync)
                s_dw.wait(sync)
            if r1f:
                sync.wait_ge(v_d.h, r1f)
            s_r1[b].dma(sync, out=r1st[:, b, :], in_=r1_dr[128 * st:128 * (st + 1), :])
        r1_in = s_r1[b].count

        @blk.vector
        def _(v, st=st, b=b, sb=sb, r1_in=r1_in):
            s_r1[b].wait(v, r1_in)
            v.reduce_sum(stat[:, sb:sb + 1], r1st[:, b, :], AX.X)
            v.tensor_tensor(sqsc[:], r1st[:, b, 0:512], r1st[:, b, 0:512], OP.mult)
            v.reduce_sum(stat[:, sb + 1:sb + 2], sqsc[:], AX.X)
            v.tensor_tensor(sqsc[:], r1st[:, b, 512:1024], r1st[:, b, 512:1024], OP.mult)
            v.reduce_sum(stat[:, sb + 5:sb + 6], sqsc[:], AX.X)
            v.drain()
            v.tensor_tensor(stat[:, sb + 1:sb + 2], stat[:, sb + 1:sb + 2],
                            stat[:, sb + 5:sb + 6], OP.add)
            v.tensor_scalar(stat[:, sb:sb + 1], stat[:, sb:sb + 1], 1.0 / D, None, OP.mult)
            v.tensor_scalar(stat[:, sb + 1:sb + 2], stat[:, sb + 1:sb + 2], 1.0 / D,
                            None, OP.mult)
            v.scalar_tensor_tensor(stat[:, sb + 2:sb + 3], stat[:, sb:sb + 1], 1.0,
                                   stat[:, sb:sb + 1], OP.mult, OP.mult)
            v.tensor_tensor(stat[:, sb + 1:sb + 2], stat[:, sb + 1:sb + 2],
                            stat[:, sb + 2:sb + 3], OP.subtract)
            v.tensor_scalar(stat[:, sb + 1:sb + 2], stat[:, sb + 1:sb + 2], LN_EPS,
                            None, OP.add)
            v.engine_nop().then_inc(v_d.h, 1)
        v_d.count += 1
        stats_v = v_d.count

        @blk.scalar
        def _(s, st=st, sb=sb, sv=stats_v):
            s.wait_ge(v_d.h, sv)
            s.activation(stat[:, sb + 3:sb + 4], stat[:, sb + 1:sb + 2],
                         AF.Sqrt).then_inc(a_d.h, 1)
        a_d.count += 1
        sqrt_a = a_d.count

        @blk.vector
        def _(v, st=st, b=b, sb=sb, sa=sqrt_a, lg=lnst_guard):
            v.wait_ge(a_d.h, sa)
            if lg:
                v.wait_ge(a_d.h, lg)
            v.reciprocal(stat[:, sb + 4:sb + 5], stat[:, sb + 3:sb + 4])
            v.tensor_scalar(lnst[:], r1st[:, b, :], stat[:, sb:sb + 1],
                            stat[:, sb + 4:sb + 5], OP.subtract, OP.mult)
            v.engine_nop().then_inc(v_d.h, 1)
        v_d.count += 1
        apply_v = v_d.count

        @blk.scalar
        def _(s, st=st, b=b, av=apply_v, xqp=xq_v_prev):
            s.wait_ge(v_d.h, av)
            if xqp:
                s.wait_ge(v_d.h, xqp)
            s.activation(q2_cA[:], lnst[:], AF.Abs, scale=6.0)
            s.activation(q2_sg[:], lnst[:], AF.Sign).then_inc(a_d.h, 1)
        a_d.count += 1
        xq_abs_a = a_d.count
        lnst_guard = xq_abs_a

        @blk.vector
        def _(v, st=st, b=b, aa=xq_abs_a, guard=list(xqT_tp_prev2[b])):
            v.wait_ge(a_d.h, aa)
            for c, val in guard:
                v.wait_ge(s_tp[c].h, val)
            emit_act_quant(v, lnst[:], q2_cA, q2_sg, D, NB_D, xq_out[:, b, 0:D])
            v.engine_nop().then_inc(v_d.h, 1)
        v_d.count += 1
        xq_v = v_d.count
        xq_v_prev = xq_v

        if dump:
            @blk.gpsimd
            def _(g, st=st, b=b, xq_v=xq_v):
                g.wait_ge(v_d.h, xq_v)
                s_dbg.dma(g, out=dbgB["ln2"][st, :, :], in_=lnst[:])
                s_dbg.dma(g, out=dbgB["xq"][st, :, :], in_=xq_out[:, b, 0:D])

        @blk.scalar
        def _(s, st=st, b=b, xq_v=xq_v, tg=gateup_t_prev2[b]):
            s.wait_ge(v_d.h, xq_v)
            if tg:
                s.wait_ge(t_d.h, tg)
            for c in range(8):
                s_tp[c].dma(s, out=xqT[:, b, c, :],
                            in_=xq_out[:, b, 128 * c:128 * (c + 1)], transpose=True)
        xq_tp = [(c, s_tp[c].count) for c in range(8)]
        xqT_tp_prev2[b] = xq_tp

        t0 = t_d.count
        a0 = a_d.count
        V0 = v_d.count

        @blk.tensor
        def _(t, st=st, b=b, a0=a0, V0=V0, xq_tp=list(xq_tp), sprev=silu_a_prev,
              hprev=hTT_v_prev):
            for c, val in xq_tp:
                t.wait_ge(s_tp[c].h, val)
            if st == 0:
                s_wq.wait(t, wq_up_ld)
            if sprev:
                t.wait_ge(a_d.h, sprev)
            for gi, (off, w) in enumerate(g_chunks):
                ps = psA if gi % 2 == 0 else psB
                if gi >= 2:
                    t.wait_ge(a_d.h, a0 + gi - 1)
                for kc in range(8):
                    i_ = t.matmul(ps[0:128, 0:w], xqT[:, b, kc, :], wqgT[:, kc, off:off + w],
                                  start=(kc == 0), stop=(kc == 7))
                i_.then_inc(t_d.h, 1)
            if hprev:
                t.wait_ge(v_d.h, hprev)
            for gi, (off, w) in enumerate(g_chunks):
                ps = psC if gi % 2 == 0 else psD
                if gi >= 2:
                    t.wait_ge(v_d.h, V0 + gi - 1)
                for kc in range(8):
                    i_ = t.matmul(ps[0:128, 0:w], xqT[:, b, kc, :], wquT[:, kc, off:off + w],
                                  start=(kc == 0), stop=(kc == 7))
                i_.then_inc(t_d.h, 1)
        t_d.count += 2 * NG
        gateup_t_prev2[b] = t_d.count

        @blk.scalar
        def _(s, st=st, b=b, t0=t0, hqp=hq_v_prev2[b]):
            if hqp:
                s.wait_ge(v_d.h, hqp)
            for gi, (off, w) in enumerate(g_chunks):
                ps = psA if gi % 2 == 0 else psB
                s.wait_ge(t_d.h, t0 + gi + 1)
                s.activation(gsil[:, b, off:off + w], ps[0:128, 0:w],
                             AF.Silu).then_inc(a_d.h, 1)
        a_d.count += NG
        silu_a_prev = a_d.count

        @blk.vector
        def _(v, st=st, b=b, t0=t0, a0=a0):
            for gi, (off, w) in enumerate(g_chunks):
                ps = psC if gi % 2 == 0 else psD
                v.wait_ge(t_d.h, t0 + NG + gi + 1)
                v.wait_ge(a_d.h, a0 + gi + 1)
                v.tensor_tensor(gsil[:, b, off:off + w], ps[0:128, 0:w],
                                gsil[:, b, off:off + w], OP.mult)
                v.engine_nop().then_inc(v_d.h, 1)
        v_d.count += NG
        hTT_v_prev = v_d.count
        h_v = v_d.count

        @blk.scalar
        def _(s, st=st, b=b, h_v=h_v, hqp=hq_v_prev_any):
            s.wait_ge(v_d.h, h_v)
            if hqp:
                s.wait_ge(v_d.h, hqp)
            s.activation(q_cA[:, 0:DI], gsil[:, b, :], AF.Abs, scale=6.0)
            s.activation(q_sg[:, 0:DI], gsil[:, b, :], AF.Sign).then_inc(a_d.h, 1)
        a_d.count += 1
        hq_abs_a = a_d.count

        @blk.vector
        def _(v, st=st, b=b, aa=hq_abs_a, guard=list(hq_tp_prev)):
            v.wait_ge(a_d.h, aa)
            for c, val in guard:
                v.wait_ge(s_tp[c].h, val)
            emit_act_quant(v, gsil[:, b, :], q_cA, q_sg, DI, NB_I, q_out[:, 0:DI])
            v.engine_nop().then_inc(v_d.h, 1)
        v_d.count += 1
        hq_v = v_d.count
        hq_v_prev_any = hq_v
        hq_v_prev2[b] = hq_v

        if dump:
            @blk.gpsimd
            def _(g, st=st, b=b, hq_v=hq_v):
                g.wait_ge(v_d.h, hq_v)
                s_dbg.dma(g, out=dbgB["h"][st, :, :], in_=gsil[:, b, :])
                s_dbg.dma(g, out=dbgB["hq"][st, :, :], in_=q_out[:, 0:DI])

        @blk.scalar
        def _(s, st=st, b=b, hq_v=hq_v, dtp=down_t_prev2[b]):
            s.wait_ge(v_d.h, hq_v)
            if dtp:
                s.wait_ge(t_d.h, dtp)
            for c in range(22):
                s_tp[c].dma(s, out=hqT[:, b, c, :], in_=q_out[:, 128 * c:128 * (c + 1)],
                            transpose=True)
        hq_tp = [(c, s_tp[c].count) for c in range(22)]
        hq_tp_prev = hq_tp

        t1 = t_d.count

        @blk.tensor
        def _(t, st=st, b=b, hq_tp=list(hq_tp), ovp=out_v_prev):
            for c, val in hq_tp:
                t.wait_ge(s_tp[c].h, val)
            if st == 0:
                s_uv.wait(t, wq_down_ld)
            if ovp:
                t.wait_ge(v_d.h, ovp)
            for half in range(2):
                ps = psS0 if half == 0 else psS1
                for kc in range(22):
                    i_ = t.matmul(ps[0:128, :], hqT[:, b, kc, :],
                                  wqdT[:, kc, 512 * half:512 * (half + 1)],
                                  start=(kc == 0), stop=(kc == 21))
                i_.then_inc(t_d.h, 1)
        t_d.count += 2
        down_t_prev2[b] = t_d.count

        @blk.vector
        def _(v, st=st, b=b, t1=t1, stp=out_store_prev2[b]):
            if stp:
                v.wait_ge(s_out.h, stp)
            for half in range(2):
                ps = psS0 if half == 0 else psS1
                v.wait_ge(t_d.h, t1 + half + 1)
                v.tensor_tensor(outst[:, b, 512 * half:512 * (half + 1)], ps[0:128, :],
                                r1st[:, b, 512 * half:512 * (half + 1)], OP.add)
            v.drain()
            v.engine_nop().then_inc(v_d.h, 1)
        v_d.count += 1
        out_v = v_d.count
        out_v_prev = out_v
        r1_free[b] = out_v

        @blk.sync
        def _(sync, st=st, b=b, out_v=out_v):
            sync.wait_ge(v_d.h, out_v)
            s_out.dma(sync, out=out_d[128 * st:128 * (st + 1), :], in_=outst[:, b, :])
        out_store_prev2[b] = s_out.count

    if dump:
        dmp2 = {}
        for nm, ten in (("wqgT", wqgT), ("wquT", wquT), ("wqdT", wqdT),
                        ("qout", q_out)):
            shp = [int(d) for d in ten.shape]
            dmp2[nm] = nc.dram_tensor("dumpB_" + nm, shp, BF16, kind="ExternalOutput")

        @blk.sync
        def _(sync):
            s_out.wait(sync)
            for nm, ten in (("wqgT", wqgT), ("wquT", wquT), ("wqdT", wqdT),
                            ("qout", q_out)):
                s_out.dma(sync, out=dmp2[nm][:], in_=ten[:])
            s_out.wait(sync)

    @blk.sync
    def _(sync):
        s_out.wait(sync)
        if dump:
            s_dbg.wait(sync)

    bctxL.close()
    ctx.close()
    return nc


_NC_CACHE = None


def kernel(L0_attn_x, L0_attn_W_DQ, L0_attn_W_UQ, L0_attn_W_QR, L0_attn_W_DKV,
           L0_attn_W_UK, L0_attn_W_UV, L0_attn_W_KR, L0_attn_W_O,
           L0_ffn_w_gate, L0_ffn_w_up, L0_ffn_w_down):
    global _NC_CACHE
    if _NC_CACHE is None:
        _NC_CACHE = build_nc()
    nc = _NC_CACHE

    cs, sn = _rope_tables()
    masks = _causal_masks()
    x = np.ascontiguousarray(np.asarray(L0_attn_x, dtype=np.float32))
    common = {
        "W_DQ": np.ascontiguousarray(np.asarray(L0_attn_W_DQ, np.float32)),
        "W_UQ": np.ascontiguousarray(np.asarray(L0_attn_W_UQ, np.float32)),
        "W_QR": np.ascontiguousarray(np.asarray(L0_attn_W_QR, np.float32)),
        "W_DKV": np.ascontiguousarray(np.asarray(L0_attn_W_DKV, np.float32)),
        "W_UK": np.ascontiguousarray(np.asarray(L0_attn_W_UK, np.float32)),
        "W_UV": np.ascontiguousarray(np.asarray(L0_attn_W_UV, np.float32)),
        "W_KR": np.ascontiguousarray(np.asarray(L0_attn_W_KR, np.float32)),
        "W_O": np.ascontiguousarray(np.asarray(L0_attn_W_O, np.float32)),
        "w_gate": np.ascontiguousarray(np.asarray(L0_ffn_w_gate, np.float32)),
        "w_up": np.ascontiguousarray(np.asarray(L0_ffn_w_up, np.float32)),
        "w_down": np.ascontiguousarray(np.asarray(L0_ffn_w_down, np.float32)),
        "rope_cos": cs, "rope_sin": sn, "masks": masks, "sel": _den_sel(),
    }
    in_maps = [dict(common, x=np.ascontiguousarray(x[c])) for c in range(8)]
    res = run_bass_kernel_spmd(nc, in_maps, core_ids=list(range(8)))
    out = np.stack([np.asarray(res.results[c]["out"], np.float32) for c in range(8)])
    return out


# version 33
# speedup vs baseline: 1.0775x; 1.0408x over previous
import numpy as np
import ml_dtypes
from contextlib import ExitStack

import concourse.bass as bass
import concourse.mybir as mybir
from concourse.bass_utils import run_bass_kernel_spmd

F32 = mybir.dt.float32
BF16 = mybir.dt.bfloat16
FP16 = mybir.dt.float16
OP = mybir.AluOpType
AF = mybir.ActivationFunctionType
AX = mybir.AxisListType

S, D, NH, DH, DHR, DL, DI = 2048, 1024, 16, 64, 32, 256, 2816
NT = S // 128
QCH = 512
NQ = S // QCH
NB_D = D // 16
NB_I = DI // 16

import math
import ctypes
C_SM = 1.0 / float(np.float32(np.sqrt(np.float32(96.0))))
C_HI = float(np.float32(C_SM))
C_LO = C_SM - C_HI
E_HI = math.exp(80.0 * C_SM)
E_LO = math.exp(-80.0 * C_SM)
LN_EPS = 1e-5

SWAP32 = [1, 0, 3, 2, 5, 4, 7, 6, 9, 8, 11, 10, 13, 12, 15, 14,
          17, 16, 19, 18, 21, 20, 23, 22, 25, 24, 27, 26, 29, 28, 31, 30]


def _rope_tables():
    libm = ctypes.CDLL("libm.so.6")
    libm.cosf.restype = ctypes.c_float
    libm.cosf.argtypes = [ctypes.c_float]
    libm.sinf.restype = ctypes.c_float
    libm.sinf.argtypes = [ctypes.c_float]
    inv = (10000.0 ** (-np.arange(0, DHR, 2, dtype=np.float64) / DHR)).astype(np.float32)
    ang = np.arange(S, dtype=np.float32)[:, None] * inv
    cos = np.array([[libm.cosf(float(a)) for a in row] for row in ang], np.float32)
    sin = np.array([[libm.sinf(float(a)) for a in row] for row in ang], np.float32)
    cs = np.zeros((128, S), np.float32)
    sn = np.zeros((128, S), np.float32)
    for p in range(128):
        c = p % DHR
        i = c // 2
        cs[p] = cos[:, i]
        sn[p] = (-sin[:, i]) if (c % 2 == 0) else sin[:, i]
    return cs, sn


def _den_sel():
    s = np.zeros((16, 8, 128), np.float32)
    for c in range(8):
        s[2 * c, c, 0:64] = 1.0
        s[2 * c + 1, c, 64:128] = 1.0
    return s


def _causal_masks():
    m = np.zeros((128, 4, QCH), np.float32)
    for j in range(4):
        for k in range(128):
            q0 = k + 128 * j
            if q0 < QCH:
                m[k, j, q0:] = 1.0
    return m


class Sem:
    def __init__(self, ctx, nc, name):
        self.h = ctx.enter_context(nc.semaphore(name))
        self.count = 0

    def wait(self, eng, at=None):
        eng.wait_ge(self.h, self.count if at is None else at)


class DStream:
    def __init__(self, ctx, nc, name):
        self.h = ctx.enter_context(nc.semaphore(name))
        self.count = 0

    def dma(self, q, out, in_, **kw):
        if self.count:
            q.wait_ge(self.h, self.count)
        q.dma_start(out=out, in_=in_, **kw).then_inc(self.h, 16)
        self.count += 16
        return self.count

    def wait(self, eng, at=None):
        eng.wait_ge(self.h, self.count if at is None else at)


def build_nc(dump=False, dump_scratch=False):
    nc = bass.Bass("TRN2", target_bir_lowering=False,
                   dynamic_dma_scratch_size=16384,
                   detect_race_conditions=False)
    scratch_kind = "ExternalOutput" if (dump or dump_scratch) else "Internal"
    ctx = ExitStack()

    x_d = nc.dram_tensor("x", [S, D], F32, kind="ExternalInput")
    wdram = {}
    for name, shape in [("W_DQ", [D, DL]), ("W_UQ", [DL, D]), ("W_QR", [DL, NH * DHR]),
                        ("W_DKV", [D, DL]), ("W_UK", [DL, D]), ("W_UV", [DL, D]),
                        ("W_KR", [D, DHR]), ("W_O", [D, D]),
                        ("w_gate", [DI, D]), ("w_up", [DI, D]), ("w_down", [D, DI])]:
        wdram[name] = nc.dram_tensor(name, shape, F32, kind="ExternalInput")
    cs_d = nc.dram_tensor("rope_cos", [128, S], F32, kind="ExternalInput")
    sn_d = nc.dram_tensor("rope_sin", [128, S], F32, kind="ExternalInput")
    mask_d = nc.dram_tensor("masks", [128, 4, QCH], F32, kind="ExternalInput")
    sel_d = nc.dram_tensor("sel", [16, 8, 128], F32, kind="ExternalInput")
    den_dr = nc.dram_tensor("den_scr", [16, S], F32, kind=scratch_kind)
    out_d = nc.dram_tensor("out", [S, D], F32, kind="ExternalOutput")
    r0_dr = nc.dram_tensor("r0_scr", [S, D], BF16, kind=scratch_kind)
    r1_dr = nc.dram_tensor("r1_scr", [S, D], BF16, kind=scratch_kind)
    at_dr = nc.dram_tensor("attn_scr", [128, 8, S], BF16, kind=scratch_kind)
    wqd_dr = nc.dram_tensor("wqd_scr", [128, 22, D], BF16, kind=scratch_kind)
    wqu_dr = nc.dram_tensor("wqu_scr", [128, 8, DI], BF16, kind=scratch_kind)
    wqg_dr = nc.dram_tensor("wqg_scr", [128, 8, DI], BF16, kind=scratch_kind)

    t_d = Sem(ctx, nc, "t_d")
    v_d = Sem(ctx, nc, "v_d")
    a_d = Sem(ctx, nc, "a_d")
    g_c = Sem(ctx, nc, "g_c")

    s_tab = DStream(ctx, nc, "s_tab")
    s_x = [DStream(ctx, nc, f"s_x{i}") for i in range(2)]
    s_a4r = [DStream(ctx, nc, f"s_a4r{i}") for i in range(2)]
    s_a4a = [DStream(ctx, nc, f"s_a4a{i}") for i in range(2)]
    s_w = [DStream(ctx, nc, f"s_w{i}") for i in range(2)]
    s_r1 = [DStream(ctx, nc, f"s_r1{i}") for i in range(2)]
    s_wl = DStream(ctx, nc, "s_wl")
    s_uv = DStream(ctx, nc, "s_uv")
    s_kh = DStream(ctx, nc, "s_kh")
    s_qh = [DStream(ctx, nc, f"s_qh{i}") for i in range(2)]
    s_r0s = [DStream(ctx, nc, f"s_r0s{i}") for i in range(2)]
    s_ats = DStream(ctx, nc, "s_ats")
    s_den = DStream(ctx, nc, "s_den")
    s_dw = DStream(ctx, nc, "s_dw")
    s_wq = DStream(ctx, nc, "s_wq")
    s_r1s = [DStream(ctx, nc, f"s_r1s{i}") for i in range(2)]
    s_tp = [DStream(ctx, nc, f"s_tp{i}") for i in range(22)]
    s_out = DStream(ctx, nc, "s_out")
    s_dbg = DStream(ctx, nc, "s_dbg")

    def AL(name, shape, dtype):
        return ctx.enter_context(nc.sbuf_tensor(name, shape, dtype, side="left"))

    stat = AL("stat", [128, 16], F32)
    sqsc = AL("sqsc", [128, 512], F32)
    lnst = AL("lnst", [128, D], BF16)
    ones1 = AL("ones1", [1, 64], F32)
    xst = AL("xst", [128, 2, D], F32)
    r0st = AL("r0st", [128, 2, D], BF16)

    blk = ctx.enter_context(nc.Block())

    actxL = ExitStack()
    def AAL(name, shape, dtype):
        return actxL.enter_context(nc.sbuf_tensor(name, shape, dtype, side="left"))
    cqT = AAL("cqT", [128, 2, S], BF16)
    ckvT = AAL("ckvT", [128, 2, S], BF16)
    Vb = AAL("Vb", [128, NT, NH * 65], BF16)
    QrT = AAL("QrT", [128, 4, S], BF16)
    KrT = AAL("KrT", [32, S], BF16)
    mask_s = AAL("mask", [128, 4, QCH], BF16)
    wb_UQ = AAL("wb_UQ", [128, 2, D], BF16)
    wb_UK = AAL("wb_UK", [128, 2, D], BF16)
    wb_QR = AAL("wb_QR", [128, 2, NH * DHR], BF16)
    wb_WO = AAL("wb_WO", [128, 8, D], BF16)
    rowr = AAL("rowr", [1, 2, QCH], F32)
    sel_s = AAL("sel_s", [16, 8, 128], F32)

    actxR = ExitStack()
    def AAR(name, shape, dtype):
        return actxR.enter_context(nc.sbuf_tensor(name, shape, dtype, side="right"))
    wb_DQ = AAR("wb_DQ", [128, 8, DL], BF16)
    wb_DKV = AAR("wb_DKV", [128, 8, DL], BF16)
    wb_KR = AAR("wb_KR", [128, 8, DHR], BF16)
    ln1T = AAR("ln1T", [128, 8, S], BF16)
    cs_s = AAR("cs", [128, S], F32)
    sn_s = AAR("sn", [128, S], F32)
    ropeA = AAR("ropeA", [128, S], F32)
    ropeB = AAR("ropeB", [128, S], F32)
    ropeC = AAR("ropeC", [128, S], F32)

    P = nc.alloc_psum_tensor
    psA = P("psA", [128, 512], F32)
    psB = P("psB", [128, 512], F32)
    psC = P("psC", [128, 512], F32)
    psD = P("psD", [128, 512], F32)
    psS0 = P("psS0", [128, QCH], F32)
    psS1 = P("psS1", [128, QCH], F32)
    psAV2 = [P("psAV0", [65, QCH], F32), P("psAV1", [65, QCH], F32)]

    @blk.sync
    def _(sync):
        s_tab.dma(sync, out=cs_s[:], in_=cs_d[:])
        s_tab.dma(sync, out=sn_s[:], in_=sn_d[:])

    @blk.gpsimd
    def _(g):
        for dst, name in ((wb_DQ, "W_DQ"), (wb_DKV, "W_DKV"), (wb_KR, "W_KR"),
                          (wb_UQ, "W_UQ"), (wb_QR, "W_QR"), (wb_UK, "W_UK"),
                          (wb_WO, "W_O")):
            s_wl.dma(g, out=dst[:],
                     in_=wdram[name][:].rearrange("(a p) l -> p a l", p=128))
        s_wl.dma(g, out=mask_s[:], in_=mask_d[:])
        s_wl.dma(g, out=sel_s[:], in_=sel_d[:])
        g.memset(ones1[:], 1.0).then_inc(g_c.h, 1)
        g.memset(Vb[:].rearrange("p t (h c) -> p t h c", c=65)[:, :, :, 64:65],
                 1.0).then_inc(g_c.h, 1)
    g_c.count += 2

    @blk.vector
    def _(v):
        s_wl.wait(v)
        for wt, ncol in ((wb_UK, 2 * D), (wb_KR, 8 * DHR)):
            v.tensor_scalar(ropeA[:, 0:ncol], wt[:].rearrange("p a l -> p (a l)"),
                            C_HI, None, OP.mult)
            v.drain()
            v.scalar_tensor_tensor(wt[:].rearrange("p a l -> p (a l)"),
                                   wt[:].rearrange("p a l -> p (a l)"), C_LO,
                                   ropeA[:, 0:ncol], OP.mult, OP.add)
            v.drain()
        v.engine_nop().then_inc(v_d.h, 1)
    v_d.count += 1
    kw_scaled_v = v_d.count

    xst_free_v = [0, 0]
    lnst_tp_prev = [s.count for s in s_tp[:8]]

    ln1_tp_tiles = []
    for st in range(NT):
        b = st % 2

        @blk.sync
        def _(sync, st=st, b=b, w=xst_free_v[b]):
            sync.wait_ge(v_d.h, w)
            s_x[b].dma(sync, out=xst[:, b, :], in_=x_d[128 * st:128 * (st + 1), :])
        x_in = s_x[b].count

        @blk.vector
        def _(v, st=st, b=b, x_in=x_in):
            s_x[b].wait(v, x_in)
            v.reduce_sum(stat[:, 0:1], xst[:, b, :], AX.X)
            v.tensor_tensor(sqsc[:], xst[:, b, 0:512], xst[:, b, 0:512], OP.mult)
            v.reduce_sum(stat[:, 1:2], sqsc[:], AX.X)
            v.tensor_tensor(sqsc[:], xst[:, b, 512:1024], xst[:, b, 512:1024], OP.mult)
            v.reduce_sum(stat[:, 5:6], sqsc[:], AX.X)
            v.drain()
            v.tensor_tensor(stat[:, 1:2], stat[:, 1:2], stat[:, 5:6], OP.add)
            v.tensor_scalar(stat[:, 0:1], stat[:, 0:1], 1.0 / D, None, OP.mult)
            v.tensor_scalar(stat[:, 1:2], stat[:, 1:2], 1.0 / D, None, OP.mult)
            v.scalar_tensor_tensor(stat[:, 2:3], stat[:, 0:1], 1.0, stat[:, 0:1],
                                   OP.mult, OP.mult)
            v.tensor_tensor(stat[:, 1:2], stat[:, 1:2], stat[:, 2:3], OP.subtract)
            v.tensor_scalar(stat[:, 1:2], stat[:, 1:2], LN_EPS, None, OP.add)
            v.engine_nop().then_inc(v_d.h, 1)
        v_d.count += 1

        @blk.scalar
        def _(s, st=st):
            v_d.wait(s)
            s.activation(stat[:, 3:4], stat[:, 1:2], AF.Sqrt).then_inc(a_d.h, 1)
        a_d.count += 1

        @blk.vector
        def _(v, st=st, b=b, tp=list(lnst_tp_prev), r0f=s_r0s[b].count):
            a_d.wait(v)
            for f in range(8):
                v.wait_ge(s_tp[f].h, tp[f])
            v.wait_ge(s_r0s[b].h, r0f)
            v.reciprocal(stat[:, 4:5], stat[:, 3:4])
            v.tensor_scalar(lnst[:], xst[:, b, :], stat[:, 0:1], stat[:, 4:5],
                            OP.subtract, OP.mult)
            v.tensor_copy(r0st[:, b, :], xst[:, b, :])
            v.drain()
            v.engine_nop().then_inc(v_d.h, 1)
        v_d.count += 1
        xst_free_v[b] = v_d.count

        @blk.gpsimd
        def _(g, st=st, b=b):
            v_d.wait(g)
            s_r0s[b].dma(g, out=r0_dr[128 * st:128 * (st + 1), :], in_=r0st[:, b, :])

        @blk.sync
        def _(sync, st=st):
            v_d.wait(sync)
            for f in range(8):
                s_tp[f].dma(sync, out=ln1T[:, f, 128 * st:128 * (st + 1)],
                            in_=lnst[:, 128 * f:128 * (f + 1)], transpose=True)
        lnst_tp_prev = [s_tp[f].count for f in range(8)]
        ln1_tp_tiles.append(list(lnst_tp_prev))

    class MMPipe:
        def __init__(self):
            self.jobs = []

        def add(self, mm_fn, evac_fn):
            self.jobs.append((mm_fn, evac_fn))

        def emit(self, pre_tensor_wait=None, banks=2):
            n0_t = t_d.count
            n0_a = a_d.count
            jobs = self.jobs

            @blk.tensor
            def _(t):
                if pre_tensor_wait is not None:
                    pre_tensor_wait(t)
                for i, (mm_fn, _e) in enumerate(jobs):
                    t.wait_ge(a_d.h, max(n0_a, n0_a + i - banks + 1))
                    mm_fn(t).then_inc(t_d.h, 1)

            @blk.scalar
            def _(s):
                for i, (_m, evac_fn) in enumerate(jobs):
                    s.wait_ge(t_d.h, n0_t + i + 1)
                    evac_fn(s).then_inc(a_d.h, 1)

            t_d.count += len(jobs)
            a_d.count += len(jobs)

    ln1_tp = [s_tp[f].count for f in range(8)]

    def wait_ln1(t):
        s_wl.wait(t)
        s_tab.wait(t)

    pipe = MMPipe()
    for (wten, dst) in ((wb_DQ, cqT), (wb_DKV, ckvT)):
        for l in range(2):
            for sc in range(4):
                ps = psA if (sc % 2 == 0) else psB
                def mm(t, wten=wten, l=l, sc=sc, ps=ps,
                       tp=list(ln1_tp_tiles[4 * sc + 3])):
                    for f in range(8):
                        t.wait_ge(s_tp[f].h, tp[f])
                    for f in range(8):
                        i_ = t.matmul(ps[0:128, :], wten[:, f, 128 * l:128 * (l + 1)],
                                      ln1T[:, f, 512 * sc:512 * (sc + 1)],
                                      start=(f == 0), stop=(f == 7))
                    return i_
                def ev(s, dst=dst, l=l, sc=sc, ps=ps):
                    return s.activation(dst[:, l, 512 * sc:512 * (sc + 1)],
                                        ps[0:128, :], AF.Copy)
                pipe.add(mm, ev)
    pipe.emit(pre_tensor_wait=wait_ln1)

    pipe = MMPipe()
    for sc in range(4):
        ps = psA if (sc % 2 == 0) else psB
        def mm(t, sc=sc, ps=ps):
            for f in range(8):
                i_ = t.matmul(ps[0:32, :], wb_KR[:, f, :],
                              ln1T[:, f, 512 * sc:512 * (sc + 1)],
                              start=(f == 0), stop=(f == 7))
            return i_
        def ev(s, sc=sc, ps=ps):
            return s.activation(ropeA[0:32, 512 * sc:512 * (sc + 1)], ps[0:32, :], AF.Copy)
        pipe.add(mm, ev)

    def wait_kscale(t):
        t.wait_ge(v_d.h, kw_scaled_v)
    pipe.emit(pre_tensor_wait=wait_kscale)

    @blk.vector
    def _(v):
        a_d.wait(v)
        v.stream_shuffle(ropeB[0:32, :], ropeA[0:32, :], SWAP32)
        v.tensor_tensor(ropeC[0:32, :], ropeA[0:32, :], cs_s[0:32, :], OP.mult)
        v.tensor_tensor(ropeB[0:32, :], ropeB[0:32, :], sn_s[0:32, :], OP.mult)
        v.tensor_tensor(ropeC[0:32, :], ropeC[0:32, :], ropeB[0:32, :], OP.add)
        v.tensor_copy(KrT[:], ropeC[0:32, :])
        v.engine_nop().then_inc(v_d.h, 1)
    v_d.count += 1

    for qt in range(4):
        rope_free = v_d.count
        pipe = MMPipe()
        for sc in range(4):
            ps = psA if (sc % 2 == 0) else psB
            def mm(t, qt=qt, sc=sc, ps=ps, rope_free=rope_free):
                if sc == 0:
                    t.wait_ge(v_d.h, rope_free)
                for l in range(2):
                    i_ = t.matmul(ps[0:128, :], wb_QR[:, l, 128 * qt:128 * (qt + 1)],
                                  cqT[:, l, 512 * sc:512 * (sc + 1)],
                                  start=(l == 0), stop=(l == 1))
                return i_
            def ev(s, sc=sc, ps=ps):
                return s.activation(ropeA[:, 512 * sc:512 * (sc + 1)], ps[0:128, :], AF.Copy)
            pipe.add(mm, ev)
        pipe.emit()

        @blk.vector
        def _(v, qt=qt):
            a_d.wait(v)
            v.stream_shuffle(ropeB[:], ropeA[:], SWAP32)
            v.tensor_tensor(ropeC[:], ropeA[:], cs_s[:], OP.mult)
            v.tensor_tensor(ropeB[:], ropeB[:], sn_s[:], OP.mult)
            v.tensor_tensor(ropeC[:], ropeC[:], ropeB[:], OP.add)
            v.tensor_copy(QrT[:, qt, :], ropeC[:])
            v.engine_nop().then_inc(v_d.h, 1)
        v_d.count += 1
    qr_done_v = v_d.count

    wb_UQv = AAR("wb_UV", [128, 2, D], BF16)
    pipe = MMPipe()
    for st in range(NT):
        for half in range(2):
            ps = psC if (half == 0) else psD
            def mm(t, st=st, half=half, ps=ps):
                for l in range(2):
                    i_ = t.matmul(ps[0:128, :], ckvT[:, l, 128 * st:128 * (st + 1)],
                                  wb_UQv[:, l, 512 * half:512 * (half + 1)],
                                  start=(l == 0), stop=(l == 1))
                return i_
            def ev(s, st=st, half=half, ps=ps):
                dst = Vb[:, st, :].rearrange("p (h c) -> p h c", c=65)[:, 8 * half:8 * (half + 1), 0:64]
                return s.activation(dst, ps[0:128, :].rearrange("p (h c) -> p h c", c=64),
                                    AF.Copy)
            pipe.add(mm, ev)

    @blk.gpsimd
    def _(g):
        s_uv.dma(g, out=wb_UQv[:],
                 in_=wdram["W_UV"][:].rearrange("(a p) l -> p a l", p=128))

    def wait_uv(t):
        s_uv.wait(t)
        g_c.wait(t)
    pipe.emit(pre_tensor_wait=wait_uv)

    actxR.close()
    Qh = AAL("Qh", [128, S], BF16)
    Kh = AAL("Kh", [128, S], BF16)
    Eb = AAL("Eb", [128, NT, QCH], BF16)
    attnSt = AAL("attnSt", [64, 2, QCH], BF16)
    attnTs = AAL("attnTs", [128, 2, 8, 128], BF16)
    r1stA = AAL("r1stA", [128, 2, D], BF16)
    den_s = AAL("den_s", [16, 2, 128], F32)
    denb = AAL("denb", [128, 4, 128], BF16)

    HDI = DI // 2
    wraw2 = AAL("wraw2", [128, 2, HDI], F32)
    qw_gm = AAL("qw_gm", [128, 128], F32)
    qw_s = AAL("qw_s", [128, 128], F32)
    qw_s2 = AAL("qw_s2", [128, 128], F32)
    qw_r = AAL("qw_r", [128, 128], F32)
    qw_s16 = AAL("qw_s16", [128, 128], FP16)
    qw_an = AAL("qw_an", [128, HDI], FP16)
    qw_acc = AAL("qw_acc", [128, HDI], FP16)
    qw_cA = AAL("qw_cA", [128, HDI], FP16)
    qw_sg = AAL("qw_sg", [128, HDI], FP16)
    qw_out = AAL("qw_out", [128, HDI], BF16)
    qfh_an = AAL("qfh_an", [128, HDI], F32)
    qfh_acc = AAL("qfh_acc", [128, HDI], F32)
    qfh_cA = AAL("qfh_cA", [128, HDI], F32)
    dstTs = AAL("dstTs", [128, 11, 128], BF16)

    def q_chain(v, an_ap, acc_ap, cA_ap, out_ap, n):
        v.tensor_scalar(acc_ap, an_ap, 5.0, 2.0, OP.is_gt, OP.mult)
        v.scalar_tensor_tensor(acc_ap, an_ap, 2.5, acc_ap, OP.is_gt, OP.add)
        v.scalar_tensor_tensor(acc_ap, an_ap, 3.5, acc_ap, OP.is_gt, OP.add)
        v.tensor_scalar(cA_ap, an_ap, 0.25, None, OP.is_gt)
        v.scalar_tensor_tensor(cA_ap, an_ap, 0.75, cA_ap, OP.is_gt, OP.add)
        v.scalar_tensor_tensor(cA_ap, an_ap, 1.25, cA_ap, OP.is_gt, OP.add)
        v.scalar_tensor_tensor(cA_ap, an_ap, 1.75, cA_ap, OP.is_gt, OP.add)
        v.scalar_tensor_tensor(out_ap, cA_ap, 0.5, acc_ap, OP.mult, OP.add)

    wq_tp_guard = [[]]
    qw_load_guard = [0, 0]
    qw_sbuf_guard = [0]

    def make_wq_unit(src_dram, row0, col0, n_cols, chain, dst_dr, c0, nchunks, wt_col):
        def emit(u):
            p = u % 2

            @blk.gpsimd
            def _(g, p=p, vq=qw_load_guard[p]):
                if vq:
                    g.wait_ge(v_d.h, vq)
                s_w[p].dma(g, out=wraw2[:, p, 0:n_cols],
                           in_=src_dram[row0:row0 + 128, col0:col0 + n_cols])
            w_in = s_w[p].count

            abuf = qw_cA if chain == "fp16" else qfh_an

            @blk.scalar
            def _(s, p=p, w_in=w_in, vq=qw_sbuf_guard[0], abuf=abuf):
                s_w[p].wait(s, w_in)
                if vq:
                    s.wait_ge(v_d.h, vq)
                s.activation(abuf[:, 0:n_cols], wraw2[:, p, 0:n_cols], AF.Abs,
                             scale=6.0)
                s.activation(qw_sg[:, 0:n_cols], wraw2[:, p, 0:n_cols],
                             AF.Sign).then_inc(a_d.h, 1)
            a_d.count += 1

            @blk.vector
            def _(v, p=p, aa=a_d.count, guard=list(wq_tp_guard[0]), nb=n_cols // 16):
                v.wait_ge(a_d.h, aa)
                for c, val in guard:
                    v.wait_ge(s_tp[c].h, val)
                v.tensor_reduce(qw_gm[:, 0:nb],
                                wraw2[:, p, 0:n_cols].rearrange("p (c k) -> p c k", k=16),
                                AX.X, OP.max, apply_absolute_value=True)
                v.tensor_scalar(qw_s[:, 0:nb], qw_gm[:, 0:nb], 6e-12, None, OP.max)
                v.tensor_scalar(qw_s2[:, 0:nb], qw_gm[:, 0:nb], 0.16666666666666666,
                                1e-12, OP.mult, OP.max)
                v.drain()
                v.reciprocal(qw_r[:, 0:nb], qw_s[:, 0:nb])
                v.tensor_copy(qw_s16[:, 0:nb], qw_s2[:, 0:nb])
                v.drain()
                if chain == "fp16":
                    v.tensor_tensor(qw_an[:, 0:n_cols].rearrange("p (c k) -> p c k", k=16),
                                    qw_cA[:, 0:n_cols].rearrange("p (c k) -> p c k", k=16),
                                    qw_r[:, 0:nb].unsqueeze(2).broadcast_to([128, nb, 16]),
                                    OP.mult)
                    q_chain(v, qw_an[:, 0:n_cols], qw_acc[:, 0:n_cols],
                            qw_cA[:, 0:n_cols], qw_acc[:, 0:n_cols], n_cols)
                    v.tensor_tensor(qw_an[:, 0:n_cols], qw_acc[:, 0:n_cols],
                                    qw_sg[:, 0:n_cols], OP.mult)
                    v.tensor_tensor(qw_out[:, 0:n_cols].rearrange("p (c k) -> p c k", k=16),
                                    qw_an[:, 0:n_cols].rearrange("p (c k) -> p c k", k=16),
                                    qw_s16[:, 0:nb].unsqueeze(2).broadcast_to([128, nb, 16]),
                                    OP.mult)
                else:
                    v.tensor_tensor(qfh_an[:, 0:n_cols].rearrange("p (c k) -> p c k", k=16),
                                    qfh_an[:, 0:n_cols].rearrange("p (c k) -> p c k", k=16),
                                    qw_r[:, 0:nb].unsqueeze(2).broadcast_to([128, nb, 16]),
                                    OP.mult)
                    q_chain(v, qfh_an[:, 0:n_cols], qfh_acc[:, 0:n_cols],
                            qfh_cA[:, 0:n_cols], qfh_an[:, 0:n_cols], n_cols)
                    v.tensor_tensor(qfh_acc[:, 0:n_cols], qfh_an[:, 0:n_cols],
                                    qw_sg[:, 0:n_cols], OP.mult)
                    v.tensor_tensor(qw_out[:, 0:n_cols].rearrange("p (c k) -> p c k", k=16),
                                    qfh_acc[:, 0:n_cols].rearrange("p (c k) -> p c k", k=16),
                                    qw_s16[:, 0:nb].unsqueeze(2).broadcast_to([128, nb, 16]),
                                    OP.mult)
                v.drain()
                v.engine_nop().then_inc(v_d.h, 1)
            v_d.count += 1
            qw_load_guard[p] = v_d.count
            qw_sbuf_guard[0] = v_d.count

            @blk.sync
            def _(sync, vv=v_d.count, dw=s_dw.count):
                sync.wait_ge(v_d.h, vv)
                sync.wait_ge(s_dw.h, dw)
                for ci in range(nchunks):
                    s_tp[ci].dma(sync, out=dstTs[:, ci, :],
                                 in_=qw_out[:, 128 * ci:128 * (ci + 1)], transpose=True)
            tpc = [(ci, s_tp[ci].count) for ci in range(nchunks)]
            wq_tp_guard[0] = tpc

            @blk.gpsimd
            def _(g, tpc=list(tpc)):
                for ci, val in tpc:
                    g.wait_ge(s_tp[ci].h, val)
                s_dw.dma(g, out=dst_dr[:, c0:c0 + nchunks,
                                       128 * wt_col:128 * (wt_col + 1)],
                         in_=dstTs[:, 0:nchunks, :])
        return emit

    wq_units = []
    for wt in range(22):
        wq_units.append(make_wq_unit(wdram["w_gate"], 128 * wt, 0, D, "fp16",
                                     wqg_dr, 0, 8, wt))
    for wt in range(22):
        wq_units.append(make_wq_unit(wdram["w_up"], 128 * wt, 0, D, "fp16",
                                     wqu_dr, 0, 8, wt))
    for wt in range(8):
        for half in range(2):
            wq_units.append(make_wq_unit(wdram["w_down"], 128 * wt, HDI * half,
                                         HDI, "fp32", wqd_dr, 11 * half, 11, wt))
    wq_done_marks = {}
    wq_unit_idx = [0]

    def pull_wq_unit():
        i = wq_unit_idx[0]
        if i < len(wq_units):
            wq_units[i](i)
            wq_unit_idx[0] = i + 1
            if i + 1 == 22:
                wq_done_marks["gate"] = s_dw.count
            elif i + 1 == 44:
                wq_done_marks["up"] = s_dw.count
            elif i + 1 == 60:
                wq_done_marks["down"] = s_dw.count

    t_after_V = t_d.count

    @blk.gpsimd
    def _(g):
        g.wait_ge(v_d.h, qr_done_v)
        g.wait_ge(t_d.h, t_after_V)
        s_kh.dma(g, out=Kh[64:96, :], in_=KrT[:])

    ats_prev2 = [s_ats.count, s_ats.count]
    den_prev2 = [s_den.count, s_den.count]
    av_tail_a = [0, 0]
    av_tail_v = [0, 0]
    for h in range(NH):
        pipe = MMPipe()
        for (wten, dst, prow) in ((wb_UQ, Qh, 0), (wb_UK, Kh, 64)):
            for sc in range(4):
                ps = psC if (sc % 2 == 0) else psD
                def mm(t, wten=wten, sc=sc, ps=ps, prow=prow, h=h):
                    src = cqT if wten is wb_UQ else ckvT
                    for l in range(2):
                        i_ = t.matmul(ps[prow:prow + 64, :], wten[:, l, 64 * h:64 * (h + 1)],
                                      src[:, l, 512 * sc:512 * (sc + 1)],
                                      start=(l == 0), stop=(l == 1))
                    return i_
                def ev(s, wten=wten, dst=dst, sc=sc, ps=ps, prow=prow):
                    return s.activation(dst[0:64, 512 * sc:512 * (sc + 1)],
                                        ps[prow:prow + 64, :], AF.Copy)
                pipe.add(mm, ev)

        def pre(t, h=h):
            if h == 0:
                t.wait_ge(v_d.h, qr_done_v)
                s_kh.wait(t)
        pipe.emit(pre_tensor_wait=pre)

        @blk.gpsimd
        def _(g, h=h):
            a_d.wait(g)
            qtile, qoff = divmod(32 * h, 128)
            s_qh[h % 2].dma(g, out=Qh[64:96, :], in_=QrT[qoff:qoff + 32, qtile, :])
        qh_in = s_qh[h % 2].count

        a_qk = a_d.count

        for qi in range(NQ):
            nkt = 4 * qi + 4
            pa = (4 * h + qi) % 2
            psAVc = psAV2[pa]
            t0 = t_d.count
            a0 = a_d.count
            vv0 = v_d.count

            @blk.tensor
            def _(t, h=h, qi=qi, nkt=nkt, a0=a0, vv0=vv0, a_qk=a_qk, qh_in=qh_in,
                  psAVc=psAVc, ta=av_tail_a[pa], tv=av_tail_v[pa]):
                if qi == 0:
                    t.wait_ge(a_d.h, a_qk)
                    s_qh[h % 2].wait(t, qh_in)
                if ta:
                    t.wait_ge(a_d.h, ta)
                    t.wait_ge(v_d.h, tv)
                for kt in range(nkt):
                    ps = psS0 if (kt % 2 == 0) else psS1
                    if kt >= 2:
                        t.wait_ge(a_d.h, a0 + kt - 1)
                    t.matmul(ps[0:128, :], Kh[0:96, 128 * kt:128 * (kt + 1)],
                             Qh[0:96, QCH * qi:QCH * (qi + 1)],
                             start=True, stop=True).then_inc(t_d.h, 1)
                    if kt >= 1:
                        t.wait_ge(v_d.h, vv0 + kt)
                        t.matmul(psAVc[:, :], Vb[:, kt - 1, 65 * h:65 * (h + 1)],
                                 Eb[:, kt - 1, :], start=(kt - 1 == 0), stop=False)
                t.wait_ge(v_d.h, vv0 + nkt)
                t.matmul(psAVc[:, :], Vb[:, nkt - 1, 65 * h:65 * (h + 1)],
                         Eb[:, nkt - 1, :], start=False, stop=True).then_inc(t_d.h, 1)
            t_d.count += nkt + 1
            t_av_done = t_d.count

            @blk.scalar
            def _(s, nkt=nkt, t0=t0):
                for kt in range(nkt):
                    ps = psS0 if (kt % 2 == 0) else psS1
                    s.wait_ge(t_d.h, t0 + kt + 1)
                    s.activation(Eb[:, kt, :], ps[0:128, :],
                                 AF.Exp).then_inc(a_d.h, 1)
            a_d.count += nkt

            @blk.vector
            def _(v, qi=qi, nkt=nkt, a0=a0):
                for kt in range(nkt):
                    v.wait_ge(a_d.h, a0 + kt + 1)
                    v.tensor_scalar(Eb[:, kt, :], Eb[:, kt, :], E_LO, E_HI, OP.max, OP.min)
                    if kt >= 4 * qi:
                        v.tensor_tensor(Eb[:, kt, :], Eb[:, kt, :],
                                        mask_s[:, kt - 4 * qi, :], OP.mult)
                    v.engine_nop().then_inc(v_d.h, 1)
            v_d.count += nkt

            @blk.vector
            def _(v, pa=pa, t_av_done=t_av_done, psAVc=psAVc, dp=den_prev2[pa]):
                v.wait_ge(t_d.h, t_av_done)
                v.wait_ge(s_den.h, dp)
                v.reciprocal(rowr[:, pa, :], psAVc[64:65, :])
                v.drain()
                v.engine_nop().then_inc(v_d.h, 1)
            v_d.count += 1
            av_tail_v[pa] = v_d.count

            @blk.scalar
            def _(s, pa=pa, t_av_done=t_av_done, psAVc=psAVc, atp=ats_prev2[pa]):
                s.wait_ge(t_d.h, t_av_done)
                s.wait_ge(s_ats.h, atp)
                s.activation(attnSt[:, pa, :], psAVc[0:64, :],
                             AF.Copy).then_inc(a_d.h, 1)
            a_d.count += 1
            av_tail_a[pa] = a_d.count

            @blk.gpsimd
            def _(g, h=h, qi=qi, pa=pa, ea=av_tail_a[pa], ev=av_tail_v[pa]):
                g.wait_ge(a_d.h, ea)
                g.wait_ge(v_d.h, ev)
                s_ats.dma(g, out=at_dr[64 * (h % 2):64 * (h % 2) + 64, h // 2,
                                       QCH * qi:QCH * (qi + 1)],
                          in_=attnSt[:, pa, :])
                s_den.dma(g, out=den_dr[h:h + 1, QCH * qi:QCH * (qi + 1)],
                          in_=rowr[:, pa, :])
            ats_prev2[pa] = s_ats.count
            den_prev2[pa] = s_den.count
            pull_wq_unit()

    while wq_unit_idx[0] < len(wq_units):
        pull_wq_unit()

    attn_done_v = v_d.count

    a4_t_prev = [0, 0]
    a4_res_prev = [0, 0]
    a4_tt_prev = 0
    for st in range(NT):
        b4 = st % 2

        @blk.sync
        def _(sync, st=st, b4=b4, tfree=a4_t_prev[b4], rfree=a4_res_prev[b4]):
            if st == 0:
                for i in range(2):
                    s_r0s[i].wait(sync)
                s_ats.wait(sync)
                s_den.wait(sync)
            if rfree:
                sync.wait_ge(v_d.h, rfree)
            sync.wait_ge(t_d.h, tfree)
            s_a4r[b4].dma(sync, out=r0st[:, b4, :],
                          in_=r0_dr[128 * st:128 * (st + 1), :])
            s_a4a[b4].dma(sync, out=attnTs[:, b4, :, :],
                          in_=at_dr[:, :, 128 * st:128 * (st + 1)])
            s_a4a[b4].dma(sync, out=den_s[:, b4, :],
                          in_=den_dr[:, 128 * st:128 * (st + 1)])
        r0_in = s_a4r[b4].count
        at_in = s_a4a[b4].count

        td0 = t_d.count
        vprev = v_d.count

        v0 = v_d.count

        @blk.tensor
        def _(t, st=st, b4=b4, at_in=at_in, ttp=a4_tt_prev, v0=v0):
            if st == 0:
                t.wait_ge(v_d.h, attn_done_v)
            s_a4a[b4].wait(t, at_in)
            if ttp:
                t.wait_ge(v_d.h, ttp)
            for c in range(8):
                ps = psS0 if c % 2 == 0 else psS1
                sl = (c % 4) // 2
                if c >= 2:
                    t.wait_ge(v_d.h, v0 + c - 1)
                t.matmul(ps[:, 128 * sl:128 * (sl + 1)], sel_s[:, c, :],
                         den_s[:, b4, :], start=True, stop=True).then_inc(t_d.h, 1)
        t_d.count += 8

        @blk.vector
        def _(v, st=st, b4=b4, td0=td0):
            for c in range(8):
                ps = psS0 if c % 2 == 0 else psS1
                sl = (c % 4) // 2
                v.wait_ge(t_d.h, td0 + c + 1)
                v.tensor_copy(denb[:, c % 4, :], ps[:, 128 * sl:128 * (sl + 1)])
                v.tensor_tensor(attnTs[:, b4, c, :], attnTs[:, b4, c, :],
                                denb[:, c % 4, :], OP.mult)
                v.engine_nop().then_inc(v_d.h, 1)
        v_d.count += 8
        a4_tt_prev = v_d.count

        t0 = t_d.count

        @blk.tensor
        def _(t, st=st, b4=b4, vprev=vprev, tte=a4_tt_prev):
            t.wait_ge(v_d.h, tte)
            for half in range(2):
                ps = psC if half == 0 else psB
                for c in range(8):
                    i_ = t.matmul(ps[0:128, :], attnTs[:, b4, c, :],
                                  wb_WO[:, c, 512 * half:512 * (half + 1)],
                                  start=(c == 0), stop=(c == 7))
                i_.then_inc(t_d.h, 1)
        t_d.count += 2
        a4_t_prev[b4] = t_d.count

        @blk.vector
        def _(v, st=st, b4=b4, t0=t0, r0_in=r0_in, r1f=s_r1s[b4].count):
            s_a4r[b4].wait(v, r0_in)
            v.wait_ge(s_r1s[b4].h, r1f)
            for half in range(2):
                ps = psC if half == 0 else psB
                v.wait_ge(t_d.h, t0 + half + 1)
                v.tensor_tensor(r1stA[:, b4, 512 * half:512 * (half + 1)], ps[0:128, :],
                                r0st[:, b4, 512 * half:512 * (half + 1)], OP.add)
                v.drain()
                v.engine_nop().then_inc(v_d.h, 1)
        v_d.count += 2
        a4_res_prev[b4] = v_d.count

        @blk.gpsimd
        def _(g, st=st, b4=b4):
            v_d.wait(g)
            s_r1s[b4].dma(g, out=r1_dr[128 * st:128 * (st + 1), :],
                          in_=r1stA[:, b4, :])

    if dump:
        dmp = {}
        for nm, ten in (("Qh", Qh), ("Kh", Kh), ("Eb", Eb), ("Vb", Vb),
                        ("QrT", QrT), ("KrT", KrT), ("cqT", cqT), ("ckvT", ckvT)):
            shp = [int(d) for d in ten.shape]
            dmp[nm] = nc.dram_tensor("dump_" + nm, shp, BF16, kind="ExternalOutput")

        @blk.sync
        def _(sync):
            v_d.wait(sync)
            a_d.wait(sync)
            for nm, ten in (("Qh", Qh), ("Kh", Kh), ("Eb", Eb), ("Vb", Vb),
                            ("QrT", QrT), ("KrT", KrT), ("cqT", cqT), ("ckvT", ckvT)):
                s_out.dma(sync, out=dmp[nm][:], in_=ten[:])
            s_out.wait(sync)

    actxL.close()

    bctxL = ExitStack()
    def BL(name, shape, dtype):
        return bctxL.enter_context(nc.sbuf_tensor(name, shape, dtype, side="left"))

    wqdT = BL("wqdT", [128, 22, D], BF16)
    q_gm = BL("q_gm", [128, NB_I], F32)
    q_s = BL("q_s", [128, NB_I], F32)
    q_r = BL("q_r", [128, NB_I], F32)
    q_s2 = BL("q_s2", [128, NB_I], F32)
    q_s16 = BL("q_s16", [128, NB_I], FP16)
    q_an = BL("q_an", [128, HDI], FP16)
    q_acc = BL("q_acc", [128, HDI], FP16)
    q_cA = BL("q_cA", [128, DI], FP16)
    q_sg = BL("q_sg", [128, DI], FP16)
    q_out = BL("q_out", [128, DI], BF16)
    wqgT = BL("wqgT", [128, 8, DI], BF16)
    wquT = BL("wquT", [128, 8, DI], BF16)

    a4_end_t = t_d.count
    a4_end_v = v_d.count

    @blk.sync
    def _(sync):
        sync.wait_ge(t_d.h, a4_end_t)
        sync.wait_ge(v_d.h, a4_end_v)
        for i in range(2):
            s_r1s[i].wait(sync)
        sync.wait_ge(s_dw.h, wq_done_marks["gate"])
        for q in range(4):
            s_wq.dma(sync, out=wqgT[:, 2 * q:2 * (q + 1), :],
                     in_=wqg_dr[:, 2 * q:2 * (q + 1), :])
        sync.wait_ge(s_dw.h, wq_done_marks["up"])
        for q in range(4):
            s_wq.dma(sync, out=wquT[:, 2 * q:2 * (q + 1), :],
                     in_=wqu_dr[:, 2 * q:2 * (q + 1), :])
    wq_gate_ld = s_wq.count - 64
    wq_up_ld = s_wq.count

    @blk.gpsimd
    def _(g):
        g.wait_ge(t_d.h, a4_end_t)
        g.wait_ge(v_d.h, a4_end_v)
        for i in range(2):
            s_r1s[i].wait(g)
        g.wait_ge(s_dw.h, wq_done_marks["down"])
        for q in range(4):
            s_uv.dma(g, out=wqdT[:, 6 * q:min(6 * (q + 1), 22), :],
                     in_=wqd_dr[:, 6 * q:min(6 * (q + 1), 22), :])
    wq_down_ld = s_uv.count

    xqT = BL("xqT", [128, 2, 8, 128], BF16)
    hqT = BL("hqT", [128, 2, 22, 128], BF16)
    gsil = BL("gsil", [128, 2, DI], BF16)
    r1st = BL("r1st", [128, 2, D], BF16)
    q2_cA = BL("q2_cA", [128, D], FP16)
    q2_sg = BL("q2_sg", [128, D], FP16)
    outst = xst
    xq_out = r0st

    if dump:
        dbgB = {}
        for nm, cols in (("ln2", D), ("xq", D), ("h", DI), ("hq", DI)):
            dbgB[nm] = nc.dram_tensor("dbgB_" + nm, [NT, 128, cols], BF16,
                                      kind="ExternalOutput")

    g_chunks = [(i * 512, min(512, DI - i * 512)) for i in range((DI + 511) // 512)]
    NG = len(g_chunks)
    out_v_prev = 0
    r1_free = [0, 0]
    xq_v_prev = 0
    hq_v_prev_any = 0
    hq_v_prev2 = [0, 0]
    down_t_prev2 = [0, 0]
    gateup_t_prev2 = [0, 0]
    xqT_tp_prev2 = [[], []]
    silu_a_prev = 0
    hTT_v_prev = 0
    lnst_guard = 0
    out_store_prev2 = [0, 0]
    hq_tp_prev = [(c, s_tp[c].count) for c in range(22)]

    def emit_act_quant(v, src_ap, cA, sg, n, nb, out_ap, h_sel=None):
        if h_sel is None or 0 in h_sel:
            v.tensor_reduce(q_gm[:, 0:nb], src_ap.rearrange("p (c k) -> p c k", k=16),
                            AX.X, OP.max, apply_absolute_value=True)
            v.tensor_scalar(q_s[:, 0:nb], q_gm[:, 0:nb], 6e-12, None, OP.max)
            v.tensor_scalar(q_s2[:, 0:nb], q_gm[:, 0:nb], 0.16666666666666666, 1e-12,
                            OP.mult, OP.max)
            v.drain()
            v.reciprocal(q_r[:, 0:nb], q_s[:, 0:nb])
            v.tensor_copy(q_s16[:, 0:nb], q_s2[:, 0:nb])
            v.drain()
        for h0 in (range(0, n, HDI) if h_sel is None
                   else [i * HDI for i in h_sel]):
            m = min(HDI, n - h0)
            mb = m // 16
            b0 = h0 // 16
            v.tensor_tensor(q_an[:, 0:m].rearrange("p (c k) -> p c k", k=16),
                            cA[:, h0:h0 + m].rearrange("p (c k) -> p c k", k=16),
                            q_r[:, b0:b0 + mb].unsqueeze(2).broadcast_to([128, mb, 16]),
                            OP.mult)
            q_chain(v, q_an[:, 0:m], q_acc[:, 0:m], cA[:, h0:h0 + m], q_acc[:, 0:m], m)
            v.tensor_tensor(q_an[:, 0:m], q_acc[:, 0:m], sg[:, h0:h0 + m], OP.mult)
            v.tensor_tensor(out_ap[:, h0:h0 + m].rearrange("p (c k) -> p c k", k=16),
                            q_an[:, 0:m].rearrange("p (c k) -> p c k", k=16),
                            q_s16[:, b0:b0 + mb].unsqueeze(2).broadcast_to([128, mb, 16]),
                            OP.mult)
        v.drain()

    for st in range(NT):
        b = st % 2
        sb = 8 * b

        @blk.sync
        def _(sync, st=st, b=b, r1f=r1_free[b]):
            if st == 0:
                for i in range(2):
                    s_r1s[i].wait(sync)
                s_dw.wait(sync)
            if r1f:
                sync.wait_ge(v_d.h, r1f)
            s_r1[b].dma(sync, out=r1st[:, b, :], in_=r1_dr[128 * st:128 * (st + 1), :])
        r1_in = s_r1[b].count

        @blk.vector
        def _(v, st=st, b=b, sb=sb, r1_in=r1_in):
            s_r1[b].wait(v, r1_in)
            v.reduce_sum(stat[:, sb:sb + 1], r1st[:, b, :], AX.X)
            v.tensor_tensor(sqsc[:], r1st[:, b, 0:512], r1st[:, b, 0:512], OP.mult)
            v.reduce_sum(stat[:, sb + 1:sb + 2], sqsc[:], AX.X)
            v.tensor_tensor(sqsc[:], r1st[:, b, 512:1024], r1st[:, b, 512:1024], OP.mult)
            v.reduce_sum(stat[:, sb + 5:sb + 6], sqsc[:], AX.X)
            v.drain()
            v.tensor_tensor(stat[:, sb + 1:sb + 2], stat[:, sb + 1:sb + 2],
                            stat[:, sb + 5:sb + 6], OP.add)
            v.tensor_scalar(stat[:, sb:sb + 1], stat[:, sb:sb + 1], 1.0 / D, None, OP.mult)
            v.tensor_scalar(stat[:, sb + 1:sb + 2], stat[:, sb + 1:sb + 2], 1.0 / D,
                            None, OP.mult)
            v.scalar_tensor_tensor(stat[:, sb + 2:sb + 3], stat[:, sb:sb + 1], 1.0,
                                   stat[:, sb:sb + 1], OP.mult, OP.mult)
            v.tensor_tensor(stat[:, sb + 1:sb + 2], stat[:, sb + 1:sb + 2],
                            stat[:, sb + 2:sb + 3], OP.subtract)
            v.tensor_scalar(stat[:, sb + 1:sb + 2], stat[:, sb + 1:sb + 2], LN_EPS,
                            None, OP.add)
            v.engine_nop().then_inc(v_d.h, 1)
        v_d.count += 1
        stats_v = v_d.count

        @blk.scalar
        def _(s, st=st, sb=sb, sv=stats_v):
            s.wait_ge(v_d.h, sv)
            s.activation(stat[:, sb + 3:sb + 4], stat[:, sb + 1:sb + 2],
                         AF.Sqrt).then_inc(a_d.h, 1)
        a_d.count += 1
        sqrt_a = a_d.count

        @blk.vector
        def _(v, st=st, b=b, sb=sb, sa=sqrt_a, lg=lnst_guard):
            v.wait_ge(a_d.h, sa)
            if lg:
                v.wait_ge(a_d.h, lg)
            v.reciprocal(stat[:, sb + 4:sb + 5], stat[:, sb + 3:sb + 4])
            v.tensor_scalar(lnst[:], r1st[:, b, :], stat[:, sb:sb + 1],
                            stat[:, sb + 4:sb + 5], OP.subtract, OP.mult)
            v.engine_nop().then_inc(v_d.h, 1)
        v_d.count += 1
        apply_v = v_d.count

        @blk.scalar
        def _(s, st=st, b=b, av=apply_v, xqp=xq_v_prev):
            s.wait_ge(v_d.h, av)
            if xqp:
                s.wait_ge(v_d.h, xqp)
            s.activation(q2_cA[:], lnst[:], AF.Abs, scale=6.0)
            s.activation(q2_sg[:], lnst[:], AF.Sign).then_inc(a_d.h, 1)
        a_d.count += 1
        xq_abs_a = a_d.count
        lnst_guard = xq_abs_a

        @blk.vector
        def _(v, st=st, b=b, aa=xq_abs_a, guard=list(xqT_tp_prev2[b])):
            v.wait_ge(a_d.h, aa)
            for c, val in guard:
                v.wait_ge(s_tp[c].h, val)
            emit_act_quant(v, lnst[:], q2_cA, q2_sg, D, NB_D, xq_out[:, b, 0:D])
            v.engine_nop().then_inc(v_d.h, 1)
        v_d.count += 1
        xq_v = v_d.count
        xq_v_prev = xq_v

        if dump:
            @blk.gpsimd
            def _(g, st=st, b=b, xq_v=xq_v):
                g.wait_ge(v_d.h, xq_v)
                s_dbg.dma(g, out=dbgB["ln2"][st, :, :], in_=lnst[:])
                s_dbg.dma(g, out=dbgB["xq"][st, :, :], in_=xq_out[:, b, 0:D])

        @blk.scalar
        def _(s, st=st, b=b, xq_v=xq_v, tg=gateup_t_prev2[b]):
            s.wait_ge(v_d.h, xq_v)
            if tg:
                s.wait_ge(t_d.h, tg)
            for c in range(8):
                s_tp[c].dma(s, out=xqT[:, b, c, :],
                            in_=xq_out[:, b, 128 * c:128 * (c + 1)], transpose=True)
        xq_tp = [(c, s_tp[c].count) for c in range(8)]
        xqT_tp_prev2[b] = xq_tp

        t0 = t_d.count
        a0 = a_d.count
        V0 = v_d.count

        @blk.tensor
        def _(t, st=st, b=b, a0=a0, V0=V0, xq_tp=list(xq_tp), sprev=silu_a_prev,
              hprev=hTT_v_prev):
            for c, val in xq_tp:
                t.wait_ge(s_tp[c].h, val)
            if st == 0:
                s_wq.wait(t, wq_up_ld)
            if sprev:
                t.wait_ge(a_d.h, sprev)
            for gi, (off, w) in enumerate(g_chunks):
                ps = psA if gi % 2 == 0 else psB
                if gi >= 2:
                    t.wait_ge(a_d.h, a0 + gi - 1)
                for kc in range(8):
                    i_ = t.matmul(ps[0:128, 0:w], xqT[:, b, kc, :], wqgT[:, kc, off:off + w],
                                  start=(kc == 0), stop=(kc == 7))
                i_.then_inc(t_d.h, 1)
            if hprev:
                t.wait_ge(v_d.h, hprev)
            for gi, (off, w) in enumerate(g_chunks):
                ps = psC if gi % 2 == 0 else psD
                if gi >= 2:
                    t.wait_ge(v_d.h, V0 + gi - 1)
                for kc in range(8):
                    i_ = t.matmul(ps[0:128, 0:w], xqT[:, b, kc, :], wquT[:, kc, off:off + w],
                                  start=(kc == 0), stop=(kc == 7))
                i_.then_inc(t_d.h, 1)
        t_d.count += 2 * NG
        gateup_t_prev2[b] = t_d.count

        @blk.scalar
        def _(s, st=st, b=b, t0=t0, hqp=hq_v_prev2[b]):
            if hqp:
                s.wait_ge(v_d.h, hqp)
            for gi, (off, w) in enumerate(g_chunks):
                ps = psA if gi % 2 == 0 else psB
                s.wait_ge(t_d.h, t0 + gi + 1)
                s.activation(gsil[:, b, off:off + w], ps[0:128, 0:w],
                             AF.Silu).then_inc(a_d.h, 1)
        a_d.count += NG
        silu_a_prev = a_d.count

        @blk.vector
        def _(v, st=st, b=b, t0=t0, a0=a0):
            for gi, (off, w) in enumerate(g_chunks):
                ps = psC if gi % 2 == 0 else psD
                v.wait_ge(t_d.h, t0 + NG + gi + 1)
                v.wait_ge(a_d.h, a0 + gi + 1)
                v.tensor_tensor(gsil[:, b, off:off + w], ps[0:128, 0:w],
                                gsil[:, b, off:off + w], OP.mult)
                v.engine_nop().then_inc(v_d.h, 1)
        v_d.count += NG
        hTT_v_prev = v_d.count
        h_v = v_d.count

        @blk.scalar
        def _(s, st=st, b=b, h_v=h_v, hqp=hq_v_prev_any):
            s.wait_ge(v_d.h, h_v)
            if hqp:
                s.wait_ge(v_d.h, hqp)
            s.activation(q_cA[:, 0:DI], gsil[:, b, :], AF.Abs, scale=6.0)
            s.activation(q_sg[:, 0:DI], gsil[:, b, :], AF.Sign).then_inc(a_d.h, 1)
        a_d.count += 1
        hq_abs_a = a_d.count

        @blk.vector
        def _(v, st=st, b=b, aa=hq_abs_a, guard=list(hq_tp_prev[:11])):
            v.wait_ge(a_d.h, aa)
            for c, val in guard:
                v.wait_ge(s_tp[c].h, val)
            emit_act_quant(v, gsil[:, b, :], q_cA, q_sg, DI, NB_I, q_out[:, 0:DI],
                           h_sel=[0])
            v.engine_nop().then_inc(v_d.h, 1)
        v_d.count += 1
        hq_v0 = v_d.count

        @blk.vector
        def _(v, st=st, b=b, guard=list(hq_tp_prev[11:])):
            for c, val in guard:
                v.wait_ge(s_tp[c].h, val)
            emit_act_quant(v, gsil[:, b, :], q_cA, q_sg, DI, NB_I, q_out[:, 0:DI],
                           h_sel=[1])
            v.engine_nop().then_inc(v_d.h, 1)
        v_d.count += 1
        hq_v = v_d.count
        hq_v_prev_any = hq_v
        hq_v_prev2[b] = hq_v

        if dump:
            @blk.gpsimd
            def _(g, st=st, b=b, hq_v=hq_v):
                g.wait_ge(v_d.h, hq_v)
                s_dbg.dma(g, out=dbgB["h"][st, :, :], in_=gsil[:, b, :])
                s_dbg.dma(g, out=dbgB["hq"][st, :, :], in_=q_out[:, 0:DI])

        @blk.scalar
        def _(s, st=st, b=b, hq_v0=hq_v0, hq_v=hq_v, dtp=down_t_prev2[b]):
            s.wait_ge(v_d.h, hq_v0)
            if dtp:
                s.wait_ge(t_d.h, dtp)
            for c in range(11):
                s_tp[c].dma(s, out=hqT[:, b, c, :], in_=q_out[:, 128 * c:128 * (c + 1)],
                            transpose=True)
            s.wait_ge(v_d.h, hq_v)
            for c in range(11, 22):
                s_tp[c].dma(s, out=hqT[:, b, c, :], in_=q_out[:, 128 * c:128 * (c + 1)],
                            transpose=True)
        hq_tp = [(c, s_tp[c].count) for c in range(22)]
        hq_tp_prev = hq_tp

        t1 = t_d.count

        @blk.tensor
        def _(t, st=st, b=b, hq_tp=list(hq_tp), ovp=out_v_prev):
            for c, val in hq_tp:
                t.wait_ge(s_tp[c].h, val)
            if st == 0:
                s_uv.wait(t, wq_down_ld)
            if ovp:
                t.wait_ge(v_d.h, ovp)
            for half in range(2):
                ps = psS0 if half == 0 else psS1
                for kc in range(22):
                    i_ = t.matmul(ps[0:128, :], hqT[:, b, kc, :],
                                  wqdT[:, kc, 512 * half:512 * (half + 1)],
                                  start=(kc == 0), stop=(kc == 21))
                i_.then_inc(t_d.h, 1)
        t_d.count += 2
        down_t_prev2[b] = t_d.count

        @blk.vector
        def _(v, st=st, b=b, t1=t1, stp=out_store_prev2[b]):
            if stp:
                v.wait_ge(s_out.h, stp)
            for half in range(2):
                ps = psS0 if half == 0 else psS1
                v.wait_ge(t_d.h, t1 + half + 1)
                v.tensor_tensor(outst[:, b, 512 * half:512 * (half + 1)], ps[0:128, :],
                                r1st[:, b, 512 * half:512 * (half + 1)], OP.add)
            v.drain()
            v.engine_nop().then_inc(v_d.h, 1)
        v_d.count += 1
        out_v = v_d.count
        out_v_prev = out_v
        r1_free[b] = out_v

        @blk.sync
        def _(sync, st=st, b=b, out_v=out_v):
            sync.wait_ge(v_d.h, out_v)
            s_out.dma(sync, out=out_d[128 * st:128 * (st + 1), :], in_=outst[:, b, :])
        out_store_prev2[b] = s_out.count

    if dump:
        dmp2 = {}
        for nm, ten in (("wqgT", wqgT), ("wquT", wquT), ("wqdT", wqdT),
                        ("qout", q_out)):
            shp = [int(d) for d in ten.shape]
            dmp2[nm] = nc.dram_tensor("dumpB_" + nm, shp, BF16, kind="ExternalOutput")

        @blk.sync
        def _(sync):
            s_out.wait(sync)
            for nm, ten in (("wqgT", wqgT), ("wquT", wquT), ("wqdT", wqdT),
                            ("qout", q_out)):
                s_out.dma(sync, out=dmp2[nm][:], in_=ten[:])
            s_out.wait(sync)

    @blk.sync
    def _(sync):
        s_out.wait(sync)
        if dump:
            s_dbg.wait(sync)

    bctxL.close()
    ctx.close()
    return nc


_NC_CACHE = None


def kernel(L0_attn_x, L0_attn_W_DQ, L0_attn_W_UQ, L0_attn_W_QR, L0_attn_W_DKV,
           L0_attn_W_UK, L0_attn_W_UV, L0_attn_W_KR, L0_attn_W_O,
           L0_ffn_w_gate, L0_ffn_w_up, L0_ffn_w_down):
    global _NC_CACHE
    if _NC_CACHE is None:
        _NC_CACHE = build_nc()
    nc = _NC_CACHE

    cs, sn = _rope_tables()
    masks = _causal_masks()
    x = np.ascontiguousarray(np.asarray(L0_attn_x, dtype=np.float32))
    common = {
        "W_DQ": np.ascontiguousarray(np.asarray(L0_attn_W_DQ, np.float32)),
        "W_UQ": np.ascontiguousarray(np.asarray(L0_attn_W_UQ, np.float32)),
        "W_QR": np.ascontiguousarray(np.asarray(L0_attn_W_QR, np.float32)),
        "W_DKV": np.ascontiguousarray(np.asarray(L0_attn_W_DKV, np.float32)),
        "W_UK": np.ascontiguousarray(np.asarray(L0_attn_W_UK, np.float32)),
        "W_UV": np.ascontiguousarray(np.asarray(L0_attn_W_UV, np.float32)),
        "W_KR": np.ascontiguousarray(np.asarray(L0_attn_W_KR, np.float32)),
        "W_O": np.ascontiguousarray(np.asarray(L0_attn_W_O, np.float32)),
        "w_gate": np.ascontiguousarray(np.asarray(L0_ffn_w_gate, np.float32)),
        "w_up": np.ascontiguousarray(np.asarray(L0_ffn_w_up, np.float32)),
        "w_down": np.ascontiguousarray(np.asarray(L0_ffn_w_down, np.float32)),
        "rope_cos": cs, "rope_sin": sn, "masks": masks, "sel": _den_sel(),
    }
    in_maps = [dict(common, x=np.ascontiguousarray(x[c])) for c in range(8)]
    res = run_bass_kernel_spmd(nc, in_maps, core_ids=list(range(8)))
    out = np.stack([np.asarray(res.results[c]["out"], np.float32) for c in range(8)])
    return out


# version 34 (final)
# speedup vs baseline: 1.0891x; 1.0108x over previous
import numpy as np
import ml_dtypes
from contextlib import ExitStack

import concourse.bass as bass
import concourse.mybir as mybir
from concourse.bass_utils import run_bass_kernel_spmd

F32 = mybir.dt.float32
BF16 = mybir.dt.bfloat16
FP16 = mybir.dt.float16
OP = mybir.AluOpType
AF = mybir.ActivationFunctionType
AX = mybir.AxisListType

S, D, NH, DH, DHR, DL, DI = 2048, 1024, 16, 64, 32, 256, 2816
NT = S // 128
QCH = 512
NQ = S // QCH
NB_D = D // 16
NB_I = DI // 16

import math
import ctypes
C_SM = 1.0 / float(np.float32(np.sqrt(np.float32(96.0))))
C_HI = float(np.float32(C_SM))
C_LO = C_SM - C_HI
E_HI = math.exp(80.0 * C_SM)
E_LO = math.exp(-80.0 * C_SM)
LN_EPS = 1e-5

SWAP32 = [1, 0, 3, 2, 5, 4, 7, 6, 9, 8, 11, 10, 13, 12, 15, 14,
          17, 16, 19, 18, 21, 20, 23, 22, 25, 24, 27, 26, 29, 28, 31, 30]


def _rope_tables():
    libm = ctypes.CDLL("libm.so.6")
    libm.cosf.restype = ctypes.c_float
    libm.cosf.argtypes = [ctypes.c_float]
    libm.sinf.restype = ctypes.c_float
    libm.sinf.argtypes = [ctypes.c_float]
    inv = (10000.0 ** (-np.arange(0, DHR, 2, dtype=np.float64) / DHR)).astype(np.float32)
    ang = np.arange(S, dtype=np.float32)[:, None] * inv
    cos = np.array([[libm.cosf(float(a)) for a in row] for row in ang], np.float32)
    sin = np.array([[libm.sinf(float(a)) for a in row] for row in ang], np.float32)
    cs = np.zeros((128, S), np.float32)
    sn = np.zeros((128, S), np.float32)
    for p in range(128):
        c = p % DHR
        i = c // 2
        cs[p] = cos[:, i]
        sn[p] = (-sin[:, i]) if (c % 2 == 0) else sin[:, i]
    return cs, sn


def _den_sel():
    s = np.zeros((16, 8, 128), np.float32)
    for c in range(8):
        s[2 * c, c, 0:64] = 1.0
        s[2 * c + 1, c, 64:128] = 1.0
    return s


def _causal_masks():
    m = np.zeros((128, 4, QCH), np.float32)
    for j in range(4):
        for k in range(128):
            q0 = k + 128 * j
            if q0 < QCH:
                m[k, j, q0:] = 1.0
    return m


class Sem:
    def __init__(self, ctx, nc, name):
        self.h = ctx.enter_context(nc.semaphore(name))
        self.count = 0

    def wait(self, eng, at=None):
        eng.wait_ge(self.h, self.count if at is None else at)


class DStream:
    def __init__(self, ctx, nc, name):
        self.h = ctx.enter_context(nc.semaphore(name))
        self.count = 0

    def dma(self, q, out, in_, **kw):
        if self.count:
            q.wait_ge(self.h, self.count)
        q.dma_start(out=out, in_=in_, **kw).then_inc(self.h, 16)
        self.count += 16
        return self.count

    def wait(self, eng, at=None):
        eng.wait_ge(self.h, self.count if at is None else at)


def build_nc(dump=False, dump_scratch=False):
    nc = bass.Bass("TRN2", target_bir_lowering=False,
                   dynamic_dma_scratch_size=16384,
                   detect_race_conditions=False)
    scratch_kind = "ExternalOutput" if (dump or dump_scratch) else "Internal"
    ctx = ExitStack()

    x_d = nc.dram_tensor("x", [S, D], F32, kind="ExternalInput")
    wdram = {}
    for name, shape in [("W_DQ", [D, DL]), ("W_UQ", [DL, D]), ("W_QR", [DL, NH * DHR]),
                        ("W_DKV", [D, DL]), ("W_UK", [DL, D]), ("W_UV", [DL, D]),
                        ("W_KR", [D, DHR]), ("W_O", [D, D]),
                        ("w_gate", [DI, D]), ("w_up", [DI, D]), ("w_down", [D, DI])]:
        wdram[name] = nc.dram_tensor(name, shape, F32, kind="ExternalInput")
    cs_d = nc.dram_tensor("rope_cos", [128, S], F32, kind="ExternalInput")
    sn_d = nc.dram_tensor("rope_sin", [128, S], F32, kind="ExternalInput")
    mask_d = nc.dram_tensor("masks", [128, 4, QCH], F32, kind="ExternalInput")
    sel_d = nc.dram_tensor("sel", [16, 8, 128], F32, kind="ExternalInput")
    den_dr = nc.dram_tensor("den_scr", [16, S], F32, kind=scratch_kind)
    out_d = nc.dram_tensor("out", [S, D], F32, kind="ExternalOutput")
    r0_dr = nc.dram_tensor("r0_scr", [S, D], BF16, kind=scratch_kind)
    r1_dr = nc.dram_tensor("r1_scr", [S, D], BF16, kind=scratch_kind)
    at_dr = nc.dram_tensor("attn_scr", [128, 8, S], BF16, kind=scratch_kind)
    wqd_dr = nc.dram_tensor("wqd_scr", [128, 22, D], BF16, kind=scratch_kind)
    wqu_dr = nc.dram_tensor("wqu_scr", [128, 8, DI], BF16, kind=scratch_kind)
    wqg_dr = nc.dram_tensor("wqg_scr", [128, 8, DI], BF16, kind=scratch_kind)

    t_d = Sem(ctx, nc, "t_d")
    v_d = Sem(ctx, nc, "v_d")
    a_d = Sem(ctx, nc, "a_d")
    g_c = Sem(ctx, nc, "g_c")

    s_tab = DStream(ctx, nc, "s_tab")
    s_x = [DStream(ctx, nc, f"s_x{i}") for i in range(2)]
    s_a4r = [DStream(ctx, nc, f"s_a4r{i}") for i in range(2)]
    s_a4a = [DStream(ctx, nc, f"s_a4a{i}") for i in range(2)]
    s_w = [DStream(ctx, nc, f"s_w{i}") for i in range(2)]
    s_r1 = [DStream(ctx, nc, f"s_r1{i}") for i in range(2)]
    s_wl = DStream(ctx, nc, "s_wl")
    s_uv = DStream(ctx, nc, "s_uv")
    s_kh = DStream(ctx, nc, "s_kh")
    s_qh = [DStream(ctx, nc, f"s_qh{i}") for i in range(2)]
    s_r0s = [DStream(ctx, nc, f"s_r0s{i}") for i in range(2)]
    s_ats = DStream(ctx, nc, "s_ats")
    s_den = DStream(ctx, nc, "s_den")
    s_dw = DStream(ctx, nc, "s_dw")
    s_wq = DStream(ctx, nc, "s_wq")
    s_r1s = [DStream(ctx, nc, f"s_r1s{i}") for i in range(2)]
    s_tp = [DStream(ctx, nc, f"s_tp{i}") for i in range(22)]
    s_out = DStream(ctx, nc, "s_out")
    s_dbg = DStream(ctx, nc, "s_dbg")

    def AL(name, shape, dtype):
        return ctx.enter_context(nc.sbuf_tensor(name, shape, dtype, side="left"))

    stat = AL("stat", [128, 16], F32)
    sqsc = AL("sqsc", [128, 512], F32)
    lnst = AL("lnst", [128, D], BF16)
    ones1 = AL("ones1", [1, 64], F32)
    xst = AL("xst", [128, 2, D], F32)
    r0st = AL("r0st", [128, 2, D], BF16)

    blk = ctx.enter_context(nc.Block())

    actxL = ExitStack()
    def AAL(name, shape, dtype):
        return actxL.enter_context(nc.sbuf_tensor(name, shape, dtype, side="left"))
    cqT = AAL("cqT", [128, 2, S], BF16)
    ckvT = AAL("ckvT", [128, 2, S], BF16)
    Vb = AAL("Vb", [128, NT, NH * 65], BF16)
    QrT = AAL("QrT", [128, 4, S], BF16)
    KrT = AAL("KrT", [32, S], BF16)
    mask_s = AAL("mask", [128, 4, QCH], BF16)
    wb_UQ = AAL("wb_UQ", [128, 2, D], BF16)
    wb_UK = AAL("wb_UK", [128, 2, D], BF16)
    wb_QR = AAL("wb_QR", [128, 2, NH * DHR], BF16)
    wb_WO = AAL("wb_WO", [128, 8, D], BF16)
    rowr = AAL("rowr", [1, 2, QCH], F32)
    sel_s = AAL("sel_s", [16, 8, 128], F32)

    actxR = ExitStack()
    def AAR(name, shape, dtype):
        return actxR.enter_context(nc.sbuf_tensor(name, shape, dtype, side="right"))
    wb_DQ = AAR("wb_DQ", [128, 8, DL], BF16)
    wb_DKV = AAR("wb_DKV", [128, 8, DL], BF16)
    wb_KR = AAR("wb_KR", [128, 8, DHR], BF16)
    ln1T = AAR("ln1T", [128, 8, S], BF16)
    cs_s = AAR("cs", [128, S], F32)
    sn_s = AAR("sn", [128, S], F32)
    ropeA = AAR("ropeA", [128, S], F32)
    ropeB = AAR("ropeB", [128, S], F32)
    ropeC = AAR("ropeC", [128, S], F32)

    P = nc.alloc_psum_tensor
    psA = P("psA", [128, 512], F32)
    psB = P("psB", [128, 512], F32)
    psC = P("psC", [128, 512], F32)
    psD = P("psD", [128, 512], F32)
    psS0 = P("psS0", [128, QCH], F32)
    psS1 = P("psS1", [128, QCH], F32)
    psAV2 = [P("psAV0", [65, QCH], F32), P("psAV1", [65, QCH], F32)]

    @blk.sync
    def _(sync):
        s_tab.dma(sync, out=cs_s[:], in_=cs_d[:])
        s_tab.dma(sync, out=sn_s[:], in_=sn_d[:])

    @blk.gpsimd
    def _(g):
        for dst, name in ((wb_DQ, "W_DQ"), (wb_DKV, "W_DKV"), (wb_KR, "W_KR"),
                          (wb_UQ, "W_UQ"), (wb_QR, "W_QR"), (wb_UK, "W_UK"),
                          (wb_WO, "W_O")):
            s_wl.dma(g, out=dst[:],
                     in_=wdram[name][:].rearrange("(a p) l -> p a l", p=128))
        s_wl.dma(g, out=mask_s[:], in_=mask_d[:])
        s_wl.dma(g, out=sel_s[:], in_=sel_d[:])
        g.memset(ones1[:], 1.0).then_inc(g_c.h, 1)
        g.memset(Vb[:].rearrange("p t (h c) -> p t h c", c=65)[:, :, :, 64:65],
                 1.0).then_inc(g_c.h, 1)
    g_c.count += 2

    @blk.vector
    def _(v):
        s_wl.wait(v)
        for wt, ncol in ((wb_UK, 2 * D), (wb_KR, 8 * DHR)):
            v.tensor_scalar(ropeA[:, 0:ncol], wt[:].rearrange("p a l -> p (a l)"),
                            C_HI, None, OP.mult)
            v.drain()
            v.scalar_tensor_tensor(wt[:].rearrange("p a l -> p (a l)"),
                                   wt[:].rearrange("p a l -> p (a l)"), C_LO,
                                   ropeA[:, 0:ncol], OP.mult, OP.add)
            v.drain()
        v.engine_nop().then_inc(v_d.h, 1)
    v_d.count += 1
    kw_scaled_v = v_d.count

    xst_free_v = [0, 0]
    lnst_tp_prev = [s.count for s in s_tp[:8]]

    ln1_tp_tiles = []
    for st in range(NT):
        b = st % 2

        @blk.sync
        def _(sync, st=st, b=b, w=xst_free_v[b]):
            sync.wait_ge(v_d.h, w)
            s_x[b].dma(sync, out=xst[:, b, :], in_=x_d[128 * st:128 * (st + 1), :])
        x_in = s_x[b].count

        @blk.vector
        def _(v, st=st, b=b, x_in=x_in):
            s_x[b].wait(v, x_in)
            v.reduce_sum(stat[:, 0:1], xst[:, b, :], AX.X)
            v.tensor_tensor(sqsc[:], xst[:, b, 0:512], xst[:, b, 0:512], OP.mult)
            v.reduce_sum(stat[:, 1:2], sqsc[:], AX.X)
            v.tensor_tensor(sqsc[:], xst[:, b, 512:1024], xst[:, b, 512:1024], OP.mult)
            v.reduce_sum(stat[:, 5:6], sqsc[:], AX.X)
            v.drain()
            v.tensor_tensor(stat[:, 1:2], stat[:, 1:2], stat[:, 5:6], OP.add)
            v.tensor_scalar(stat[:, 0:1], stat[:, 0:1], 1.0 / D, None, OP.mult)
            v.tensor_scalar(stat[:, 1:2], stat[:, 1:2], 1.0 / D, None, OP.mult)
            v.scalar_tensor_tensor(stat[:, 2:3], stat[:, 0:1], 1.0, stat[:, 0:1],
                                   OP.mult, OP.mult)
            v.tensor_tensor(stat[:, 1:2], stat[:, 1:2], stat[:, 2:3], OP.subtract)
            v.tensor_scalar(stat[:, 1:2], stat[:, 1:2], LN_EPS, None, OP.add)
            v.engine_nop().then_inc(v_d.h, 1)
        v_d.count += 1

        @blk.scalar
        def _(s, st=st):
            v_d.wait(s)
            s.activation(stat[:, 3:4], stat[:, 1:2], AF.Sqrt).then_inc(a_d.h, 1)
        a_d.count += 1

        @blk.vector
        def _(v, st=st, b=b, tp=list(lnst_tp_prev), r0f=s_r0s[b].count):
            a_d.wait(v)
            for f in range(8):
                v.wait_ge(s_tp[f].h, tp[f])
            v.wait_ge(s_r0s[b].h, r0f)
            v.reciprocal(stat[:, 4:5], stat[:, 3:4])
            v.tensor_scalar(lnst[:], xst[:, b, :], stat[:, 0:1], stat[:, 4:5],
                            OP.subtract, OP.mult)
            v.tensor_copy(r0st[:, b, :], xst[:, b, :])
            v.drain()
            v.engine_nop().then_inc(v_d.h, 1)
        v_d.count += 1
        xst_free_v[b] = v_d.count

        @blk.gpsimd
        def _(g, st=st, b=b):
            v_d.wait(g)
            s_r0s[b].dma(g, out=r0_dr[128 * st:128 * (st + 1), :], in_=r0st[:, b, :])

        @blk.sync
        def _(sync, st=st):
            v_d.wait(sync)
            for f in range(8):
                s_tp[f].dma(sync, out=ln1T[:, f, 128 * st:128 * (st + 1)],
                            in_=lnst[:, 128 * f:128 * (f + 1)], transpose=True)
        lnst_tp_prev = [s_tp[f].count for f in range(8)]
        ln1_tp_tiles.append(list(lnst_tp_prev))

    class MMPipe:
        def __init__(self):
            self.jobs = []

        def add(self, mm_fn, evac_fn):
            self.jobs.append((mm_fn, evac_fn))

        def emit(self, pre_tensor_wait=None, banks=2):
            n0_t = t_d.count
            n0_a = a_d.count
            jobs = self.jobs

            @blk.tensor
            def _(t):
                if pre_tensor_wait is not None:
                    pre_tensor_wait(t)
                for i, (mm_fn, _e) in enumerate(jobs):
                    t.wait_ge(a_d.h, max(n0_a, n0_a + i - banks + 1))
                    mm_fn(t).then_inc(t_d.h, 1)

            @blk.scalar
            def _(s):
                for i, (_m, evac_fn) in enumerate(jobs):
                    s.wait_ge(t_d.h, n0_t + i + 1)
                    evac_fn(s).then_inc(a_d.h, 1)

            t_d.count += len(jobs)
            a_d.count += len(jobs)

    ln1_tp = [s_tp[f].count for f in range(8)]

    def wait_ln1(t):
        s_wl.wait(t)
        s_tab.wait(t)

    pipe = MMPipe()
    for (wten, dst) in ((wb_DQ, cqT), (wb_DKV, ckvT)):
        for l in range(2):
            for sc in range(4):
                ps = psA if (sc % 2 == 0) else psB
                def mm(t, wten=wten, l=l, sc=sc, ps=ps,
                       tp=list(ln1_tp_tiles[4 * sc + 3])):
                    for f in range(8):
                        t.wait_ge(s_tp[f].h, tp[f])
                    for f in range(8):
                        i_ = t.matmul(ps[0:128, :], wten[:, f, 128 * l:128 * (l + 1)],
                                      ln1T[:, f, 512 * sc:512 * (sc + 1)],
                                      start=(f == 0), stop=(f == 7))
                    return i_
                def ev(s, dst=dst, l=l, sc=sc, ps=ps):
                    return s.activation(dst[:, l, 512 * sc:512 * (sc + 1)],
                                        ps[0:128, :], AF.Copy)
                pipe.add(mm, ev)
    pipe.emit(pre_tensor_wait=wait_ln1)

    pipe = MMPipe()
    for sc in range(4):
        ps = psA if (sc % 2 == 0) else psB
        def mm(t, sc=sc, ps=ps):
            for f in range(8):
                i_ = t.matmul(ps[0:32, :], wb_KR[:, f, :],
                              ln1T[:, f, 512 * sc:512 * (sc + 1)],
                              start=(f == 0), stop=(f == 7))
            return i_
        def ev(s, sc=sc, ps=ps):
            return s.activation(ropeA[0:32, 512 * sc:512 * (sc + 1)], ps[0:32, :], AF.Copy)
        pipe.add(mm, ev)

    def wait_kscale(t):
        t.wait_ge(v_d.h, kw_scaled_v)
    pipe.emit(pre_tensor_wait=wait_kscale)

    @blk.vector
    def _(v):
        a_d.wait(v)
        v.stream_shuffle(ropeB[0:32, :], ropeA[0:32, :], SWAP32)
        v.tensor_tensor(ropeC[0:32, :], ropeA[0:32, :], cs_s[0:32, :], OP.mult)
        v.tensor_tensor(ropeB[0:32, :], ropeB[0:32, :], sn_s[0:32, :], OP.mult)
        v.tensor_tensor(ropeC[0:32, :], ropeC[0:32, :], ropeB[0:32, :], OP.add)
        v.tensor_copy(KrT[:], ropeC[0:32, :])
        v.engine_nop().then_inc(v_d.h, 1)
    v_d.count += 1

    for qt in range(4):
        rope_free = v_d.count
        pipe = MMPipe()
        for sc in range(4):
            ps = psA if (sc % 2 == 0) else psB
            def mm(t, qt=qt, sc=sc, ps=ps, rope_free=rope_free):
                if sc == 0:
                    t.wait_ge(v_d.h, rope_free)
                for l in range(2):
                    i_ = t.matmul(ps[0:128, :], wb_QR[:, l, 128 * qt:128 * (qt + 1)],
                                  cqT[:, l, 512 * sc:512 * (sc + 1)],
                                  start=(l == 0), stop=(l == 1))
                return i_
            def ev(s, sc=sc, ps=ps):
                return s.activation(ropeA[:, 512 * sc:512 * (sc + 1)], ps[0:128, :], AF.Copy)
            pipe.add(mm, ev)
        pipe.emit()

        @blk.vector
        def _(v, qt=qt):
            a_d.wait(v)
            v.stream_shuffle(ropeB[:], ropeA[:], SWAP32)
            v.tensor_tensor(ropeC[:], ropeA[:], cs_s[:], OP.mult)
            v.tensor_tensor(ropeB[:], ropeB[:], sn_s[:], OP.mult)
            v.tensor_tensor(ropeC[:], ropeC[:], ropeB[:], OP.add)
            v.tensor_copy(QrT[:, qt, :], ropeC[:])
            v.engine_nop().then_inc(v_d.h, 1)
        v_d.count += 1
    qr_done_v = v_d.count

    wb_UQv = AAR("wb_UV", [128, 2, D], BF16)
    pipe = MMPipe()
    for st in range(NT):
        for half in range(2):
            ps = psC if (half == 0) else psD
            def mm(t, st=st, half=half, ps=ps):
                for l in range(2):
                    i_ = t.matmul(ps[0:128, :], ckvT[:, l, 128 * st:128 * (st + 1)],
                                  wb_UQv[:, l, 512 * half:512 * (half + 1)],
                                  start=(l == 0), stop=(l == 1))
                return i_
            def ev(s, st=st, half=half, ps=ps):
                dst = Vb[:, st, :].rearrange("p (h c) -> p h c", c=65)[:, 8 * half:8 * (half + 1), 0:64]
                return s.activation(dst, ps[0:128, :].rearrange("p (h c) -> p h c", c=64),
                                    AF.Copy)
            pipe.add(mm, ev)

    @blk.gpsimd
    def _(g):
        s_uv.dma(g, out=wb_UQv[:],
                 in_=wdram["W_UV"][:].rearrange("(a p) l -> p a l", p=128))

    def wait_uv(t):
        s_uv.wait(t)
        g_c.wait(t)
    pipe.emit(pre_tensor_wait=wait_uv)

    actxR.close()
    Qh = AAL("Qh", [128, S], BF16)
    Kh = AAL("Kh", [128, S], BF16)
    Eb = AAL("Eb", [128, NT, QCH], BF16)
    attnSt = AAL("attnSt", [64, 2, QCH], BF16)
    attnTs = AAL("attnTs", [128, 2, 8, 128], BF16)
    r1stA = AAL("r1stA", [128, 2, D], BF16)
    den_s = AAL("den_s", [16, 2, 128], F32)
    denb = AAL("denb", [128, 4, 128], BF16)

    HDI = DI // 2
    wraw2 = AAL("wraw2", [128, 2, HDI], F32)
    qw_gm = AAL("qw_gm", [128, 128], F32)
    qw_s = AAL("qw_s", [128, 128], F32)
    qw_s2 = AAL("qw_s2", [128, 128], F32)
    qw_r = AAL("qw_r", [128, 128], F32)
    qw_s16 = AAL("qw_s16", [128, 128], FP16)
    qw_an = AAL("qw_an", [128, HDI], FP16)
    qw_acc = AAL("qw_acc", [128, HDI], FP16)
    qw_cA = AAL("qw_cA", [128, HDI], FP16)
    qw_sg = AAL("qw_sg", [128, HDI], FP16)
    qw_out = AAL("qw_out", [128, HDI], BF16)
    qfh_an = AAL("qfh_an", [128, HDI], F32)
    qfh_acc = AAL("qfh_acc", [128, HDI], F32)
    qfh_cA = AAL("qfh_cA", [128, HDI], F32)
    dstTs = AAL("dstTs", [128, 11, 128], BF16)

    def q_chain(v, an_ap, acc_ap, cA_ap, out_ap, n):
        v.tensor_scalar(acc_ap, an_ap, 5.0, 2.0, OP.is_gt, OP.mult)
        v.scalar_tensor_tensor(acc_ap, an_ap, 2.5, acc_ap, OP.is_gt, OP.add)
        v.scalar_tensor_tensor(acc_ap, an_ap, 3.5, acc_ap, OP.is_gt, OP.add)
        v.tensor_scalar(cA_ap, an_ap, 0.25, None, OP.is_gt)
        v.scalar_tensor_tensor(cA_ap, an_ap, 0.75, cA_ap, OP.is_gt, OP.add)
        v.scalar_tensor_tensor(cA_ap, an_ap, 1.25, cA_ap, OP.is_gt, OP.add)
        v.scalar_tensor_tensor(cA_ap, an_ap, 1.75, cA_ap, OP.is_gt, OP.add)
        v.scalar_tensor_tensor(out_ap, cA_ap, 0.5, acc_ap, OP.mult, OP.add)

    wq_tp_guard = [[]]
    qw_load_guard = [0, 0]
    qw_sbuf_guard = [0]

    def make_wq_unit(src_dram, row0, col0, n_cols, chain, dst_dr, c0, nchunks, wt_col):
        def emit(u):
            p = u % 2

            @blk.gpsimd
            def _(g, p=p, vq=qw_load_guard[p]):
                if vq:
                    g.wait_ge(v_d.h, vq)
                s_w[p].dma(g, out=wraw2[:, p, 0:n_cols],
                           in_=src_dram[row0:row0 + 128, col0:col0 + n_cols])
            w_in = s_w[p].count

            abuf = qw_cA if chain == "fp16" else qfh_an

            @blk.scalar
            def _(s, p=p, w_in=w_in, vq=qw_sbuf_guard[0], abuf=abuf):
                s_w[p].wait(s, w_in)
                if vq:
                    s.wait_ge(v_d.h, vq)
                s.activation(abuf[:, 0:n_cols], wraw2[:, p, 0:n_cols], AF.Abs,
                             scale=6.0)
                s.activation(qw_sg[:, 0:n_cols], wraw2[:, p, 0:n_cols],
                             AF.Sign).then_inc(a_d.h, 1)
            a_d.count += 1

            @blk.vector
            def _(v, p=p, aa=a_d.count, guard=list(wq_tp_guard[0]), nb=n_cols // 16):
                v.wait_ge(a_d.h, aa)
                for c, val in guard:
                    v.wait_ge(s_tp[c].h, val)
                v.tensor_reduce(qw_gm[:, 0:nb],
                                wraw2[:, p, 0:n_cols].rearrange("p (c k) -> p c k", k=16),
                                AX.X, OP.max, apply_absolute_value=True)
                v.tensor_scalar(qw_s[:, 0:nb], qw_gm[:, 0:nb], 6e-12, None, OP.max)
                v.tensor_scalar(qw_s2[:, 0:nb], qw_gm[:, 0:nb], 0.16666666666666666,
                                1e-12, OP.mult, OP.max)
                v.drain()
                v.reciprocal(qw_r[:, 0:nb], qw_s[:, 0:nb])
                v.tensor_copy(qw_s16[:, 0:nb], qw_s2[:, 0:nb])
                v.drain()
                if chain == "fp16":
                    v.tensor_tensor(qw_an[:, 0:n_cols].rearrange("p (c k) -> p c k", k=16),
                                    qw_cA[:, 0:n_cols].rearrange("p (c k) -> p c k", k=16),
                                    qw_r[:, 0:nb].unsqueeze(2).broadcast_to([128, nb, 16]),
                                    OP.mult)
                    q_chain(v, qw_an[:, 0:n_cols], qw_acc[:, 0:n_cols],
                            qw_cA[:, 0:n_cols], qw_acc[:, 0:n_cols], n_cols)
                    v.tensor_tensor(qw_an[:, 0:n_cols], qw_acc[:, 0:n_cols],
                                    qw_sg[:, 0:n_cols], OP.mult)
                    v.tensor_tensor(qw_out[:, 0:n_cols].rearrange("p (c k) -> p c k", k=16),
                                    qw_an[:, 0:n_cols].rearrange("p (c k) -> p c k", k=16),
                                    qw_s16[:, 0:nb].unsqueeze(2).broadcast_to([128, nb, 16]),
                                    OP.mult)
                else:
                    v.tensor_tensor(qfh_an[:, 0:n_cols].rearrange("p (c k) -> p c k", k=16),
                                    qfh_an[:, 0:n_cols].rearrange("p (c k) -> p c k", k=16),
                                    qw_r[:, 0:nb].unsqueeze(2).broadcast_to([128, nb, 16]),
                                    OP.mult)
                    q_chain(v, qfh_an[:, 0:n_cols], qfh_acc[:, 0:n_cols],
                            qfh_cA[:, 0:n_cols], qfh_an[:, 0:n_cols], n_cols)
                    v.tensor_tensor(qfh_acc[:, 0:n_cols], qfh_an[:, 0:n_cols],
                                    qw_sg[:, 0:n_cols], OP.mult)
                    v.tensor_tensor(qw_out[:, 0:n_cols].rearrange("p (c k) -> p c k", k=16),
                                    qfh_acc[:, 0:n_cols].rearrange("p (c k) -> p c k", k=16),
                                    qw_s16[:, 0:nb].unsqueeze(2).broadcast_to([128, nb, 16]),
                                    OP.mult)
                v.drain()
                v.engine_nop().then_inc(v_d.h, 1)
            v_d.count += 1
            qw_load_guard[p] = v_d.count
            qw_sbuf_guard[0] = v_d.count

            @blk.sync
            def _(sync, vv=v_d.count, dw=s_dw.count):
                sync.wait_ge(v_d.h, vv)
                sync.wait_ge(s_dw.h, dw)
                for ci in range(nchunks):
                    s_tp[ci].dma(sync, out=dstTs[:, ci, :],
                                 in_=qw_out[:, 128 * ci:128 * (ci + 1)], transpose=True)
            tpc = [(ci, s_tp[ci].count) for ci in range(nchunks)]
            wq_tp_guard[0] = tpc

            @blk.gpsimd
            def _(g, tpc=list(tpc)):
                for ci, val in tpc:
                    g.wait_ge(s_tp[ci].h, val)
                s_dw.dma(g, out=dst_dr[:, c0:c0 + nchunks,
                                       128 * wt_col:128 * (wt_col + 1)],
                         in_=dstTs[:, 0:nchunks, :])
        return emit

    wq_units = []
    for wt in range(22):
        wq_units.append(make_wq_unit(wdram["w_gate"], 128 * wt, 0, D, "fp16",
                                     wqg_dr, 0, 8, wt))
    for wt in range(22):
        wq_units.append(make_wq_unit(wdram["w_up"], 128 * wt, 0, D, "fp16",
                                     wqu_dr, 0, 8, wt))
    for wt in range(8):
        for half in range(2):
            wq_units.append(make_wq_unit(wdram["w_down"], 128 * wt, HDI * half,
                                         HDI, "fp32", wqd_dr, 11 * half, 11, wt))
    wq_done_marks = {}
    wq_unit_idx = [0]

    def pull_wq_unit():
        i = wq_unit_idx[0]
        if i < len(wq_units):
            wq_units[i](i)
            wq_unit_idx[0] = i + 1
            if i + 1 == 22:
                wq_done_marks["gate"] = s_dw.count
            elif i + 1 == 44:
                wq_done_marks["up"] = s_dw.count
            elif i + 1 == 60:
                wq_done_marks["down"] = s_dw.count

    t_after_V = t_d.count

    @blk.gpsimd
    def _(g):
        g.wait_ge(v_d.h, qr_done_v)
        g.wait_ge(t_d.h, t_after_V)
        s_kh.dma(g, out=Kh[64:96, :], in_=KrT[:])

    ats_prev2 = [s_ats.count, s_ats.count]
    den_prev2 = [s_den.count, s_den.count]
    av_tail_a = [0, 0]
    av_tail_v = [0, 0]
    for h in range(NH):
        pipe = MMPipe()
        for (wten, dst, prow) in ((wb_UQ, Qh, 0), (wb_UK, Kh, 64)):
            for sc in range(4):
                ps = psC if (sc % 2 == 0) else psD
                def mm(t, wten=wten, sc=sc, ps=ps, prow=prow, h=h):
                    src = cqT if wten is wb_UQ else ckvT
                    for l in range(2):
                        i_ = t.matmul(ps[prow:prow + 64, :], wten[:, l, 64 * h:64 * (h + 1)],
                                      src[:, l, 512 * sc:512 * (sc + 1)],
                                      start=(l == 0), stop=(l == 1))
                    return i_
                def ev(s, wten=wten, dst=dst, sc=sc, ps=ps, prow=prow):
                    return s.activation(dst[0:64, 512 * sc:512 * (sc + 1)],
                                        ps[prow:prow + 64, :], AF.Copy)
                pipe.add(mm, ev)

        def pre(t, h=h):
            if h == 0:
                t.wait_ge(v_d.h, qr_done_v)
                s_kh.wait(t)
        pipe.emit(pre_tensor_wait=pre)

        @blk.gpsimd
        def _(g, h=h):
            a_d.wait(g)
            qtile, qoff = divmod(32 * h, 128)
            s_qh[h % 2].dma(g, out=Qh[64:96, :], in_=QrT[qoff:qoff + 32, qtile, :])
        qh_in = s_qh[h % 2].count

        a_qk = a_d.count

        for qi in range(NQ):
            nkt = 4 * qi + 4
            pa = (4 * h + qi) % 2
            psAVc = psAV2[pa]
            t0 = t_d.count
            a0 = a_d.count
            vv0 = v_d.count

            @blk.tensor
            def _(t, h=h, qi=qi, nkt=nkt, a0=a0, vv0=vv0, a_qk=a_qk, qh_in=qh_in,
                  psAVc=psAVc, ta=av_tail_a[pa], tv=av_tail_v[pa]):
                if qi == 0:
                    t.wait_ge(a_d.h, a_qk)
                    s_qh[h % 2].wait(t, qh_in)
                if ta:
                    t.wait_ge(a_d.h, ta)
                    t.wait_ge(v_d.h, tv)
                for kt in range(nkt):
                    ps = psS0 if (kt % 2 == 0) else psS1
                    if kt >= 2:
                        t.wait_ge(a_d.h, a0 + kt - 1)
                    t.matmul(ps[0:128, :], Kh[0:96, 128 * kt:128 * (kt + 1)],
                             Qh[0:96, QCH * qi:QCH * (qi + 1)],
                             start=True, stop=True).then_inc(t_d.h, 1)
                    if kt >= 1:
                        t.wait_ge(v_d.h, vv0 + kt)
                        t.matmul(psAVc[:, :], Vb[:, kt - 1, 65 * h:65 * (h + 1)],
                                 Eb[:, kt - 1, :], start=(kt - 1 == 0), stop=False)
                t.wait_ge(v_d.h, vv0 + nkt)
                t.matmul(psAVc[:, :], Vb[:, nkt - 1, 65 * h:65 * (h + 1)],
                         Eb[:, nkt - 1, :], start=False, stop=True).then_inc(t_d.h, 1)
            t_d.count += nkt + 1
            t_av_done = t_d.count

            @blk.scalar
            def _(s, nkt=nkt, t0=t0):
                for kt in range(nkt):
                    ps = psS0 if (kt % 2 == 0) else psS1
                    s.wait_ge(t_d.h, t0 + kt + 1)
                    s.activation(Eb[:, kt, :], ps[0:128, :],
                                 AF.Exp).then_inc(a_d.h, 1)
            a_d.count += nkt

            @blk.vector
            def _(v, qi=qi, nkt=nkt, a0=a0):
                for kt in range(nkt):
                    v.wait_ge(a_d.h, a0 + kt + 1)
                    v.tensor_scalar(Eb[:, kt, :], Eb[:, kt, :], E_LO, E_HI, OP.max, OP.min)
                    if kt >= 4 * qi:
                        v.tensor_tensor(Eb[:, kt, :], Eb[:, kt, :],
                                        mask_s[:, kt - 4 * qi, :], OP.mult)
                    v.engine_nop().then_inc(v_d.h, 1)
            v_d.count += nkt

            @blk.vector
            def _(v, pa=pa, t_av_done=t_av_done, psAVc=psAVc, dp=den_prev2[pa]):
                v.wait_ge(t_d.h, t_av_done)
                v.wait_ge(s_den.h, dp)
                v.reciprocal(rowr[:, pa, :], psAVc[64:65, :])
                v.drain()
                v.engine_nop().then_inc(v_d.h, 1)
            v_d.count += 1
            av_tail_v[pa] = v_d.count

            @blk.scalar
            def _(s, pa=pa, t_av_done=t_av_done, psAVc=psAVc, atp=ats_prev2[pa]):
                s.wait_ge(t_d.h, t_av_done)
                s.wait_ge(s_ats.h, atp)
                s.activation(attnSt[:, pa, :], psAVc[0:64, :],
                             AF.Copy).then_inc(a_d.h, 1)
            a_d.count += 1
            av_tail_a[pa] = a_d.count

            @blk.gpsimd
            def _(g, h=h, qi=qi, pa=pa, ea=av_tail_a[pa], ev=av_tail_v[pa]):
                g.wait_ge(a_d.h, ea)
                g.wait_ge(v_d.h, ev)
                s_ats.dma(g, out=at_dr[64 * (h % 2):64 * (h % 2) + 64, h // 2,
                                       QCH * qi:QCH * (qi + 1)],
                          in_=attnSt[:, pa, :])
                s_den.dma(g, out=den_dr[h:h + 1, QCH * qi:QCH * (qi + 1)],
                          in_=rowr[:, pa, :])
            ats_prev2[pa] = s_ats.count
            den_prev2[pa] = s_den.count
            pull_wq_unit()

    while wq_unit_idx[0] < len(wq_units):
        pull_wq_unit()

    attn_done_v = v_d.count

    a4_t_prev = [0, 0]
    a4_res_prev = [0, 0]
    a4_tt_prev = 0
    for st in range(NT):
        b4 = st % 2

        @blk.sync
        def _(sync, st=st, b4=b4, tfree=a4_t_prev[b4], rfree=a4_res_prev[b4]):
            if st == 0:
                for i in range(2):
                    s_r0s[i].wait(sync)
                s_ats.wait(sync)
                s_den.wait(sync)
            if rfree:
                sync.wait_ge(v_d.h, rfree)
            sync.wait_ge(t_d.h, tfree)
            s_a4r[b4].dma(sync, out=r0st[:, b4, :],
                          in_=r0_dr[128 * st:128 * (st + 1), :])
            s_a4a[b4].dma(sync, out=attnTs[:, b4, :, :],
                          in_=at_dr[:, :, 128 * st:128 * (st + 1)])
            s_a4a[b4].dma(sync, out=den_s[:, b4, :],
                          in_=den_dr[:, 128 * st:128 * (st + 1)])
        r0_in = s_a4r[b4].count
        at_in = s_a4a[b4].count

        td0 = t_d.count
        vprev = v_d.count

        v0 = v_d.count

        @blk.tensor
        def _(t, st=st, b4=b4, at_in=at_in, ttp=a4_tt_prev, v0=v0):
            if st == 0:
                t.wait_ge(v_d.h, attn_done_v)
            s_a4a[b4].wait(t, at_in)
            if ttp:
                t.wait_ge(v_d.h, ttp)
            for c in range(8):
                ps = psS0 if c % 2 == 0 else psS1
                sl = (c % 4) // 2
                if c >= 2:
                    t.wait_ge(v_d.h, v0 + c - 1)
                t.matmul(ps[:, 128 * sl:128 * (sl + 1)], sel_s[:, c, :],
                         den_s[:, b4, :], start=True, stop=True).then_inc(t_d.h, 1)
        t_d.count += 8

        @blk.vector
        def _(v, st=st, b4=b4, td0=td0):
            for c in range(8):
                ps = psS0 if c % 2 == 0 else psS1
                sl = (c % 4) // 2
                v.wait_ge(t_d.h, td0 + c + 1)
                v.tensor_copy(denb[:, c % 4, :], ps[:, 128 * sl:128 * (sl + 1)])
                v.tensor_tensor(attnTs[:, b4, c, :], attnTs[:, b4, c, :],
                                denb[:, c % 4, :], OP.mult)
                v.engine_nop().then_inc(v_d.h, 1)
        v_d.count += 8
        a4_tt_prev = v_d.count

        t0 = t_d.count

        @blk.tensor
        def _(t, st=st, b4=b4, vprev=vprev, tte=a4_tt_prev):
            t.wait_ge(v_d.h, tte)
            for half in range(2):
                ps = psC if half == 0 else psB
                for c in range(8):
                    i_ = t.matmul(ps[0:128, :], attnTs[:, b4, c, :],
                                  wb_WO[:, c, 512 * half:512 * (half + 1)],
                                  start=(c == 0), stop=(c == 7))
                i_.then_inc(t_d.h, 1)
        t_d.count += 2
        a4_t_prev[b4] = t_d.count

        @blk.vector
        def _(v, st=st, b4=b4, t0=t0, r0_in=r0_in, r1f=s_r1s[b4].count):
            s_a4r[b4].wait(v, r0_in)
            v.wait_ge(s_r1s[b4].h, r1f)
            for half in range(2):
                ps = psC if half == 0 else psB
                v.wait_ge(t_d.h, t0 + half + 1)
                v.tensor_tensor(r1stA[:, b4, 512 * half:512 * (half + 1)], ps[0:128, :],
                                r0st[:, b4, 512 * half:512 * (half + 1)], OP.add)
                v.drain()
                v.engine_nop().then_inc(v_d.h, 1)
        v_d.count += 2
        a4_res_prev[b4] = v_d.count

        @blk.gpsimd
        def _(g, st=st, b4=b4):
            v_d.wait(g)
            s_r1s[b4].dma(g, out=r1_dr[128 * st:128 * (st + 1), :],
                          in_=r1stA[:, b4, :])

    if dump:
        dmp = {}
        for nm, ten in (("Qh", Qh), ("Kh", Kh), ("Eb", Eb), ("Vb", Vb),
                        ("QrT", QrT), ("KrT", KrT), ("cqT", cqT), ("ckvT", ckvT)):
            shp = [int(d) for d in ten.shape]
            dmp[nm] = nc.dram_tensor("dump_" + nm, shp, BF16, kind="ExternalOutput")

        @blk.sync
        def _(sync):
            v_d.wait(sync)
            a_d.wait(sync)
            for nm, ten in (("Qh", Qh), ("Kh", Kh), ("Eb", Eb), ("Vb", Vb),
                            ("QrT", QrT), ("KrT", KrT), ("cqT", cqT), ("ckvT", ckvT)):
                s_out.dma(sync, out=dmp[nm][:], in_=ten[:])
            s_out.wait(sync)

    actxL.close()

    bctxL = ExitStack()
    def BL(name, shape, dtype):
        return bctxL.enter_context(nc.sbuf_tensor(name, shape, dtype, side="left"))

    wqdT = BL("wqdT", [128, 22, D], BF16)
    q_gm = BL("q_gm", [128, NB_I], F32)
    q_s = BL("q_s", [128, NB_I], F32)
    q_r = BL("q_r", [128, NB_I], F32)
    q_s2 = BL("q_s2", [128, NB_I], F32)
    q_s16 = BL("q_s16", [128, NB_I], FP16)
    q_an = BL("q_an", [128, HDI], FP16)
    q_acc = BL("q_acc", [128, HDI], FP16)
    q_cA = BL("q_cA", [128, DI], FP16)
    q_sg = BL("q_sg", [128, DI], FP16)
    q_out = BL("q_out", [128, DI], BF16)
    wqgT = BL("wqgT", [128, 8, DI], BF16)
    wquT = BL("wquT", [128, 8, DI], BF16)

    a4_end_t = t_d.count
    a4_end_v = v_d.count

    @blk.sync
    def _(sync):
        sync.wait_ge(t_d.h, a4_end_t)
        sync.wait_ge(v_d.h, a4_end_v)
        for i in range(2):
            s_r1s[i].wait(sync)
        sync.wait_ge(s_dw.h, wq_done_marks["gate"])
        for q in range(4):
            s_wq.dma(sync, out=wqgT[:, 2 * q:2 * (q + 1), :],
                     in_=wqg_dr[:, 2 * q:2 * (q + 1), :])
        sync.wait_ge(s_dw.h, wq_done_marks["up"])
        for q in range(4):
            s_wq.dma(sync, out=wquT[:, 2 * q:2 * (q + 1), :],
                     in_=wqu_dr[:, 2 * q:2 * (q + 1), :])
    wq_gate_ld = s_wq.count - 64
    wq_up_ld = s_wq.count

    @blk.gpsimd
    def _(g):
        g.wait_ge(t_d.h, a4_end_t)
        g.wait_ge(v_d.h, a4_end_v)
        for i in range(2):
            s_r1s[i].wait(g)
        g.wait_ge(s_dw.h, wq_done_marks["down"])
        for q in range(4):
            s_uv.dma(g, out=wqdT[:, 6 * q:min(6 * (q + 1), 22), :],
                     in_=wqd_dr[:, 6 * q:min(6 * (q + 1), 22), :])
    wq_down_ld = s_uv.count

    xqT = BL("xqT", [128, 2, 8, 128], BF16)
    hqT = BL("hqT", [128, 2, 22, 128], BF16)
    gsil = BL("gsil", [128, 2, DI], BF16)
    r1st = BL("r1st", [128, 2, D], BF16)
    q2_cA = BL("q2_cA", [128, D], FP16)
    q2_sg = BL("q2_sg", [128, D], FP16)
    outst = xst
    xq_out = r0st

    if dump:
        dbgB = {}
        for nm, cols in (("ln2", D), ("xq", D), ("h", DI), ("hq", DI)):
            dbgB[nm] = nc.dram_tensor("dbgB_" + nm, [NT, 128, cols], BF16,
                                      kind="ExternalOutput")

    g_chunks = [(i * 512, min(512, DI - i * 512)) for i in range((DI + 511) // 512)]
    NG = len(g_chunks)
    out_v_prev = 0
    r1_free = [0, 0]
    xq_v_prev = 0
    hq_v_prev_any = 0
    hq_v_prev2 = [0, 0]
    down_t_prev2 = [0, 0]
    gateup_t_prev2 = [0, 0]
    xqT_tp_prev2 = [[], []]
    silu_a_prev = 0
    hTT_v_prev = 0
    lnst_guard = 0
    out_store_prev2 = [0, 0]
    hq_tp_prev = [(c, s_tp[c].count) for c in range(22)]

    def emit_act_quant(v, src_ap, cA, sg, n, nb, out_ap, h_sel=None):
        if h_sel is None or 0 in h_sel:
            v.tensor_reduce(q_gm[:, 0:nb], src_ap.rearrange("p (c k) -> p c k", k=16),
                            AX.X, OP.max, apply_absolute_value=True)
            v.tensor_scalar(q_s[:, 0:nb], q_gm[:, 0:nb], 6e-12, None, OP.max)
            v.tensor_scalar(q_s2[:, 0:nb], q_gm[:, 0:nb], 0.16666666666666666, 1e-12,
                            OP.mult, OP.max)
            v.drain()
            v.reciprocal(q_r[:, 0:nb], q_s[:, 0:nb])
            v.tensor_copy(q_s16[:, 0:nb], q_s2[:, 0:nb])
            v.drain()
        for h0 in (range(0, n, HDI) if h_sel is None
                   else [i * HDI for i in h_sel]):
            m = min(HDI, n - h0)
            mb = m // 16
            b0 = h0 // 16
            v.tensor_tensor(q_an[:, 0:m].rearrange("p (c k) -> p c k", k=16),
                            cA[:, h0:h0 + m].rearrange("p (c k) -> p c k", k=16),
                            q_r[:, b0:b0 + mb].unsqueeze(2).broadcast_to([128, mb, 16]),
                            OP.mult)
            q_chain(v, q_an[:, 0:m], q_acc[:, 0:m], cA[:, h0:h0 + m], q_acc[:, 0:m], m)
            v.tensor_tensor(q_an[:, 0:m], q_acc[:, 0:m], sg[:, h0:h0 + m], OP.mult)
            v.tensor_tensor(out_ap[:, h0:h0 + m].rearrange("p (c k) -> p c k", k=16),
                            q_an[:, 0:m].rearrange("p (c k) -> p c k", k=16),
                            q_s16[:, b0:b0 + mb].unsqueeze(2).broadcast_to([128, mb, 16]),
                            OP.mult)
        v.drain()

    for st in range(NT):
        b = st % 2
        sb = 8 * b

        @blk.sync
        def _(sync, st=st, b=b, r1f=r1_free[b]):
            if st == 0:
                for i in range(2):
                    s_r1s[i].wait(sync)
                s_dw.wait(sync)
            if r1f:
                sync.wait_ge(v_d.h, r1f)
            s_r1[b].dma(sync, out=r1st[:, b, :], in_=r1_dr[128 * st:128 * (st + 1), :])
        r1_in = s_r1[b].count

        @blk.vector
        def _(v, st=st, b=b, sb=sb, r1_in=r1_in):
            s_r1[b].wait(v, r1_in)
            v.reduce_sum(stat[:, sb:sb + 1], r1st[:, b, :], AX.X)
            v.tensor_tensor(sqsc[:], r1st[:, b, 0:512], r1st[:, b, 0:512], OP.mult)
            v.reduce_sum(stat[:, sb + 1:sb + 2], sqsc[:], AX.X)
            v.tensor_tensor(sqsc[:], r1st[:, b, 512:1024], r1st[:, b, 512:1024], OP.mult)
            v.reduce_sum(stat[:, sb + 5:sb + 6], sqsc[:], AX.X)
            v.drain()
            v.tensor_tensor(stat[:, sb + 1:sb + 2], stat[:, sb + 1:sb + 2],
                            stat[:, sb + 5:sb + 6], OP.add)
            v.tensor_scalar(stat[:, sb:sb + 1], stat[:, sb:sb + 1], 1.0 / D, None, OP.mult)
            v.tensor_scalar(stat[:, sb + 1:sb + 2], stat[:, sb + 1:sb + 2], 1.0 / D,
                            None, OP.mult)
            v.scalar_tensor_tensor(stat[:, sb + 2:sb + 3], stat[:, sb:sb + 1], 1.0,
                                   stat[:, sb:sb + 1], OP.mult, OP.mult)
            v.tensor_tensor(stat[:, sb + 1:sb + 2], stat[:, sb + 1:sb + 2],
                            stat[:, sb + 2:sb + 3], OP.subtract)
            v.tensor_scalar(stat[:, sb + 1:sb + 2], stat[:, sb + 1:sb + 2], LN_EPS,
                            None, OP.add)
            v.engine_nop().then_inc(v_d.h, 1)
        v_d.count += 1
        stats_v = v_d.count

        @blk.scalar
        def _(s, st=st, sb=sb, sv=stats_v):
            s.wait_ge(v_d.h, sv)
            s.activation(stat[:, sb + 3:sb + 4], stat[:, sb + 1:sb + 2],
                         AF.Sqrt).then_inc(a_d.h, 1)
        a_d.count += 1
        sqrt_a = a_d.count

        @blk.vector
        def _(v, st=st, b=b, sb=sb, sa=sqrt_a, lg=lnst_guard):
            v.wait_ge(a_d.h, sa)
            if lg:
                v.wait_ge(a_d.h, lg)
            v.reciprocal(stat[:, sb + 4:sb + 5], stat[:, sb + 3:sb + 4])
            v.tensor_scalar(lnst[:], r1st[:, b, :], stat[:, sb:sb + 1],
                            stat[:, sb + 4:sb + 5], OP.subtract, OP.mult)
            v.engine_nop().then_inc(v_d.h, 1)
        v_d.count += 1
        apply_v = v_d.count

        @blk.scalar
        def _(s, st=st, b=b, av=apply_v, xqp=xq_v_prev):
            s.wait_ge(v_d.h, av)
            if xqp:
                s.wait_ge(v_d.h, xqp)
            s.activation(q2_cA[:], lnst[:], AF.Abs, scale=6.0)
            s.activation(q2_sg[:], lnst[:], AF.Sign).then_inc(a_d.h, 1)
        a_d.count += 1
        xq_abs_a = a_d.count
        lnst_guard = xq_abs_a

        @blk.vector
        def _(v, st=st, b=b, aa=xq_abs_a, guard=list(xqT_tp_prev2[b])):
            v.wait_ge(a_d.h, aa)
            for c, val in guard:
                v.wait_ge(s_tp[c].h, val)
            emit_act_quant(v, lnst[:], q2_cA, q2_sg, D, NB_D, xq_out[:, b, 0:D])
            v.engine_nop().then_inc(v_d.h, 1)
        v_d.count += 1
        xq_v = v_d.count
        xq_v_prev = xq_v

        if dump:
            @blk.gpsimd
            def _(g, st=st, b=b, xq_v=xq_v):
                g.wait_ge(v_d.h, xq_v)
                s_dbg.dma(g, out=dbgB["ln2"][st, :, :], in_=lnst[:])
                s_dbg.dma(g, out=dbgB["xq"][st, :, :], in_=xq_out[:, b, 0:D])

        @blk.scalar
        def _(s, st=st, b=b, xq_v=xq_v, tg=gateup_t_prev2[b]):
            s.wait_ge(v_d.h, xq_v)
            if tg:
                s.wait_ge(t_d.h, tg)
            for c in range(8):
                s_tp[c].dma(s, out=xqT[:, b, c, :],
                            in_=xq_out[:, b, 128 * c:128 * (c + 1)], transpose=True)
        xq_tp = [(c, s_tp[c].count) for c in range(8)]
        xqT_tp_prev2[b] = xq_tp

        t0 = t_d.count
        a0 = a_d.count
        V0 = v_d.count

        @blk.tensor
        def _(t, st=st, b=b, a0=a0, V0=V0, xq_tp=list(xq_tp), sprev=silu_a_prev,
              hprev=hTT_v_prev):
            for c, val in xq_tp:
                t.wait_ge(s_tp[c].h, val)
            if st == 0:
                s_wq.wait(t, wq_up_ld)
            if sprev:
                t.wait_ge(a_d.h, sprev)
            for gi, (off, w) in enumerate(g_chunks):
                ps = psA if gi % 2 == 0 else psB
                if gi >= 2:
                    t.wait_ge(a_d.h, a0 + gi - 1)
                for kc in range(8):
                    i_ = t.matmul(ps[0:128, 0:w], xqT[:, b, kc, :], wqgT[:, kc, off:off + w],
                                  start=(kc == 0), stop=(kc == 7))
                i_.then_inc(t_d.h, 1)
            if hprev:
                t.wait_ge(v_d.h, hprev)
            for gi, (off, w) in enumerate(g_chunks):
                ps = psC if gi % 2 == 0 else psD
                if gi >= 2:
                    t.wait_ge(v_d.h, V0 + gi - 1)
                for kc in range(8):
                    i_ = t.matmul(ps[0:128, 0:w], xqT[:, b, kc, :], wquT[:, kc, off:off + w],
                                  start=(kc == 0), stop=(kc == 7))
                i_.then_inc(t_d.h, 1)
        t_d.count += 2 * NG
        gateup_t_prev2[b] = t_d.count

        @blk.scalar
        def _(s, st=st, b=b, t0=t0, hqp=hq_v_prev2[b]):
            if hqp:
                s.wait_ge(v_d.h, hqp)
            for gi, (off, w) in enumerate(g_chunks):
                ps = psA if gi % 2 == 0 else psB
                s.wait_ge(t_d.h, t0 + gi + 1)
                s.activation(gsil[:, b, off:off + w], ps[0:128, 0:w],
                             AF.Silu).then_inc(a_d.h, 1)
        a_d.count += NG
        silu_a_prev = a_d.count

        @blk.vector
        def _(v, st=st, b=b, t0=t0, a0=a0):
            for gi, (off, w) in enumerate(g_chunks):
                ps = psC if gi % 2 == 0 else psD
                v.wait_ge(t_d.h, t0 + NG + gi + 1)
                v.wait_ge(a_d.h, a0 + gi + 1)
                v.tensor_tensor(gsil[:, b, off:off + w], ps[0:128, 0:w],
                                gsil[:, b, off:off + w], OP.mult)
                v.engine_nop().then_inc(v_d.h, 1)
        v_d.count += NG
        hTT_v_prev = v_d.count
        h_v = v_d.count

        @blk.scalar
        def _(s, st=st, b=b, h_v=h_v, hqp=hq_v_prev_any):
            s.wait_ge(v_d.h, h_v)
            if hqp:
                s.wait_ge(v_d.h, hqp)
            s.activation(q_cA[:, 0:DI], gsil[:, b, :], AF.Abs, scale=6.0)
            s.activation(q_sg[:, 0:DI], gsil[:, b, :], AF.Sign).then_inc(a_d.h, 1)
        a_d.count += 1
        hq_abs_a = a_d.count

        @blk.vector
        def _(v, st=st, b=b, aa=hq_abs_a, guard=list(hq_tp_prev[:11])):
            v.wait_ge(a_d.h, aa)
            for c, val in guard:
                v.wait_ge(s_tp[c].h, val)
            emit_act_quant(v, gsil[:, b, :], q_cA, q_sg, DI, NB_I, q_out[:, 0:DI],
                           h_sel=[0])
            v.engine_nop().then_inc(v_d.h, 1)
        v_d.count += 1
        hq_v0 = v_d.count

        @blk.vector
        def _(v, st=st, b=b, guard=list(hq_tp_prev[11:])):
            for c, val in guard:
                v.wait_ge(s_tp[c].h, val)
            emit_act_quant(v, gsil[:, b, :], q_cA, q_sg, DI, NB_I, q_out[:, 0:DI],
                           h_sel=[1])
            v.engine_nop().then_inc(v_d.h, 1)
        v_d.count += 1
        hq_v = v_d.count
        hq_v_prev_any = hq_v
        hq_v_prev2[b] = hq_v

        if dump:
            @blk.gpsimd
            def _(g, st=st, b=b, hq_v=hq_v):
                g.wait_ge(v_d.h, hq_v)
                s_dbg.dma(g, out=dbgB["h"][st, :, :], in_=gsil[:, b, :])
                s_dbg.dma(g, out=dbgB["hq"][st, :, :], in_=q_out[:, 0:DI])

        @blk.scalar
        def _(s, st=st, b=b, hq_v0=hq_v0, hq_v=hq_v, dtp=down_t_prev2[b]):
            s.wait_ge(v_d.h, hq_v0)
            if dtp:
                s.wait_ge(t_d.h, dtp)
            for c in range(11):
                s_tp[c].dma(s, out=hqT[:, b, c, :], in_=q_out[:, 128 * c:128 * (c + 1)],
                            transpose=True)
            s.wait_ge(v_d.h, hq_v)
            for c in range(11, 22):
                s_tp[c].dma(s, out=hqT[:, b, c, :], in_=q_out[:, 128 * c:128 * (c + 1)],
                            transpose=True)
        hq_tp = [(c, s_tp[c].count) for c in range(22)]
        hq_tp_prev = hq_tp

        t1 = t_d.count

        @blk.tensor
        def _(t, st=st, b=b, hq_tp=list(hq_tp), ovp=out_v_prev):
            if st == 0:
                s_uv.wait(t, wq_down_ld)
            if ovp:
                t.wait_ge(v_d.h, ovp)
            for half in range(2):
                ps = psS0 if half == 0 else psS1
                for kc in range(22):
                    if half == 0:
                        t.wait_ge(s_tp[kc].h, hq_tp[kc][1])
                    i_ = t.matmul(ps[0:128, :], hqT[:, b, kc, :],
                                  wqdT[:, kc, 512 * half:512 * (half + 1)],
                                  start=(kc == 0), stop=(kc == 21))
                i_.then_inc(t_d.h, 1)
        t_d.count += 2
        down_t_prev2[b] = t_d.count

        @blk.vector
        def _(v, st=st, b=b, t1=t1, stp=out_store_prev2[b]):
            if stp:
                v.wait_ge(s_out.h, stp)
            for half in range(2):
                ps = psS0 if half == 0 else psS1
                v.wait_ge(t_d.h, t1 + half + 1)
                v.tensor_tensor(outst[:, b, 512 * half:512 * (half + 1)], ps[0:128, :],
                                r1st[:, b, 512 * half:512 * (half + 1)], OP.add)
            v.drain()
            v.engine_nop().then_inc(v_d.h, 1)
        v_d.count += 1
        out_v = v_d.count
        out_v_prev = out_v
        r1_free[b] = out_v

        @blk.sync
        def _(sync, st=st, b=b, out_v=out_v):
            sync.wait_ge(v_d.h, out_v)
            s_out.dma(sync, out=out_d[128 * st:128 * (st + 1), :], in_=outst[:, b, :])
        out_store_prev2[b] = s_out.count

    if dump:
        dmp2 = {}
        for nm, ten in (("wqgT", wqgT), ("wquT", wquT), ("wqdT", wqdT),
                        ("qout", q_out)):
            shp = [int(d) for d in ten.shape]
            dmp2[nm] = nc.dram_tensor("dumpB_" + nm, shp, BF16, kind="ExternalOutput")

        @blk.sync
        def _(sync):
            s_out.wait(sync)
            for nm, ten in (("wqgT", wqgT), ("wquT", wquT), ("wqdT", wqdT),
                            ("qout", q_out)):
                s_out.dma(sync, out=dmp2[nm][:], in_=ten[:])
            s_out.wait(sync)

    @blk.sync
    def _(sync):
        s_out.wait(sync)
        if dump:
            s_dbg.wait(sync)

    bctxL.close()
    ctx.close()
    return nc


_NC_CACHE = None


def kernel(L0_attn_x, L0_attn_W_DQ, L0_attn_W_UQ, L0_attn_W_QR, L0_attn_W_DKV,
           L0_attn_W_UK, L0_attn_W_UV, L0_attn_W_KR, L0_attn_W_O,
           L0_ffn_w_gate, L0_ffn_w_up, L0_ffn_w_down):
    global _NC_CACHE
    if _NC_CACHE is None:
        _NC_CACHE = build_nc()
    nc = _NC_CACHE

    cs, sn = _rope_tables()
    masks = _causal_masks()
    x = np.ascontiguousarray(np.asarray(L0_attn_x, dtype=np.float32))
    common = {
        "W_DQ": np.ascontiguousarray(np.asarray(L0_attn_W_DQ, np.float32)),
        "W_UQ": np.ascontiguousarray(np.asarray(L0_attn_W_UQ, np.float32)),
        "W_QR": np.ascontiguousarray(np.asarray(L0_attn_W_QR, np.float32)),
        "W_DKV": np.ascontiguousarray(np.asarray(L0_attn_W_DKV, np.float32)),
        "W_UK": np.ascontiguousarray(np.asarray(L0_attn_W_UK, np.float32)),
        "W_UV": np.ascontiguousarray(np.asarray(L0_attn_W_UV, np.float32)),
        "W_KR": np.ascontiguousarray(np.asarray(L0_attn_W_KR, np.float32)),
        "W_O": np.ascontiguousarray(np.asarray(L0_attn_W_O, np.float32)),
        "w_gate": np.ascontiguousarray(np.asarray(L0_ffn_w_gate, np.float32)),
        "w_up": np.ascontiguousarray(np.asarray(L0_ffn_w_up, np.float32)),
        "w_down": np.ascontiguousarray(np.asarray(L0_ffn_w_down, np.float32)),
        "rope_cos": cs, "rope_sin": sn, "masks": masks, "sel": _den_sel(),
    }
    in_maps = [dict(common, x=np.ascontiguousarray(x[c])) for c in range(8)]
    res = run_bass_kernel_spmd(nc, in_maps, core_ids=list(range(8)))
    out = np.stack([np.asarray(res.results[c]["out"], np.float32) for c in range(8)])
    return out
